# version 07
# baseline (speedup 1.0000x reference)
import contextlib
import numpy as np
import concourse.bass as bass
import concourse.mybir as mybir
from concourse.bass_utils import run_bass_kernel_spmd

F32 = mybir.dt.float32
F32R = mybir.dt.float32r
BF16 = mybir.dt.bfloat16
FP8 = mybir.dt.float8e4
I32 = mybir.dt.int32
AF = mybir.ActivationFunctionType
ALU = mybir.AluOpType
AX = mybir.AxisListType

T = 2048
D = 1024
NT = 16
ND = 8


def ts(i, n=128):
    return slice(i * n, (i + 1) * n)


class Plan:

    ENGS = ("sync", "gpsimd", "vector", "scalar", "tensor")

    NLANE = 4

    def __init__(self, nc):
        self.nc = nc
        self.q = {e: [] for e in self.ENGS}
        self.h = {}
        self.cnt = {}
        self.last_wait = {}
        names = ["t", "v", "s"]
        for qn in ("y", "g"):
            for i in range(self.NLANE):
                names.append(qn + str(i))
        for n in names:
            self.h[n] = nc.alloc_semaphore("sem_" + n)
            self.cnt[n] = 0
        self.rr = {"y": 0, "g": 0}

    def dma(self, queue, fn):
        eng = "sync" if queue == "y" else "gpsimd"
        i = self.rr[queue]
        self.rr[queue] = (i + 1) % self.NLANE
        lane = queue + str(i)
        if self.cnt[lane] > 0:
            self.wait(eng, lane, self.cnt[lane])
        tok = (lane, self.inc(eng, fn, lane, 16))
        return tok

    def wait_tok(self, eng, tok):
        self.wait(eng, tok[0], tok[1])

    def cop(self, eng, fn):
        sem = {"vector": "v", "scalar": "s", "tensor": "t"}[eng]
        self.wait(eng, sem, self.cnt[sem])
        return self.inc(eng, fn, sem)

    def fence(self):
        sems = [n for n in self.cnt if self.cnt[n] > 0]
        for eng in self.ENGS:
            for n in sems:
                self.wait(eng, n, self.cnt[n])

    def dma_barrier(self, eng):
        for lane in list(self.cnt):
            if lane[0] in "yg" and len(lane) == 2 and self.cnt[lane] > 0:
                self.wait(eng, lane, self.cnt[lane])

    def emit(self, eng, fn):
        self.q[eng].append(fn)

    def inc(self, eng, fn, sem, k=1):
        h = self.h[sem]
        self.q[eng].append(lambda e, fn=fn, h=h, k=k: fn(e).then_inc(h, k))
        self.cnt[sem] += k
        return self.cnt[sem]

    def wait(self, eng, sem, val=None):
        if val is None:
            val = self.cnt[sem]
        assert val <= self.cnt[sem], f"wait on future value of {sem}"
        key = (eng, sem)
        if self.last_wait.get(key, -1) >= val:
            return
        self.last_wait[key] = val
        h = self.h[sem]
        self.q[eng].append(lambda e, h=h, v=val: e.wait_ge(h, v))

    def run(self, block):
        for eng in self.ENGS:
            block.__getattribute__(eng)(lambda e, eng=eng: [f(e) for f in self.q[eng]])


def build(nc, stop_after="full"):
    dt = nc.dram_tensor
    x_in = dt("x", [T, D], F32, kind="ExternalInput").ap()
    y_out = dt("y", [T, D], F32, kind="ExternalOutput").ap()
    mq = dt("mq", [D, 512], F32, kind="ExternalInput").ap()
    mk = dt("mk", [D, 512], F32, kind="ExternalInput").ap()
    mv = dt("mv", [D, 1024], F32, kind="ExternalInput").ap()
    w1t0 = dt("w1t0", [D, 4096], F32, kind="ExternalInput").ap()
    w2t0 = dt("w2t0", [4096, D], F32, kind="ExternalInput").ap()
    n1w_bc = dt("n1w_bc", [128, D], F32, kind="ExternalInput").ap()
    n2w_bc = dt("n2w_bc", [128, 4096], F32, kind="ExternalInput").ap()
    actsb = dt("actsb", [128, 2], F32, kind="ExternalInput").ap()
    lnw_bc = dt("lnw_bc", [128, D], F32, kind="ExternalInput").ap()
    lnb_bc = dt("lnb_bc", [128, D], F32, kind="ExternalInput").ap()
    wq1 = dt("wq1", [D, 512], F32, kind="ExternalInput").ap()
    wk1 = dt("wk1", [D, 512], F32, kind="ExternalInput").ap()
    wv1 = dt("wv1", [D, 1024], F32, kind="ExternalInput").ap()
    wg1 = dt("wg1", [D, 1024], F32, kind="ExternalInput").ap()
    gkw1 = dt("gkw1", [D, 16], F32, kind="ExternalInput").ap()
    gkw2 = dt("gkw2", [16, 512], F32, kind="ExternalInput").ap()
    gkb2_bc = dt("gkb2_bc", [128, 512], F32, kind="ExternalInput").ap()
    rmsw_bc = dt("rmsw_bc", [128, D], F32, kind="ExternalInput").ap()
    wo1 = dt("wo1", [D, D], F32, kind="ExternalInput").ap()
    wmlp1 = dt("wmlp1", [D, 2048], F32, kind="ExternalInput").ap()
    bmlp1 = dt("bmlp1", [128, 16], F32, kind="ExternalInput").ap()
    wmlp2 = dt("wmlp2", [2048, D], F32, kind="ExternalInput").ap()
    bmlp2_bc = dt("bmlp2_bc", [128, D], F32, kind="ExternalInput").ap()
    w1t1 = dt("w1t1", [D, 4096], F32, kind="ExternalInput").ap()
    b1f = dt("b1f", [128, 32], F32, kind="ExternalInput").ap()
    w2t1 = dt("w2t1", [4096, D], F32, kind="ExternalInput").ap()
    b2f_bc = dt("b2f_bc", [128, D], F32, kind="ExternalInput").ap()

    c_ident = nc.inline_tensor(np.eye(128, dtype=np.float32), "c_ident")
    c_triu = nc.inline_tensor(np.triu(np.ones((128, 128), np.float32)), "c_triu")
    x2d = dt("x2d", [T, D], BF16, kind="Internal").ap()
    xn1td = dt("xn1td", [128, ND * T], BF16, kind="Internal").ap()

    P = Plan(nc)

    with (
        nc.sbuf_tensor("XRES", [128, NT * 1024], F32) as XRES,
        nc.sbuf_tensor("IDBF", [128, 128], BF16) as IDBF,
        nc.sbuf_tensor("ST", [128, 10 * NT], F32) as ST,
        nc.sbuf_tensor("COLS", [128, 64], F32) as COLS,
        nc.sbuf_tensor("ONESR", [1, 128], F32) as ONESR,
        nc.sbuf_tensor("CONSTC", [128, 16], F32) as CONSTC,
        nc.sbuf_tensor("SCRA", [128, 1024], BF16) as SCRA,
        nc.Block() as block,
    ):
        def st(i):
            return ST[:, i * NT:(i + 1) * NT]
        R0, NMR0, R2, NMR2, MX0, MX2 = (st(i) for i in range(6))
        R1, NMR1 = st(6), st(7)

        CVALS = [1e-5, 1e-8, 4096.0 * 1e-8, float(np.log(127.0 * 1e5)),
                 float(0.5 * np.log(4096.0)), 1.0,
                 float(-0.5 * np.log(32.0)), float(np.log(2.0)), 1e-35,
                 float(-3.0 * np.log(2.0)), float(3.0 * np.log(2.0)),
                 float(-10.0 * np.log(2.0)), float(10.0 * np.log(2.0))]
        CIDX = {v: i for i, v in enumerate(CVALS)}

        def getc(v):
            return CONSTC[:, CIDX[v]:CIDX[v] + 1]

        x_toks = [P.dma("y", lambda e, ti=ti: e.dma_start(out=XRES[:, ts(ti, 1024)],
                    in_=x_in[ts(ti), :])) for ti in range(NT)]
        id_tok = P.dma("g", lambda e: e.dma_start(out=IDBF[:], in_=c_ident.ap()[:, :]))
        P.cop("vector", lambda e: e.memset(ONESR[:], 1.0))
        for i, v in enumerate(CVALS):
            P.cop("vector", lambda e, i=i, v=v: e.memset(CONSTC[:, i:i + 1], v))
        ones_done = P.cop("vector", lambda e: e.memset(COLS[:, 63:64], 1.0))
        P.wait("scalar", "v", ones_done)

        def ln_stats(r_col, nmr_col, dep_v):
            SUM, SUM2 = COLS[:, 0:NT], COLS[:, 16:16 + NT]
            for tok in x_toks:
                P.wait_tok("vector", tok)
                P.wait_tok("scalar", tok)
            P.wait("vector", "v", dep_v)
            if dep_v:
                P.wait("scalar", "v", dep_v)
            for ti in range(NT):
                P.cop("vector", lambda e, ti=ti: e.tensor_reduce(
                    SUM[:, ti:ti + 1], XRES[:, ts(ti, 1024)], AX.X, ALU.add))
                P.wait("scalar", "s", P.cnt["s"])
                P.cop("scalar", lambda e, ti=ti: e.activation(
                    SCRA[:], XRES[:, ts(ti, 1024)], AF.Square, bias=0.0, scale=1.0,
                    accum_out=SUM2[:, ti:ti + 1]))
            P.wait("vector", "s", P.cnt["s"])
            P.wait("vector", "v", None)
            M = COLS[:, 32:32 + NT]
            VAR = COLS[:, 0:NT]
            P.cop("vector", lambda e: e.tensor_scalar(M, SUM, 1.0 / D, None, ALU.mult))
            P.cop("vector", lambda e: e.tensor_scalar(SUM2, SUM2, 1.0 / D, None, ALU.mult))
            P.cop("vector", lambda e: e.tensor_tensor(VAR, M, M, ALU.mult))
            vd = P.cop("vector", lambda e: e.tensor_tensor(VAR, SUM2, VAR, ALU.subtract))
            P.wait("scalar", "v", vd)
            P.cop("scalar", lambda e: e.activation(SUM2, VAR, AF.Ln, bias=getc(1e-5), scale=1.0))
            rd = P.cop("scalar", lambda e: e.activation(r_col, SUM2, AF.Exp, bias=0.0, scale=-0.5))
            P.wait("vector", "s", rd)
            P.cop("vector", lambda e: e.tensor_tensor(nmr_col, M, r_col, ALU.mult))
            nm = P.cop("vector", lambda e: e.tensor_scalar(nmr_col, nmr_col, -1.0, None, ALU.mult))
            return VAR, nm

        VAR0, nm0 = ln_stats(R0, NMR0, dep_v=0)
        TMP = COLS[:, 16:16 + NT]
        P.cop("vector", lambda e: e.tensor_tensor(TMP, VAR0, R0, ALU.mult))
        t2 = P.cop("vector", lambda e: e.tensor_tensor(TMP, TMP, R0, ALU.mult))
        P.wait("scalar", "v", t2)
        P.cop("scalar", lambda e: e.activation(TMP, TMP, AF.Ln, bias=getc(1e-8), scale=1.0))
        iv = P.cop("scalar", lambda e: e.activation(TMP, TMP, AF.Exp, bias=0.0, scale=-0.5))
        P.wait("vector", "s", iv)
        P.cop("vector", lambda e: e.tensor_tensor(R2, R0, TMP, ALU.mult))
        l0_cols = P.cop("vector", lambda e: e.tensor_tensor(NMR2, NMR0, TMP, ALU.mult))

        with nc.sbuf_tensor("XNT", [128, ND * T], BF16) as XNT:
            P.wait("vector", "v", l0_cols)
            P.wait_tok("tensor", id_tok)
            l0_ffn(nc, P, XRES, w1t0, w2t0, n1w_bc, n2w_bc, actsb, R0, NMR0,
                   R2, NMR2, MX0, COLS, ONESR, IDBF, SCRA, getc, XNT)
            if stop_after == "l0ffn":
                finish(nc, P, XRES, y_out)
                return P

            P.fence()
            l0_attn(nc, P, XRES, XNT, mq, mk, mv, COLS)
        if stop_after == "l0":
            finish(nc, P, XRES, y_out)
            return P

        P.fence()
        _, nm1 = ln_stats(R1, NMR1, dep_v=P.cnt["v"])
        l1_attn(nc, P, XRES, R1, NMR1, COLS, ST, IDBF, ONESR, c_ident, c_triu,
                lnw_bc, lnb_bc, wq1, wk1, wv1, wg1, gkw1, gkw2, gkb2_bc, rmsw_bc,
                wo1, wmlp1, bmlp1, wmlp2, bmlp2_bc, SCRA, getc, x2d, xn1td)
        if stop_after == "l1attn":
            finish(nc, P, XRES, y_out)
            return P
        P.fence()
        l1_ffn(nc, P, XRES, COLS, IDBF, w1t1, b1f, w2t1, b2f_bc, getc, xn1td)

        finish(nc, P, XRES, y_out)
    return P


def finish(nc, P, XRES, y_out):
    P.wait("gpsimd", "v", P.cnt["v"])
    P.wait("gpsimd", "s", P.cnt["s"])
    P.wait("gpsimd", "t", P.cnt["t"])
    for ti in range(NT):
        P.dma("g", lambda e, ti=ti: e.dma_start(
            out=y_out[ts(ti), :], in_=XRES[:, ts(ti, 1024)]))
    P.dma_barrier("gpsimd")
    P.dma_barrier("sync")



def l0_attn(nc, P, XRES, XNT, mq, mk, mv, COLS):
    with (
        nc.sbuf_tensor("QT2", [128, 4 * T], BF16) as QT2,
        nc.sbuf_tensor("KT2", [128, 4 * T], BF16) as KT2,
        nc.sbuf_tensor("VB", [128, 16384], BF16) as VB,
        nc.sbuf_tensor("EB", [128, 2 * T], BF16) as EB,
        nc.sbuf_tensor("VSC", [128, 256], BF16) as VSC,
        nc.sbuf_tensor("MW", [128, 8192], BF16) as MW,
        nc.psum_tensor("SC", [128, 2048], F32) as SC,
        nc.psum_tensor("OPS", [128, 2048], F32) as OPS,
    ):
        for wsrc, dstT in ((mq, QT2), (mk, KT2)):
            P.wait("gpsimd", "t", P.cnt["t"])
            wl = P.dma("g", lambda e, wsrc=wsrc: e.dma_start(
                out=MW[:, 0:4096].rearrange("p (k o) -> p k o", k=ND),
                in_=wsrc.rearrange("(k p) o -> p k o", p=128)))
            P.wait_tok("tensor", wl)
            P.wait("tensor", "v", P.cnt["v"])
            for m in range(4):
                h0 = 2 * m
                r0 = 64 * (h0 // 4)
                for sub in range(8):
                    g = (m * 8 + sub) % 4
                    reg = SC[r0:r0 + 64, 512 * g:512 * g + 512]
                    for k in range(ND):
                        mm = P.inc("tensor", lambda e, m=m, sub=sub, k=k, reg=reg: e.matmul(
                            reg,
                            MW[:, k * 512 + 64 * sub: k * 512 + 64 * (sub + 1)],
                            XNT[:, k * T + 512 * m: k * T + 512 * (m + 1)],
                            start=(k == 0), stop=(k == ND - 1)), "t")
                    P.wait("vector", "t", mm)
                    blk0 = h0 % 4
                    ev = P.cop("vector", lambda e, dstT=dstT, r0=r0, blk0=blk0, sub=sub,
                               reg=reg: e.tensor_copy(
                        dstT[r0:r0 + 64, blk0 * T:(blk0 + 2) * T].rearrange(
                            "p (hh tl rest) -> p hh tl rest", hh=2, tl=2)
                            [:, :, :, sub * 128:(sub + 1) * 128],
                        reg.rearrange("p (hh tl t) -> p hh tl t", hh=2, tl=2)))
                    if g == 3:
                        P.wait("tensor", "v", ev)
        P.wait("gpsimd", "t", P.cnt["t"])
        wl = P.dma("g", lambda e: e.dma_start(
            out=MW[:].rearrange("p (k o) -> p k o", k=ND),
            in_=mv.rearrange("(k p) o -> p k o", p=128)))
        P.wait_tok("tensor", wl)
        P.wait("tensor", "v", P.cnt["v"])
        for h in range(8):
            for tlc in range(2):
                for sg in range(2):
                    g = (h * 4 + tlc * 2 + sg) % 4
                    reg = OPS[:, 512 * g:512 * g + 512]
                    for k in range(ND):
                        mm = P.inc("tensor", lambda e, h=h, tlc=tlc, sg=sg, k=k, reg=reg:
                            e.matmul(
                                reg,
                                XNT[:, k * T + 256 * h + 128 * tlc:
                                    k * T + 256 * h + 128 * (tlc + 1)],
                                MW[:, k * 1024 + 512 * sg: k * 1024 + 512 * (sg + 1)],
                                start=(k == 0), stop=(k == ND - 1)), "t")
                    P.wait("vector", "t", mm)
                    t0_ = h * 16 + 8 * sg + tlc
                    ev = P.cop("vector", lambda e, t0_=t0_, reg=reg: e.tensor_copy(
                        VB[:].rearrange("p (n t) -> p n t", t=128)[:, t0_:t0_ + 7:2, :],
                        reg.rearrange("p (s t) -> p s t", s=4)))
                    if g == 3:
                        P.wait("tensor", "v", ev)
        exp_s = {}
        outmm_t = {}
        rv_v = {}
        vs_v = {}
        evac_v = None

        def emit_out(h, jt):
            par = jt % 2
            P.wait("tensor", "v", vs_v[jt])
            if jt == 0 and evac_v is not None:
                P.wait("tensor", "v", evac_v)
            for tau in range(2):
                for s_ in range(8):
                    outmm_t[par] = P.inc("tensor", lambda e, tau=tau, s_=s_, jt=jt,
                                         par=par: e.matmul(
                        OPS[:, ts(8 * tau + s_)],
                        EB[:, 2048 * par + 1024 * tau + 128 * s_:
                            2048 * par + 1024 * tau + 128 * (s_ + 1)],
                        VSC[:, 128 * par:128 * (par + 1)],
                        start=(jt == 0 and (8 * tau + s_) % 4 == 0),
                        stop=(jt == NT - 1 and (8 * tau + s_) % 4 == 3)), "t")

        for h in range(8):
            r0 = 64 * (h // 4)
            blk = h % 4
            for jt in range(NT):
                sub, tlc = jt // 2, jt % 2
                par = jt % 2
                jb = blk * T + 1024 * tlc + 128 * sub
                for half in range(2):
                    if half in exp_s:
                        P.wait("tensor", "s", exp_s[half])
                    for nn in range(2):
                        scd = P.inc("tensor", lambda e, r0=r0, blk=blk, jb=jb,
                                    half=half, nn=nn: e.matmul(
                            SC[:, 1024 * half + 512 * nn:1024 * half + 512 * (nn + 1)],
                            KT2[r0:r0 + 64, jb:jb + 128],
                            QT2[r0:r0 + 64, blk * T + 1024 * half + 512 * nn:
                                blk * T + 1024 * half + 512 * (nn + 1)],
                            start=True, stop=True), "t")
                    P.wait("scalar", "t", scd)
                    if half == 0:
                        if par in rv_v:
                            P.wait("scalar", "v", rv_v[par])
                        if par in outmm_t:
                            P.wait("scalar", "t", outmm_t[par])
                    exp_s[half] = P.inc("scalar", lambda e, par=par, half=half: e.activation(
                        EB[:, 2048 * par + 1024 * half:2048 * par + 1024 * (half + 1)],
                        SC[:, 1024 * half:1024 * (half + 1)],
                        AF.Exp, bias=0.0, scale=0.125,
                        accum_out=COLS[:, 44 + 2 * par + half:45 + 2 * par + half]), "s")
                P.wait("vector", "s", exp_s[1])
                P.cop("vector", lambda e, par=par: e.tensor_tensor(
                    COLS[:, 51 + par:52 + par], COLS[:, 44 + 2 * par:45 + 2 * par],
                    COLS[:, 45 + 2 * par:46 + 2 * par], ALU.add))
                rv_v[par] = P.cop("vector", lambda e, par=par: e.reciprocal(
                    COLS[:, 51 + par:52 + par], COLS[:, 51 + par:52 + par]))
                if par in outmm_t:
                    P.wait("vector", "t", outmm_t[par])
                vs_v[jt] = P.cop("vector", lambda e, h=h, jt=jt, par=par: e.tensor_scalar(
                    VSC[:, 128 * par:128 * (par + 1)],
                    VB[:, (h * 16 + jt) * 128:(h * 16 + jt) * 128 + 128],
                    COLS[:, 51 + par:52 + par], None, ALU.mult))
                if jt >= 1:
                    emit_out(h, jt - 1)
            emit_out(h, NT - 1)
            P.wait("vector", "t", outmm_t[(NT - 1) % 2])
            evac_v = P.cop("vector", lambda e, h=h: e.tensor_tensor(
                XRES[:, 2048 * h:2048 * (h + 1)], XRES[:, 2048 * h:2048 * (h + 1)],
                OPS[:, 0:2048], ALU.add))



def l0_ffn(nc, P, XRES, w1t0, w2t0, n1w_bc, n2w_bc, actsb, R0, NMR0, R2, NMR2,
           MX0, COLS, ONESR, IDBF, SCRA, getc, XNT):
    SY, MXC = 0, 8
    CS, CLN, CRSQ, CMZ, CMXP, CF, CD2 = 16, 17, 18, 19, 20, 21, 22
    CDSC = 24
    CPM, CPR = 26, 27
    CW1A, CW1T, CW1NT = 53, 54, 55
    CW2A, CW2T, CW2NT = 56, 49, 50
    T2A, R2A, T1A, R1A = 57, 58, 59, 60
    CCW1, CCW2 = 46, 48
    CA, CB = 61, 62
    CONES = 63
    with (
        nc.sbuf_tensor("IW1", [128, ND * 4096], FP8) as IW1,
        nc.sbuf_tensor("IW2", [128, 32 * 1024], FP8) as IW2,
        nc.sbuf_tensor("N1W", [128, D], F32) as N1W,
        nc.sbuf_tensor("N2W", [128, 4096], F32) as N2W,
    ):
        n1ld = P.dma("g", lambda e: e.dma_start(out=N1W[:], in_=n1w_bc[:, :]))
        n2ld = P.dma("g", lambda e: e.dma_start(out=N2W[:], in_=n2w_bc[:, :]))
        asld = P.dma("y", lambda e: e.dma_start(out=COLS[:, CA:CB + 1], in_=actsb[:, :]))
        iw2_v = {}
        with (
            nc.sbuf_tensor("WR0", [128, 1024], F32) as WR0,
            nc.sbuf_tensor("WR1", [128, 1024], F32) as WR1,
            nc.sbuf_tensor("WR2", [128, 1024], F32) as WR2,
            nc.sbuf_tensor("WR3", [128, 1024], F32) as WR3,
            nc.sbuf_tensor("PRA", [128, 2048], BF16) as PRA,
            nc.sbuf_tensor("TILE_A", [128, 1024], BF16) as TILE_A,
            nc.psum_tensor("PT", [128, 1024], BF16) as PT,
            nc.psum_tensor("RPS", [128, 512], F32) as RPS,
        ):
            WY, WG_ = (WR0, WR1), (WR2, WR3)

            def w1piece(i):
                k, c = i // 4, i % 4
                return w1t0[ts(k), c * 1024:(c + 1) * 1024]

            def w2piece(i):
                return w2t0[ts(i), :]

            streams = (("y", WY, w1piece, R1A, T1A, 0),
                       ("g", WG_, w2piece, R2A, T2A, 1024))
            P.cop("vector", lambda e: e.memset(COLS[:, R1A:R1A + 1], 0.0))
            P.cop("vector", lambda e: e.memset(COLS[:, R2A:R2A + 1], 0.0))
            abs_s = {}
            acc_v = {}
            for i in range(32):
                for (qn, Wp, piece, runc, tmpc, po) in streams:
                    eng = "sync" if qn == "y" else "gpsimd"
                    buf = Wp[i % 2]
                    if i >= 2:
                        P.wait(eng, "s", abs_s[(qn, i - 2)])
                    tok = P.dma(qn, lambda e, buf=buf, src=piece(i): e.dma_start(
                        out=buf[:], in_=src))
                    P.wait_tok("scalar", tok)
                    if i >= 1:
                        P.wait("scalar", "v", acc_v[(qn, i - 1)])
                    abs_s[(qn, i)] = P.cop("scalar", lambda e, buf=buf, po=po, tmpc=tmpc:
                        e.activation(PRA[:, po:po + 1024], buf[:], AF.Abs, bias=0.0,
                                     scale=1.0, accum_out=COLS[:, tmpc:tmpc + 1]))
                    P.wait("vector", "s", abs_s[(qn, i)])
                    acc_v[(qn, i)] = P.cop("vector", lambda e, runc=runc, tmpc=tmpc:
                        e.tensor_tensor(
                            COLS[:, runc:runc + 1], COLS[:, runc:runc + 1],
                            COLS[:, tmpc:tmpc + 1], ALU.add))
            acc = P.cnt["v"]
            P.wait("tensor", "v", acc)
            P.inc("tensor", lambda e: e.matmul(
                RPS[0:1, 0:1], COLS[:, R1A:R1A + 1], COLS[:, CONES:CONES + 1],
                start=True, stop=True), "t")
            mmc = P.inc("tensor", lambda e: e.matmul(
                RPS[0:1, 1:2], COLS[:, R2A:R2A + 1], COLS[:, CONES:CONES + 1],
                start=True, stop=True), "t")
            P.wait("vector", "t", mmc)
            P.cop("vector", lambda e: e.tensor_copy(COLS[0:1, T1A:T1A + 1], RPS[0:1, 0:1]))
            ev = P.cop("vector", lambda e: e.tensor_copy(COLS[0:1, T2A:T2A + 1], RPS[0:1, 1:2]))
            P.wait("tensor", "v", ev)
            P.inc("tensor", lambda e: e.matmul(
                RPS[:, 0:1], ONESR[:], COLS[0:1, T1A:T1A + 1], start=True, stop=True), "t")
            bc = P.inc("tensor", lambda e: e.matmul(
                RPS[:, 1:2], ONESR[:], COLS[0:1, T2A:T2A + 1], start=True, stop=True), "t")
            P.wait("vector", "t", bc)
            numel = float(D * 4096)
            P.cop("vector", lambda e: e.tensor_scalar(
                COLS[:, CW1A:CW1A + 1], RPS[:, 0:1], 1.0 / numel, 1e-5, ALU.mult, ALU.max))
            P.cop("vector", lambda e: e.tensor_scalar(
                COLS[:, CW2A:CW2A + 1], RPS[:, 1:2], 1.0 / numel, 1e-5, ALU.mult, ALU.max))
            for (ca, ct, cnt_) in ((CW1A, CW1T, CW1NT), (CW2A, CW2T, CW2NT)):
                P.cop("vector", lambda e, ca=ca, ct=ct: e.tensor_scalar(
                    COLS[:, ct:ct + 1], COLS[:, ca:ca + 1], 0.5, None, ALU.mult))
                P.cop("vector", lambda e, ct=ct, cnt_=cnt_: e.tensor_scalar(
                    COLS[:, cnt_:cnt_ + 1], COLS[:, ct:ct + 1], -1.0, None, ALU.mult))
            P.cop("vector", lambda e: e.tensor_scalar(
                COLS[:, CCW1:CCW1 + 1], COLS[:, CW1A:CW1A + 1], -1.0 / 127.0, None, ALU.mult))
            thd = P.cop("vector", lambda e: e.tensor_scalar(
                COLS[:, CCW2:CCW2 + 1], COLS[:, CW2A:CW2A + 1], -1.0 / 127.0, None, ALU.mult))
            P.wait("vector", "v", thd)
            stt_v = {}
            for i in range(32):
                for (qn, Wp, piece, runc, tmpc, po) in streams:
                    eng = "sync" if qn == "y" else "gpsimd"
                    tcol = CW1T if qn == "y" else CW2T
                    ncol = CW1NT if qn == "y" else CW2NT
                    buf = Wp[i % 2]
                    if i >= 2:
                        P.wait(eng, "v", stt_v[(qn, i - 2)])
                    else:
                        P.wait(eng, "s", abs_s[(qn, 30 + i)])
                    tok = P.dma(qn, lambda e, buf=buf, src=piece(i): e.dma_start(
                        out=buf[:], in_=src))
                    P.wait_tok("vector", tok)
                    if qn == "y":
                        k, c = i // 4, i % 4
                        dst = IW1[:, k * 4096 + c * 1024: k * 4096 + (c + 1) * 1024]
                    else:
                        dst = IW2[:, i * 1024:(i + 1) * 1024]
                    P.cop("vector", lambda e, buf=buf, po=po, tcol=tcol: e.tensor_scalar(
                        PRA[:, po:po + 1024], buf[:], COLS[:, tcol:tcol + 1], None, ALU.is_ge))
                    stt_v[(qn, i)] = P.cop("vector", lambda e, buf=buf, po=po, dst=dst, ncol=ncol:
                        e.scalar_tensor_tensor(
                            dst, buf[:], COLS[:, ncol:ncol + 1], PRA[:, po:po + 1024],
                            ALU.is_le, ALU.subtract))
                    if qn == "g":
                        iw2_v[i] = stt_v[(qn, i)]
            iw1_v = stt_v[("y", 31)]
            def pt(k):
                return PT[:, ts(k)]
            for ti in range(NT):
                xn = P.cop("vector", lambda e, ti=ti: e.tensor_scalar(
                    TILE_A[:], XRES[:, ts(ti, 1024)], R0[:, ti:ti + 1],
                    NMR0[:, ti:ti + 1], ALU.mult, ALU.add))
                P.wait("tensor", "v", xn)
                for k in range(ND):
                    trd = P.inc("tensor", lambda e, k=k: e.transpose(
                        pt(k), TILE_A[:, ts(k)], IDBF[:]), "t")
                P.wait("vector", "t", trd)
                for k in range(ND):
                    P.cop("vector", lambda e, ti=ti, k=k: e.tensor_copy(
                        XNT[:, k * T + 128 * ti: k * T + 128 * (ti + 1)], pt(k)))
                P.wait("tensor", "v", P.cnt["v"])
        with (
            nc.sbuf_tensor("YB", [128, 4096], F32) as YB,
            nc.sbuf_tensor("IXT2", [128, 2048], BF16) as IXT2,
            nc.sbuf_tensor("TILE_I", [128, 256], I32) as TILE_I,
            nc.sbuf_tensor("IHT", [128, 512], BF16) as IHT,
            nc.sbuf_tensor("QCB", [128, 512], BF16) as QCB,
            nc.psum_tensor("HPS", [128, 1024], F32) as HPS,
            nc.psum_tensor("TPS", [128, 1024], BF16) as TPS,
            nc.psum_tensor("OPS2", [128, 2048], F32) as OPS2,
        ):
            trc = [0]
            tps_cons = {}

            def tslot():
                n = trc[0]
                trc[0] += 1
                return n

            def pslot(n):
                return TPS[:, ts(n % 8)]

            cast_v = {}
            s1_s = {}
            s3_s = {}
            prep_pb_v = {}
            prep_t = {}
            prepvb_v = {}
            mm1_last = {}
            mm2_t = {}
            tr_t = {}
            add_v = {}
            dsc_v = {}

            def emit_prep(tj):
                PB = YB[:, 0:1024]
                P.wait_tok("vector", n1ld)
                P.cop("vector", lambda e, tj=tj: e.tensor_scalar(
                    PB, XRES[:, ts(tj, 1024)], R2[:, tj:tj + 1],
                    NMR2[:, tj:tj + 1], ALU.mult, ALU.add))
                P.cop("vector", lambda e: e.tensor_tensor(PB, PB, N1W[:], ALU.mult))
                P.cop("vector", lambda e: e.tensor_reduce(
                    COLS[:, CPM:CPM + 1], PB, AX.X, ALU.max, apply_absolute_value=True))
                P.cop("vector", lambda e: e.tensor_scalar(
                    COLS[:, CPM:CPM + 1], COLS[:, CPM:CPM + 1], 1e-5, None, ALU.max))
                P.cop("vector", lambda e: e.reciprocal(COLS[:, CPR:CPR + 1], COLS[:, CPM:CPM + 1]))
                P.cop("vector", lambda e: e.tensor_scalar(
                    PB, PB, COLS[:, CPR:CPR + 1], 127.0, ALU.mult, ALU.mult))
                if tj >= 1:
                    P.wait("vector", "s", s3_s[(tj - 1, 7)])
                for q in range(4):
                    P.cop("vector", lambda e, q=q: e.tensor_copy(
                        TILE_I[:], PB[:, 256 * q:256 * (q + 1)]))
                    cp = P.cop("vector", lambda e, q=q: e.tensor_copy(
                        SCRA[:, 256 * q:256 * (q + 1)], TILE_I[:]))
                prep_pb_v[tj] = cp
                P.wait("tensor", "v", cp)
                if tj >= 2:
                    P.wait("tensor", "t", mm1_last[tj - 2])
                for k in range(ND):
                    n = tslot()
                    if n >= 8:
                        P.wait("tensor", "v", tps_cons[n - 8])
                    trd = P.inc("tensor", lambda e, n=n, k=k: e.transpose(
                        pslot(n), SCRA[:, ts(k)], IDBF[:]), "t")
                prep_t[tj] = trd
                base = trc[0] - 8
                P.wait("vector", "t", trd)
                for g2 in range(2):
                    cc = P.cop("vector", lambda e, tj=tj, g2=g2, base=base: e.tensor_copy(
                        IXT2[:, (tj % 2) * 1024 + 512 * g2:(tj % 2) * 1024 + 512 * (g2 + 1)],
                        TPS[:, ((base + 4 * g2) % 8) * 128:((base + 4 * g2) % 8) * 128 + 512]))
                    for n in range(base + 4 * g2, base + 4 * g2 + 4):
                        tps_cons[n] = cc
                prepvb_v[tj] = cc
                if tj >= 2:
                    P.wait("vector", "s", s1_s[(tj - 2, 7)])
                dsc_v[tj] = P.cop("vector", lambda e, tj=tj: e.tensor_tensor(
                    COLS[:, CDSC + tj % 2:CDSC + tj % 2 + 1], COLS[:, CCW1:CCW1 + 1],
                    COLS[:, CPM:CPM + 1], ALU.mult))

            emit_prep(0)
            for ti in range(NT):
                for hc in range(8):
                    P.wait("tensor", "v", prepvb_v[ti])
                    if ti == 0 and hc == 0:
                        P.wait("tensor", "v", iw1_v)
                    if hc >= 2:
                        P.wait("tensor", "s", s1_s[(ti, hc - 2)])
                    elif ti > 0:
                        P.wait("tensor", "s", s1_s[(ti - 1, 6 + hc)])
                    for k in range(ND):
                        mm = P.inc("tensor", lambda e, ti=ti, hc=hc, k=k: e.matmul(
                            HPS[:, ts(hc % 2, 512)],
                            IXT2[:, (ti % 2) * 1024 + 128 * k:(ti % 2) * 1024 + 128 * (k + 1)],
                            IW1[:, k * 4096 + 512 * hc:k * 4096 + 512 * (hc + 1)],
                            start=(k == 0), stop=(k == ND - 1)), "t")
                    mm1_last[ti] = mm
                    YBc = YB[:, ts(hc, 512)]
                    P.wait("scalar", "t", mm)
                    P.wait("scalar", "v", dsc_v[ti])
                    if ti > 0:
                        P.wait("scalar", "v", cast_v[(ti - 1, hc)])
                    if hc < 2:
                        P.wait("scalar", "v", prep_pb_v[ti])
                    if hc == 0:
                        P.wait("scalar", "t", prep_t[ti])
                    s1_s[(ti, hc)] = P.cop("scalar", lambda e, ti=ti, hc=hc, YBc=YBc:
                        e.activation(YBc, HPS[:, ts(hc % 2, 512)], AF.Relu, bias=0.0,
                                     scale=COLS[:, CDSC + ti % 2:CDSC + ti % 2 + 1]))
                    sq2 = P.cop("scalar", lambda e, YBc=YBc: e.activation(
                        YBc, YBc, AF.Square, bias=0.0, scale=1.0))
                    P.wait("vector", "s", sq2)
                    P.wait_tok("vector", asld)
                    v1 = P.cop("vector", lambda e, YBc=YBc: e.tensor_scalar(
                        YBc, YBc, COLS[:, CA:CA + 1], COLS[:, CB:CB + 1], ALU.mult, ALU.add))
                    P.wait("scalar", "v", v1)
                    s3_s[(ti, hc)] = P.cop("scalar", lambda e, hc=hc, YBc=YBc: e.activation(
                        SCRA[:, 0:512], YBc, AF.Square, bias=0.0, scale=1.0,
                        accum_out=COLS[:, SY + hc:SY + hc + 1]))
                    P.wait("vector", "s", s3_s[(ti, hc)])
                    P.wait_tok("vector", n2ld)
                    P.cop("vector", lambda e, hc=hc, YBc=YBc: e.tensor_tensor(
                        YBc, YBc, N2W[:, ts(hc, 512)], ALU.mult))
                    P.cop("vector", lambda e, hc=hc, YBc=YBc: e.tensor_reduce(
                        COLS[:, MXC + hc:MXC + hc + 1], YBc, AX.X, ALU.max,
                        apply_absolute_value=True))
                sv = P.cop("vector", lambda e: e.tensor_reduce(
                    COLS[:, CS:CS + 1], COLS[:, SY:SY + 8], AX.X, ALU.add))
                P.cop("vector", lambda e: e.tensor_reduce(
                    COLS[:, CMZ:CMZ + 1], COLS[:, MXC:MXC + 8], AX.X, ALU.max))
                P.wait("scalar", "v", sv)
                P.cop("scalar", lambda e: e.activation(
                    COLS[:, CLN:CLN + 1], COLS[:, CS:CS + 1], AF.Ln,
                    bias=getc(4096.0 * 1e-8), scale=1.0))
                rq = P.cop("scalar", lambda e: e.activation(
                    COLS[:, CRSQ:CRSQ + 1], COLS[:, CLN:CLN + 1], AF.Exp,
                    bias=getc(float(0.5 * np.log(4096.0))), scale=-0.5))
                P.wait("vector", "s", rq)
                P.cop("vector", lambda e: e.tensor_tensor(
                    COLS[:, CMXP:CMXP + 1], COLS[:, CRSQ:CRSQ + 1],
                    COLS[:, CMZ:CMZ + 1], ALU.mult))
                P.cop("vector", lambda e: e.tensor_scalar(
                    COLS[:, CMXP:CMXP + 1], COLS[:, CMXP:CMXP + 1], 1e-5, None, ALU.max))
                P.cop("vector", lambda e: e.reciprocal(COLS[:, CF:CF + 1], COLS[:, CMXP:CMXP + 1]))
                P.cop("vector", lambda e: e.tensor_scalar(
                    COLS[:, CF:CF + 1], COLS[:, CF:CF + 1],
                    COLS[:, CRSQ:CRSQ + 1], 127.0, ALU.mult, ALU.mult))
                P.cop("vector", lambda e: e.tensor_tensor(
                    COLS[:, CD2:CD2 + 1], COLS[:, CCW2:CCW2 + 1],
                    COLS[:, CMXP:CMXP + 1], ALU.mult))
                for c8 in range(8):
                    YBc = YB[:, ts(c8, 512)]
                    P.cop("vector", lambda e, YBc=YBc: e.tensor_scalar(
                        YBc, YBc, COLS[:, CF:CF + 1], None, ALU.mult))
                    if c8 >= 1:
                        P.wait("vector", "t", tr_t[(ti, c8 - 1)])
                    for q in range(2):
                        P.cop("vector", lambda e, YBc=YBc, q=q: e.tensor_copy(
                            TILE_I[:], YBc[:, 256 * q:256 * (q + 1)]))
                        ct = P.cop("vector", lambda e, q=q: e.tensor_copy(
                            QCB[:, 256 * q:256 * (q + 1)], TILE_I[:]))
                    cast_v[(ti, c8)] = ct
                    P.wait("tensor", "v", ct)
                    base = trc[0]
                    for k4 in range(4):
                        n = tslot()
                        if n >= 8:
                            P.wait("tensor", "v", tps_cons[n - 8])
                        trd = P.inc("tensor", lambda e, n=n, k4=k4: e.transpose(
                            pslot(n), QCB[:, ts(k4)], IDBF[:]), "t")
                    tr_t[(ti, c8)] = trd
                    P.wait("vector", "t", trd)
                    if c8 >= 1:
                        P.wait("vector", "t", mm2_t[(ti, c8 - 1)])
                    cc = P.cop("vector", lambda e, base=base: e.tensor_copy(
                        IHT[:], TPS[:, (base % 8) * 128:(base % 8) * 128 + 512]))
                    for n in range(base, base + 4):
                        tps_cons[n] = cc
                    P.wait("tensor", "v", cc)
                    if ti == 0:
                        P.wait("tensor", "v", iw2_v[4 * c8 + 3])
                    if ti >= 2 and c8 == 0:
                        P.wait("tensor", "v", add_v[ti - 2])
                    for k4 in range(4):
                        k = 4 * c8 + k4
                        for nn2 in range(2):
                            mm2 = P.inc("tensor", lambda e, ti=ti, k=k, k4=k4, nn2=nn2: e.matmul(
                                OPS2[:, ts((ti % 2) * 2 + nn2, 512)],
                                IHT[:, ts(k4)],
                                IW2[:, k * 1024 + 512 * nn2:k * 1024 + 512 * (nn2 + 1)],
                                start=(k == 0), stop=(k == 31)), "t")
                    mm2_t[(ti, c8)] = mm2
                if ti + 1 < NT:
                    emit_prep(ti + 1)
                P.wait("vector", "t", mm2_t[(ti, 7)])
                add_v[ti] = P.cop("vector", lambda e, ti=ti: e.scalar_tensor_tensor(
                    XRES[:, ts(ti, 1024)],
                    OPS2[:, (ti % 2) * 1024:(ti % 2) * 1024 + 1024],
                    COLS[:, CD2:CD2 + 1], XRES[:, ts(ti, 1024)], ALU.mult, ALU.add))



def l1_attn(nc, P, XRES, R1, NMR1, COLS, ST, IDBF, ONESR, c_ident, c_triu,
            lnw_bc, lnb_bc, wq1, wk1, wv1, wg1, gkw1, gkw2, gkb2_bc, rmsw_bc,
            wo1, wmlp1, bmlp1, wmlp2, bmlp2_bc, SCRA, getc, x2d, xn1td):
    HEADS = 16
    with (
        nc.sbuf_tensor("HT", [128, ND * T], BF16) as HT,
        nc.sbuf_tensor("IDF32", [128, 128], F32) as IDF32,
        nc.sbuf_tensor("TRIUF", [128, 128], F32) as TRIUF,
        nc.sbuf_tensor("MSKBF", [128, 128], BF16) as MSKBF,
        nc.sbuf_tensor("LNW", [128, D], BF16) as LNW,
        nc.sbuf_tensor("LNB", [128, D], BF16) as LNB,
        nc.sbuf_tensor("LNWI", [128, D], BF16) as LNWI,
        nc.sbuf_tensor("Z1T", [128, T], BF16) as Z1T,
        nc.sbuf_tensor("GK2S", [16, 512], BF16) as GK2S,
    ):
        idl = P.dma("y", lambda e: e.dma_start(out=IDF32[:], in_=c_ident.ap()[:, :]))
        trl = P.dma("y", lambda e: e.dma_start(out=TRIUF[:], in_=c_triu.ap()[:, :]))
        mkl = P.dma("g", lambda e: e.dma_start(out=MSKBF[:], in_=c_triu.ap()[:, :]))
        lwl = P.dma("g", lambda e: e.dma_start(out=LNW[:], in_=lnw_bc[:, :]))
        lbl = P.dma("g", lambda e: e.dma_start(out=LNB[:], in_=lnb_bc[:, :]))
        g2l = P.dma("g", lambda e: e.dma_start(out=GK2S[:], in_=gkw2[:, :]))
        for tok in (lwl, lbl, mkl, g2l):
            P.wait_tok("vector", tok)
        P.wait("vector", "v", None)
        def _recip_lnw(e, nc=nc):
            with nc.allow_low_precision("bf16 reciprocal of ln gamma (xn1 recovery)"):
                return e.reciprocal(LNWI[:], LNW[:])
        lwi = P.cop("vector", _recip_lnw)

        with (
            nc.sbuf_tensor("TA", [128, 1024], BF16) as TA,
            nc.sbuf_tensor("TB", [128, 1024], BF16) as TB,
            nc.sbuf_tensor("TC", [128, 1024], BF16) as TC,
            nc.psum_tensor("PT1", [128, 1024], BF16) as PT1,
        ):
            P.wait_tok("tensor", idl)
            tc_tok = None
            for ti in range(NT):
                P.cop("vector", lambda e, ti=ti: e.tensor_scalar(
                    TA[:], XRES[:, ts(ti, 1024)], R1[:, ti:ti + 1],
                    NMR1[:, ti:ti + 1], ALU.mult, ALU.add))
                P.cop("vector", lambda e: e.tensor_tensor(TB[:], TA[:], LNW[:], ALU.mult))
                hb = P.cop("vector", lambda e: e.tensor_tensor(TB[:], TB[:], LNB[:], ALU.add))
                P.wait("tensor", "v", hb)
                for k in range(ND):
                    trd = P.inc("tensor", lambda e, k=k: e.transpose(
                        PT1[:, ts(k)], TA[:, ts(k)], IDBF[:]), "t")
                P.wait("vector", "t", trd)
                if tc_tok is not None:
                    P.wait_tok("vector", tc_tok)
                for k in range(ND):
                    P.cop("vector", lambda e, k=k: e.tensor_copy(TC[:, ts(k)], PT1[:, ts(k)]))
                tcd = P.cnt["v"]
                P.wait("sync", "v", tcd)
                tc_tok = P.dma("y", lambda e, ti=ti: e.dma_start(
                    out=xn1td.rearrange("p (k t) -> p k t", k=ND)[:, :, 128 * ti:128 * (ti + 1)],
                    in_=TC[:].rearrange("p (k t) -> p k t", k=ND)))
                P.wait("tensor", "v", tcd)
                for k in range(ND):
                    trd = P.inc("tensor", lambda e, k=k: e.transpose(
                        PT1[:, ts(k)], TB[:, ts(k)], IDBF[:]), "t")
                P.wait("vector", "t", trd)
                for k in range(ND):
                    P.cop("vector", lambda e, ti=ti, k=k: e.tensor_copy(
                        HT[:, k * T + 128 * ti: k * T + 128 * (ti + 1)], PT1[:, ts(k)]))
                P.wait("tensor", "v", P.cnt["v"])

        P.fence()
        with (
            nc.sbuf_tensor("GK1S", [128, ND * 16], BF16) as GK1S,
            nc.sbuf_tensor("Z1", [128, 16], BF16) as Z1,
            nc.psum_tensor("ZP", [128, 512], F32) as ZP,
            nc.psum_tensor("ZPT", [128, 128], BF16) as ZPT,
        ):
            g1l = P.dma("g", lambda e: e.dma_start(
                out=GK1S[:].rearrange("p (k o) -> p k o", k=ND),
                in_=gkw1.rearrange("(k p) o -> p k o", p=128)))
            P.wait_tok("tensor", g1l)
            for ti in range(NT):
                for k in range(ND):
                    mm = P.inc("tensor", lambda e, ti=ti, k=k: e.matmul(
                        ZP[:, 0:16],
                        HT[:, k * T + 128 * ti: k * T + 128 * (ti + 1)],
                        GK1S[:, 16 * k: 16 * (k + 1)],
                        start=(k == 0), stop=(k == ND - 1)), "t")
                P.wait("vector", "t", mm)
                zv = P.cop("vector", lambda e: e.tensor_copy(Z1[:], ZP[:, 0:16]))
                P.wait("tensor", "v", zv)
                trd = P.inc("tensor", lambda e: e.transpose(
                    ZPT[0:16, 0:128], Z1[:], IDBF[:]), "t")
                P.wait("vector", "t", trd)
                P.cop("vector", lambda e, ti=ti: e.tensor_copy(
                    Z1T[0:16, 128 * ti:128 * (ti + 1)], ZPT[0:16, 0:128]))
                P.wait("tensor", "v", P.cnt["v"])

        P.fence()
        es = contextlib.ExitStack()
        WQ = es.enter_context(nc.sbuf_tensor("WQ", [128, ND * 512], BF16))
        WK = es.enter_context(nc.sbuf_tensor("WK", [128, ND * 512], BF16))
        WV = es.enter_context(nc.sbuf_tensor("WV", [128, ND * 1024], BF16))
        WG = es.enter_context(nc.sbuf_tensor("WG", [128, ND * 1024], BF16))
        WO = es.enter_context(nc.sbuf_tensor("WO", [128, ND * 1024], BF16))
        GKB = es.enter_context(nc.sbuf_tensor("GKB", [128, 512], BF16))
        RMSW = es.enter_context(nc.sbuf_tensor("RMSW", [128, D], BF16))
        QC = es.enter_context(nc.sbuf_tensor("QC", [128, 512], BF16))
        KC = es.enter_context(nc.sbuf_tensor("KC", [128, 512], BF16))
        VC = es.enter_context(nc.sbuf_tensor("VC", [128, 1024], BF16))
        GC = es.enter_context(nc.sbuf_tensor("GC", [128, 1024], BF16))
        SPF = es.enter_context(nc.sbuf_tensor("SPF", [128, 512], F32))
        ASB = es.enter_context(nc.sbuf_tensor("ASB1", [128, 512], F32))
        E1 = es.enter_context(nc.sbuf_tensor("E1", [128, 512], BF16))
        E2 = es.enter_context(nc.sbuf_tensor("E2", [128, 512], BF16))
        ALC = es.enter_context(nc.sbuf_tensor("ALC", [128, 4], F32))
        EALC = es.enter_context(nc.sbuf_tensor("EALC", [128, 4], F32))
        QPT = es.enter_context(nc.sbuf_tensor("QPT", [128, 512], BF16))
        KPT = es.enter_context(nc.sbuf_tensor("KPT", [128, 512], BF16))
        PB = es.enter_context(nc.sbuf_tensor("PB", [128, 128], BF16))
        OG = es.enter_context(nc.sbuf_tensor("OG", [128, 1024], BF16))
        OGT = es.enter_context(nc.sbuf_tensor("OGT", [128, 1024], BF16))
        SSB = es.enter_context(nc.sbuf_tensor("SSB", [128, 256], F32))
        SBB = es.enter_context(nc.sbuf_tensor("SBB", [128, 256], BF16))
        OCOL = es.enter_context(nc.sbuf_tensor("OCOL", [128, 16], F32))
        INVC = es.enter_context(nc.sbuf_tensor("INVC", [128, 16], F32))
        XNC = es.enter_context(nc.sbuf_tensor("XNC", [128, 1024], BF16))
        X2C = es.enter_context(nc.sbuf_tensor("X2C", [128, 1024], F32))
        SQO = es.enter_context(nc.sbuf_tensor("SQO", [128, 1024], BF16))
        PRJ = es.enter_context(nc.psum_tensor("PRJ", [128, 1024], F32))
        TPB = es.enter_context(nc.psum_tensor("TPB", [128, 1024], BF16))
        PPS = es.enter_context(nc.psum_tensor("PPS", [128, 512], F32))
        OP2 = es.enter_context(nc.psum_tensor("OP2", [128, 1024], F32))
        XP = es.enter_context(nc.psum_tensor("XP", [128, 1024], F32))
        for wsrc, wdst in ((wq1, WQ), (wk1, WK), (wv1, WV), (wg1, WG), (wo1, WO)):
            P.dma("g", lambda e, wsrc=wsrc, wdst=wdst: e.dma_start(
                out=wdst[:].rearrange("p (k o) -> p k o", k=ND),
                in_=wsrc.rearrange("(k p) o -> p k o", p=128)))
        P.dma("g", lambda e: e.dma_start(out=GKB[:], in_=gkb2_bc[:, :]))
        P.dma("g", lambda e: e.dma_start(out=RMSW[:], in_=rmsw_bc[:, :]))
        P.dma_barrier("tensor")
        P.dma_barrier("vector")
        P.wait("vector", "v", None)
        P.cop("vector", lambda e: e.memset(SSB[:], 0.0))
        P.cop("vector", lambda e: e.memset(SBB[:], 0.0))
        x2_tok = None

        for c in range(NT):
            def hsl(k, c=c):
                return HT[:, k * T + 128 * c: k * T + 128 * (c + 1)]
            P.wait("tensor", "v", P.cnt["v"])
            for p0, wbuf in ((0, WQ), (512, WK)):
                for k in range(ND):
                    hk = hsl(k)
                    mm = P.inc("tensor", lambda e, wbuf=wbuf, p0=p0, k=k, hk=hk: e.matmul(
                        PRJ[:, p0:p0 + 512], hk,
                        wbuf[:, 512 * k: 512 * (k + 1)],
                        start=(k == 0), stop=(k == ND - 1)), "t")
            P.wait("vector", "t", P.cnt["t"])
            P.cop("vector", lambda e: e.tensor_copy(QC[:], PRJ[:, 0:512]))
            ev = P.cop("vector", lambda e: e.tensor_copy(KC[:], PRJ[:, 512:1024]))
            P.wait("tensor", "v", ev)
            for dst, wbuf in ((VC, WV), (GC, WG)):
                for k in range(ND):
                    hk = hsl(k)
                    for nn in range(2):
                        mm = P.inc("tensor", lambda e, wbuf=wbuf, k=k, nn=nn, hk=hk: e.matmul(
                            PRJ[:, ts(nn, 512)], hk,
                            wbuf[:, 1024 * k + 512 * nn: 1024 * k + 512 * (nn + 1)],
                            start=(k == 0), stop=(k == ND - 1)), "t")
                P.wait("vector", "t", mm)
                if dst is VC:
                    ev = P.cop("vector", lambda e: e.tensor_copy(VC[:], PRJ[:, 0:1024]))
                else:
                    P.wait("scalar", "t", mm)
                    P.wait("scalar", "s", P.cnt["s"])
                    sgd = P.inc("scalar", lambda e: e.activation(
                        SQO[:], PRJ[:, 0:1024], AF.Exp, bias=0.0, scale=-1.0), "s")
                    P.wait("vector", "s", sgd)
                    P.cop("vector", lambda e: e.tensor_scalar(SQO[:], SQO[:], 1.0, None, ALU.add))
                    def _recip_sq(e, nc=nc):
                        with nc.allow_low_precision("bf16 sigmoid denominator"):
                            return e.reciprocal(SQO[:], SQO[:])
                    P.cop("vector", _recip_sq)
                    P.cop("vector", lambda e: e.tensor_tensor(GC[:], PRJ[:, 0:1024], SQO[:], ALU.mult))
                    ev = P.cop("vector", lambda e: e.tensor_tensor(GC[:], GC[:], RMSW[:], ALU.mult))
                P.wait("tensor", "v", ev)
            mm = P.inc("tensor", lambda e, c=c: e.matmul(
                PRJ[:, 0:512], Z1T[0:16, 128 * c:128 * (c + 1)], GK2S[:],
                start=True, stop=True), "t")
            P.wait("vector", "t", mm)
            zb = P.cop("vector", lambda e: e.tensor_tensor(SPF[:], PRJ[:, 0:512], GKB[:], ALU.add))
            P.wait("scalar", "v", zb)
            P.wait("scalar", "s", P.cnt["s"])
            P.inc("scalar", lambda e: e.activation(SPF[:], SPF[:], AF.Exp, bias=0.0, scale=-1.0), "s")
            P.wait("scalar", "s", P.cnt["s"])
            spd = P.inc("scalar", lambda e: e.activation(SPF[:], SPF[:], AF.Ln, bias=getc(1.0), scale=1.0), "s")
            P.wait("tensor", "s", spd)
            mm = P.inc("tensor", lambda e: e.matmul(
                PRJ[:, 0:512], TRIUF[:], SPF[:], start=True, stop=True), "t")
            P.wait("vector", "t", mm)
            asd = P.cop("vector", lambda e: e.tensor_copy(ASB[:], PRJ[:, 0:512]))
            P.wait("scalar", "v", asd)
            P.wait("scalar", "s", P.cnt["s"])
            P.inc("scalar", lambda e: e.activation(
                E1[:], ASB[:], AF.Exp, bias=getc(float(-0.5 * np.log(32.0))),
                scale=-1.0 / 16.0), "s")
            P.wait("scalar", "s", P.cnt["s"])
            e2d = P.inc("scalar", lambda e: e.activation(
                E2[:], ASB[:], AF.Exp, bias=0.0, scale=1.0 / 16.0), "s")
            P.wait("tensor", "v", asd)
            for b in range(4):
                trd = P.inc("tensor", lambda e, b=b: e.transpose(
                    PRJ[:, 512 + 128 * b: 512 + 128 * (b + 1)],
                    ASB[:, ts(b)], IDF32[:]), "t")
            P.wait("vector", "t", trd)
            for b in range(4):
                P.cop("vector", lambda e, b=b: e.tensor_copy(
                    ALC[:, b:b + 1], PRJ[:, 512 + 128 * b + 127: 512 + 128 * b + 128]))
            alc = P.cnt["v"]
            P.wait("scalar", "v", alc)
            P.wait("scalar", "s", P.cnt["s"])
            ead = P.inc("scalar", lambda e: e.activation(
                EALC[:], ALC[:], AF.Exp, bias=0.0, scale=-1.0 / 16.0), "s")
            P.wait("vector", "s", e2d)
            P.cop("vector", lambda e: e.tensor_tensor(QC[:], QC[:], E1[:], ALU.mult))
            qpd = P.cop("vector", lambda e: e.tensor_tensor(KC[:], KC[:], E2[:], ALU.mult))
            P.wait("tensor", "v", qpd)
            for b in range(4):
                trd = P.inc("tensor", lambda e, b=b: e.transpose(
                    TPB[:, ts(b)], QC[:, ts(b)], IDBF[:]), "t")
            P.wait("vector", "t", trd)
            for b in range(4):
                P.cop("vector", lambda e, b=b: e.tensor_copy(QPT[:, ts(b)], TPB[:, ts(b)]))
            P.wait("tensor", "v", P.cnt["v"])
            for b in range(4):
                trd = P.inc("tensor", lambda e, b=b: e.transpose(
                    TPB[:, ts(b)], KC[:, ts(b)], IDBF[:]), "t")
            P.wait("vector", "t", trd)
            for b in range(4):
                P.cop("vector", lambda e, b=b: e.tensor_copy(KPT[:, ts(b)], TPB[:, ts(b)]))
            kpt = P.cnt["v"]
            P.wait("vector", "s", ead)
            P.wait("tensor", "v", kpt)
            for h in range(HEADS):
                r0 = 32 * (h % 4)
                cb = 128 * (h // 4)
                mm = P.inc("tensor", lambda e, r0=r0, cb=cb: e.matmul(
                    PPS[:, 0:128],
                    KPT[r0:r0 + 32, cb:cb + 128],
                    QPT[r0:r0 + 32, cb:cb + 128],
                    start=True, stop=True, tile_position=(r0, 0)), "t")
                P.wait("vector", "t", mm)
                P.wait("vector", "s", ead)
                msk = P.cop("vector", lambda e: e.tensor_tensor(
                    PB[:], PPS[:, 0:128], MSKBF[:], ALU.mult))
                P.wait("tensor", "v", msk)
                P.inc("tensor", lambda e, h=h: e.matmul(
                    OP2[:, 64 * h: 64 * h + 64], PB[:], VC[:, 64 * h: 64 * h + 64],
                    start=(h % 8 == 0), stop=False), "t")
                P.inc("tensor", lambda e, h=h, r0=r0, cb=cb: e.matmul(
                    OP2[:, 64 * h: 64 * h + 64],
                    QPT[r0:r0 + 32, cb:cb + 128],
                    SBB[r0:r0 + 32, 64 * (h // 4): 64 * (h // 4) + 64],
                    start=False, stop=(h % 8 == 7), tile_position=(r0, 0)), "t")
                ds = P.inc("tensor", lambda e, h=h, r0=r0: e.matmul(
                    PPS[r0:r0 + 32, 128 + 64 * (h // 4): 192 + 64 * (h // 4)],
                    KC[:, 32 * h: 32 * h + 32], VC[:, 64 * h: 64 * h + 64],
                    start=True, stop=True, tile_position=(0, r0)), "t")
                P.wait("vector", "t", ds)
                P.cop("vector", lambda e, h=h, r0=r0: e.tensor_tensor(
                    SSB[r0:r0 + 32, 64 * (h // 4): 64 * (h // 4) + 64],
                    SSB[r0:r0 + 32, 64 * (h // 4): 64 * (h // 4) + 64],
                    PPS[r0:r0 + 32, 128 + 64 * (h // 4): 192 + 64 * (h // 4)], ALU.add))
                P.cop("vector", lambda e, h=h, r0=r0: e.tensor_scalar(
                    SSB[r0:r0 + 32, 64 * (h // 4): 64 * (h // 4) + 64],
                    SSB[r0:r0 + 32, 64 * (h // 4): 64 * (h // 4) + 64],
                    EALC[r0:r0 + 32, h // 4: h // 4 + 1], None, ALU.mult))
                P.cop("vector", lambda e, h=h, r0=r0: e.tensor_copy(
                    SBB[r0:r0 + 32, 64 * (h // 4): 64 * (h // 4) + 64],
                    SSB[r0:r0 + 32, 64 * (h // 4): 64 * (h // 4) + 64]))
            P.wait("vector", "t", P.cnt["t"])
            P.wait("scalar", "t", P.cnt["t"])
            P.wait("scalar", "s", P.cnt["s"])
            sqd = P.inc("scalar", lambda e: e.activation(
                SQO[:], OP2[:, 0:1024], AF.Square, bias=0.0, scale=1.0), "s")
            P.wait("vector", "s", sqd)
            for h in range(HEADS):
                P.cop("vector", lambda e, h=h: e.tensor_reduce(
                    OCOL[:, h:h + 1], SQO[:, 64 * h:64 * (h + 1)], AX.X, ALU.add))
            oc = P.cnt["v"]
            P.wait("scalar", "v", oc)
            P.wait("scalar", "s", P.cnt["s"])
            P.inc("scalar", lambda e: e.activation(
                INVC[:], OCOL[:], AF.Ln, bias=getc(1e-5), scale=1.0 / 64.0), "s")
            P.wait("scalar", "s", P.cnt["s"])
            ivd = P.inc("scalar", lambda e: e.activation(
                INVC[:], INVC[:], AF.Exp, bias=0.0, scale=-0.5), "s")
            P.wait("vector", "s", ivd)
            for h in range(HEADS):
                P.cop("vector", lambda e, h=h: e.scalar_tensor_tensor(
                    OG[:, 64 * h:64 * (h + 1)], OP2[:, 64 * h:64 * (h + 1)],
                    INVC[:, h:h + 1], GC[:, 64 * h:64 * (h + 1)],
                    ALU.mult, ALU.mult))
            ogd = P.cnt["v"]
            P.wait("tensor", "v", ogd)
            for b in range(ND):
                trd = P.inc("tensor", lambda e, b=b: e.transpose(
                    TPB[:, ts(b)], OG[:, ts(b)], IDBF[:]), "t")
            P.wait("vector", "t", trd)
            for b in range(ND):
                P.cop("vector", lambda e, b=b: e.tensor_copy(OGT[:, ts(b)], TPB[:, ts(b)]))
            ogt = P.cnt["v"]
            P.wait("tensor", "v", ogt)
            for k in range(ND):
                for nn in range(2):
                    wo_mm = P.inc("tensor", lambda e, k=k, nn=nn: e.matmul(
                        XP[:, ts(nn, 512)], OGT[:, ts(k)],
                        WO[:, 1024 * k + 512 * nn: 1024 * k + 512 * (nn + 1)],
                        start=(k == 0), stop=(k == ND - 1)), "t")
            for k in range(ND):
                trd = P.inc("tensor", lambda e, k=k, c=c: e.transpose(
                    TPB[:, ts(k)], HT[:, k * T + 128 * c: k * T + 128 * (c + 1)], IDBF[:]), "t")
            P.wait("vector", "t", trd)
            if x2_tok is not None:
                P.wait_tok("vector", x2_tok)
            for k in range(ND):
                P.cop("vector", lambda e, k=k: e.tensor_copy(XNC[:, ts(k)], TPB[:, ts(k)]))
            P.cop("vector", lambda e: e.tensor_tensor(X2C[:], XNC[:], LNB[:], ALU.subtract))
            P.cop("vector", lambda e: e.tensor_tensor(X2C[:], X2C[:], LNWI[:], ALU.mult))
            P.wait("vector", "t", wo_mm)
            P.cop("vector", lambda e: e.tensor_tensor(X2C[:], X2C[:], XP[:, 0:1024], ALU.add))
            P.cop("vector", lambda e, c=c: e.tensor_tensor(
                XRES[:, ts(c, 1024)], XRES[:, ts(c, 1024)], X2C[:], ALU.add))
            x2b = P.cop("vector", lambda e: e.tensor_copy(XNC[:], X2C[:]))
            P.wait("sync", "v", x2b)
            x2_tok = P.dma("y", lambda e, c=c: e.dma_start(out=x2d[ts(c), :], in_=XNC[:]))
            P.wait("tensor", "v", P.cnt["v"])
        es.close()
    P.fence()
    l1_mlp(nc, P, XRES, COLS, ST, IDBF, SCRA, wmlp1, bmlp1, wmlp2, bmlp2_bc,
           getc, x2d, lnw_bc, lnb_bc)


def l1_mlp(nc, P, XRES, COLS, ST, IDBF, SCRA, wmlp1, bmlp1, wmlp2, bmlp2_bc,
           getc, x2d, lnw_bc, lnb_bc):
    R2C, NMR2C = ST[:, 8 * NT:9 * NT], ST[:, 9 * NT:10 * NT]
    with (
        nc.sbuf_tensor("X2G", [128, 4 * 1024], BF16) as X2G,
        nc.sbuf_tensor("X2T", [128, ND * 512], BF16) as X2T,
        nc.sbuf_tensor("HMT", [128, 16 * 512], BF16) as HMT,
        nc.sbuf_tensor("MW1", [128, ND * 2048], BF16) as MW1,
        nc.sbuf_tensor("MW2", [128, 16 * 1024], BF16) as MW2,
        nc.sbuf_tensor("BM1", [128, 16], F32) as BM1,
        nc.sbuf_tensor("BM2", [128, D], F32) as BM2,
        nc.sbuf_tensor("TD", [128, 1024], BF16) as TD,
        nc.sbuf_tensor("LNW2", [128, D], BF16) as LNW2,
        nc.sbuf_tensor("LNB2", [128, D], BF16) as LNB2,
        nc.psum_tensor("MPS", [128, 1024], F32) as MPS,
        nc.psum_tensor("TPM", [128, 1024], BF16) as TPM,
        nc.psum_tensor("OPM", [128, 2048], F32) as OPM,
    ):
        w1tok = P.dma("g", lambda e: e.dma_start(
            out=MW1[:].rearrange("p (k o) -> p k o", k=ND),
            in_=wmlp1.rearrange("(k p) o -> p k o", p=128)))
        w2tok = P.dma("g", lambda e: e.dma_start(
            out=MW2[:].rearrange("p (k o) -> p k o", k=16),
            in_=wmlp2.rearrange("(k p) o -> p k o", p=128)))
        b1tok = P.dma("y", lambda e: e.dma_start(out=BM1[:], in_=bmlp1[:, :]))
        lw2 = P.dma("g", lambda e: e.dma_start(out=LNW2[:], in_=lnw_bc[:, :]))
        lb2 = P.dma("g", lambda e: e.dma_start(out=LNB2[:], in_=lnb_bc[:, :]))
        P.wait_tok("vector", lw2)
        P.wait_tok("vector", lb2)
        b2tok = P.dma("y", lambda e: e.dma_start(out=BM2[:], in_=bmlp2_bc[:, :]))
        P.wait_tok("tensor", w1tok)
        P.wait_tok("tensor", w2tok)
        P.wait_tok("scalar", b1tok)
        P.wait_tok("vector", b2tok)
        x2toks = []
        for gb in range(4):
            x2toks.append(P.dma("y", lambda e, gb=gb: e.dma_start(
                out=X2G[:, ts(gb, 1024)], in_=x2d[ts(gb), :])))
        SUM, SUM2 = COLS[:, 0:NT], COLS[:, 16:16 + NT]
        for ti in range(NT):
            gslot = ti % 4
            P.wait_tok("vector", x2toks[ti])
            P.wait_tok("scalar", x2toks[ti])
            P.cop("vector", lambda e, gslot=gslot, ti=ti: e.tensor_reduce(
                SUM[:, ti:ti + 1], X2G[:, ts(gslot, 1024)], AX.X, ALU.add))
            P.wait("scalar", "s", P.cnt["s"])
            sq = P.inc("scalar", lambda e, gslot=gslot, ti=ti: e.activation(
                SCRA[:], X2G[:, ts(gslot, 1024)], AF.Square, bias=0.0, scale=1.0,
                accum_out=SUM2[:, ti:ti + 1]), "s")
            if ti + 4 < NT:
                P.wait("sync", "v", P.cnt["v"])
                P.wait("sync", "s", P.cnt["s"])
                x2toks.append(P.dma("y", lambda e, ti=ti: e.dma_start(
                    out=X2G[:, ts((ti + 4) % 4, 1024)], in_=x2d[ts(ti + 4), :])))
        P.wait("vector", "s", P.cnt["s"])
        P.wait("vector", "v", None)
        M = COLS[:, 32:32 + NT]
        VAR = COLS[:, 0:NT]
        P.cop("vector", lambda e: e.tensor_scalar(M, SUM, 1.0 / D, None, ALU.mult))
        P.cop("vector", lambda e: e.tensor_scalar(SUM2, SUM2, 1.0 / D, None, ALU.mult))
        P.cop("vector", lambda e: e.tensor_tensor(VAR, M, M, ALU.mult))
        vd = P.cop("vector", lambda e: e.tensor_tensor(VAR, SUM2, VAR, ALU.subtract))
        P.wait("scalar", "v", vd)
        P.wait("scalar", "s", P.cnt["s"])
        P.inc("scalar", lambda e: e.activation(SUM2, VAR, AF.Ln, bias=getc(1e-5), scale=1.0), "s")
        P.wait("scalar", "s", P.cnt["s"])
        rd = P.inc("scalar", lambda e: e.activation(R2C, SUM2, AF.Exp, bias=0.0, scale=-0.5), "s")
        P.wait("vector", "s", rd)
        P.cop("vector", lambda e: e.tensor_tensor(NMR2C, M, R2C, ALU.mult))
        nm = P.cop("vector", lambda e: e.tensor_scalar(NMR2C, NMR2C, -1.0, None, ALU.mult))
        P.wait("sync", "v", P.cnt["v"])
        P.wait("sync", "s", P.cnt["s"])
        for g in range(4):
            for gb in range(4):
                ti = 4 * g + gb
                P.wait("sync", "v", P.cnt["v"])
                tok = P.dma("y", lambda e, gb=gb, ti=ti: e.dma_start(
                    out=X2G[:, ts(gb, 1024)], in_=x2d[ts(ti), :]))
                P.wait_tok("vector", tok)
                P.wait("vector", "v", nm)
                P.cop("vector", lambda e, gb=gb, ti=ti: e.tensor_scalar(
                    TD[:], X2G[:, ts(gb, 1024)], R2C[:, ti:ti + 1],
                    NMR2C[:, ti:ti + 1], ALU.mult, ALU.add))
                P.cop("vector", lambda e: e.tensor_tensor(TD[:], TD[:], LNW2[:], ALU.mult))
                xn = P.cop("vector", lambda e: e.tensor_tensor(TD[:], TD[:], LNB2[:], ALU.add))
                P.wait("tensor", "v", xn)
                for k in range(ND):
                    trd = P.inc("tensor", lambda e, k=k: e.transpose(
                        TPM[:, ts(k)], TD[:, ts(k)], IDBF[:]), "t")
                P.wait("vector", "t", trd)
                for k in range(ND):
                    P.cop("vector", lambda e, k=k, gb=gb: e.tensor_copy(
                        X2T[:, k * 512 + 128 * gb: k * 512 + 128 * (gb + 1)], TPM[:, ts(k)]))
                P.wait("tensor", "v", P.cnt["v"])
            evs = 0
            for ht in range(16):
                if ht >= 2:
                    P.wait("tensor", "s", evs - 1)
                for k in range(ND):
                    mm = P.inc("tensor", lambda e, ht=ht, k=k: e.matmul(
                        MPS[:, ts(ht % 2, 512)],
                        MW1[:, 2048 * k + 128 * ht: 2048 * k + 128 * (ht + 1)],
                        X2T[:, ts(k, 512)],
                        start=(k == 0), stop=(k == ND - 1)), "t")
                P.wait("scalar", "t", mm)
                P.wait("scalar", "s", P.cnt["s"])
                evs = P.inc("scalar", lambda e, ht=ht: e.activation(
                    HMT[:, ts(ht, 512)], MPS[:, ts(ht % 2, 512)], AF.Gelu,
                    bias=BM1[:, ht:ht + 1], scale=1.0), "s")
            P.wait("tensor", "s", evs)
            for tc in range(4):
                ti = 4 * g + tc
                for k in range(16):
                    for nn in range(2):
                        mm = P.inc("tensor", lambda e, tc=tc, k=k, nn=nn: e.matmul(
                            OPM[:, ts((tc % 2) * 2 + nn, 512)],
                            HMT[:, 512 * k + 128 * tc: 512 * k + 128 * (tc + 1)],
                            MW2[:, 1024 * k + 512 * nn: 1024 * k + 512 * (nn + 1)],
                            start=(k == 0), stop=(k == 15)), "t")
                P.wait("vector", "t", mm)
                P.cop("vector", lambda e, ti=ti, tc=tc: e.tensor_tensor(
                    XRES[:, ts(ti, 1024)], XRES[:, ts(ti, 1024)],
                    OPM[:, (tc % 2) * 1024: (tc % 2) * 1024 + 1024], ALU.add))
                ad = P.cop("vector", lambda e, ti=ti: e.tensor_tensor(
                    XRES[:, ts(ti, 1024)], XRES[:, ts(ti, 1024)], BM2[:], ALU.add))
                P.wait("tensor", "v", ad)



def l1_ffn(nc, P, XRES, COLS, IDBF, w1t1, b1f, w2t1, b2f_bc, getc, xn1td):
    with (
        nc.sbuf_tensor("XN1F", [128, ND * T], BF16) as XN1F,
        nc.sbuf_tensor("HQT", [128, 32 * T], FP8) as HQT,
        nc.sbuf_tensor("QW2", [128, 32 * 1024], FP8) as QW2,
        nc.sbuf_tensor("MWS", [128, ND * 128], BF16) as MWS,
        nc.sbuf_tensor("B1F", [128, 32], F32) as B1F,
        nc.sbuf_tensor("BQ16", [128, D], mybir.dt.float16) as BQ16,
        nc.sbuf_tensor("HF", [128, 2048], F32) as HF,
        nc.psum_tensor("HP1", [128, 2048], F32) as HP1,
        nc.psum_tensor("OPF", [128, 2048], F32) as OPF,
    ):
        OT16 = HF[:, 0:512].bitcast(mybir.dt.float16)
        xl = P.dma("y", lambda e: e.dma_start(out=XN1F[:], in_=xn1td[:, :]))
        ql = P.dma("g", lambda e: e.dma_start(
            out=QW2[:].rearrange("p (k o) -> p k o", k=32),
            in_=w2t1.rearrange("(k p) o -> p k o", p=128)))
        bl = P.dma("y", lambda e: e.dma_start(out=B1F[:], in_=b1f[:, :]))
        b2l = P.dma("g", lambda e: e.dma_start(out=BQ16[:], in_=b2f_bc[:, :]))
        P.wait_tok("tensor", xl)
        P.wait_tok("scalar", bl)
        for ht in range(32):
            P.wait("gpsimd", "t", P.cnt["t"])
            wtok = P.dma("g", lambda e, ht=ht: e.dma_start(
                out=MWS[:].rearrange("p (k o) -> p k o", k=ND),
                in_=w1t1.rearrange("(k p) o -> p k o", p=128)[:, :, 128 * ht:128 * (ht + 1)]))
            P.wait_tok("tensor", wtok)
            for k in range(ND):
                for nn in range(4):
                    mm = P.inc("tensor", lambda e, ht=ht, k=k, nn=nn: e.matmul(
                        HP1[:, ts(nn, 512)],
                        MWS[:, 128 * k: 128 * (k + 1)],
                        XN1F[:, k * T + 512 * nn: k * T + 512 * (nn + 1)],
                        start=(k == 0), stop=(k == ND - 1)), "t")
            P.wait("scalar", "t", mm)
            P.wait("scalar", "s", P.cnt["s"])
            hr = P.inc("scalar", lambda e, ht=ht: e.activation(
                HF[:], HP1[:], AF.Relu, bias=B1F[:, ht:ht + 1], scale=1.0), "s")
            P.wait("vector", "s", hr)
            hq = P.cop("vector", lambda e, ht=ht: e.tensor_copy(
                HQT[:, ts(ht, 2048)], HF[:]))
            P.wait("tensor", "v", hq)
        P.wait_tok("tensor", ql)
        for ti in range(NT):
            for k in range(32):
                for nn in range(2):
                    mm = P.inc("tensor", lambda e, ti=ti, k=k, nn=nn: e.matmul(
                        OPF[:, ts((ti % 2) * 2 + nn, 512)],
                        HQT[:, 2048 * k + 128 * ti: 2048 * k + 128 * (ti + 1)],
                        QW2[:, 1024 * k + 512 * nn: 1024 * k + 512 * (nn + 1)],
                        start=(k == 0), stop=(k == 31)), "t")
            P.wait("vector", "t", mm)
            P.wait_tok("vector", b2l)
            P.cop("vector", lambda e, ti=ti: e.tensor_tensor(
                OT16[:], OPF[:, (ti % 2) * 1024: (ti % 2) * 1024 + 1024], BQ16[:], ALU.add))
            ad = P.cop("vector", lambda e, ti=ti: e.tensor_tensor(
                XRES[:, ts(ti, 1024)], XRES[:, ts(ti, 1024)], OT16[:], ALU.add))
            P.wait("tensor", "v", ad)



_CACHE = {}


def _build_program(stop_after="full"):
    if stop_after not in _CACHE:
        nc = bass.Bass("TRN2", target_bir_lowering=False, debug=False)
        P = build(nc, stop_after)
        with nc.Block() as blk:
            P.run(blk)
        _CACHE[stop_after] = nc
    return _CACHE[stop_after]


def _in_maps(inputs):
    d = inputs
    f32 = np.float32

    def asc(a):
        return np.ascontiguousarray(np.asarray(a, dtype=f32))

    shared = {
        "mq": asc(d["L0_attn_Mq"]),
        "mk": asc(d["L0_attn_Mk"]),
        "mv": asc(d["L0_attn_Mv"]),
        "w1t0": asc(np.asarray(d["L0_ffn_w1"], dtype=f32).T),
        "w2t0": asc(np.asarray(d["L0_ffn_w2"], dtype=f32).T),
        "n1w_bc": asc(np.broadcast_to(np.asarray(d["L0_ffn_norm1_w"], f32), (128, D))),
        "n2w_bc": asc(np.broadcast_to(np.asarray(d["L0_ffn_norm2_w"], f32), (128, 4096))),
        "actsb": asc(np.broadcast_to(np.concatenate([
            np.asarray(d["L0_ffn_act_scale"], f32).reshape(1),
            np.asarray(d["L0_ffn_act_bias"], f32).reshape(1)]), (128, 2))),
        "lnw_bc": asc(np.broadcast_to(np.asarray(d["L1_attn_ln_w"], f32), (128, D))),
        "lnb_bc": asc(np.broadcast_to(np.asarray(d["L1_attn_ln_b"], f32), (128, D))),
        "wq1": asc(d["L1_attn_Wq"]),
        "wk1": asc(d["L1_attn_Wk"]),
        "wv1": asc(d["L1_attn_Wv"]),
        "wg1": asc(d["L1_attn_Wg"]),
        "gkw1": asc(d["L1_attn_gk_w1"]),
        "gkw2": asc(d["L1_attn_gk_w2"]),
        "gkb2_bc": asc(np.broadcast_to(np.asarray(d["L1_attn_gk_b2"], f32), (128, 512))),
        "rmsw_bc": asc(np.broadcast_to(np.tile(np.asarray(d["L1_attn_rms_w"], f32), 16), (128, D))),
        "wo1": asc(d["L1_attn_Wo"]),
        "wmlp1": asc(d["L1_attn_W1"]),
        "bmlp1": asc(np.asarray(d["L1_attn_b1"], f32).reshape(16, 128).T),
        "wmlp2": asc(d["L1_attn_W2"]),
        "bmlp2_bc": asc(np.broadcast_to(np.asarray(d["L1_attn_b2"], f32), (128, D))),
        "w1t1": asc(np.asarray(d["L1_ffn_w1"], dtype=f32).T),
        "b1f": asc(np.asarray(d["L1_ffn_b1"], f32).reshape(32, 128).T),
        "w2t1": asc(np.asarray(d["L1_ffn_w2"], dtype=f32).T),
        "b2f_bc": asc(np.broadcast_to(np.asarray(d["L1_ffn_b2"], f32), (128, D))),
    }
    x = np.asarray(d["L0_attn_x"], dtype=f32)
    return [{**shared, "x": np.ascontiguousarray(x[b])} for b in range(x.shape[0])]


def kernel(**inputs):
    nc = _build_program("full")
    maps = _in_maps(inputs)
    res = run_bass_kernel_spmd(nc, maps, core_ids=list(range(8)))
    return np.stack([r["y"] for r in res.results], axis=0)


def kernel_dev(stop_after, inputs, cores=1):
    nc = _build_program(stop_after)
    maps = _in_maps(inputs)[:cores]
    res = run_bass_kernel_spmd(nc, maps, core_ids=list(range(cores)))
    return np.stack([r["y"] for r in res.results], axis=0)


# version 10
# speedup vs baseline: 1.1800x; 1.1800x over previous
import contextlib
import numpy as np
import concourse.bass as bass
import concourse.mybir as mybir
from concourse.bass_utils import run_bass_kernel_spmd

F32 = mybir.dt.float32
F32R = mybir.dt.float32r
BF16 = mybir.dt.bfloat16
FP8 = mybir.dt.float8e4
I32 = mybir.dt.int32
AF = mybir.ActivationFunctionType
ALU = mybir.AluOpType
AX = mybir.AxisListType

T = 2048
D = 1024
NT = 16
ND = 8


def ts(i, n=128):
    return slice(i * n, (i + 1) * n)


def TPSLICE(PT, base):
    return PT[:, (base % 8) * 128:(base % 8) * 128 + 512]


class Plan:

    ENGS = ("sync", "gpsimd", "vector", "scalar", "tensor")

    NLANE = 4

    def __init__(self, nc):
        self.nc = nc
        self.q = {e: [] for e in self.ENGS}
        self.h = {}
        self.cnt = {}
        self.last_wait = {}
        names = ["t", "v", "s"]
        for qn in ("y", "g"):
            for i in range(self.NLANE):
                names.append(qn + str(i))
        for n in names:
            self.h[n] = nc.alloc_semaphore("sem_" + n)
            self.cnt[n] = 0
        self.rr = {"y": 0, "g": 0}

    def dma(self, queue, fn):
        eng = "sync" if queue == "y" else "gpsimd"
        i = self.rr[queue]
        self.rr[queue] = (i + 1) % self.NLANE
        lane = queue + str(i)
        if self.cnt[lane] > 0:
            self.wait(eng, lane, self.cnt[lane])
        tok = (lane, self.inc(eng, fn, lane, 16))
        return tok

    def wait_tok(self, eng, tok):
        self.wait(eng, tok[0], tok[1])

    def cop(self, eng, fn):
        sem = {"vector": "v", "scalar": "s", "tensor": "t"}[eng]
        self.wait(eng, sem, self.cnt[sem])
        return self.inc(eng, fn, sem)

    def fence(self):
        sems = [n for n in self.cnt if self.cnt[n] > 0]
        for eng in self.ENGS:
            for n in sems:
                self.wait(eng, n, self.cnt[n])

    def dma_barrier(self, eng):
        for lane in list(self.cnt):
            if lane[0] in "yg" and len(lane) == 2 and self.cnt[lane] > 0:
                self.wait(eng, lane, self.cnt[lane])

    def emit(self, eng, fn):
        self.q[eng].append(fn)

    def inc(self, eng, fn, sem, k=1):
        h = self.h[sem]
        self.q[eng].append(lambda e, fn=fn, h=h, k=k: fn(e).then_inc(h, k))
        self.cnt[sem] += k
        return self.cnt[sem]

    def wait(self, eng, sem, val=None):
        if val is None:
            val = self.cnt[sem]
        assert val <= self.cnt[sem], f"wait on future value of {sem}"
        key = (eng, sem)
        if self.last_wait.get(key, -1) >= val:
            return
        self.last_wait[key] = val
        h = self.h[sem]
        self.q[eng].append(lambda e, h=h, v=val: e.wait_ge(h, v))

    def run(self, block):
        for eng in self.ENGS:
            block.__getattribute__(eng)(lambda e, eng=eng: [f(e) for f in self.q[eng]])


def build(nc, stop_after="full"):
    dt = nc.dram_tensor
    x_in = dt("x", [T, D], F32, kind="ExternalInput").ap()
    y_out = dt("y", [T, D], F32, kind="ExternalOutput").ap()
    mq = dt("mq", [D, 512], F32, kind="ExternalInput").ap()
    mk = dt("mk", [D, 512], F32, kind="ExternalInput").ap()
    mv = dt("mv", [D, 1024], F32, kind="ExternalInput").ap()
    w1t0 = dt("w1t0", [D, 4096], F32, kind="ExternalInput").ap()
    w2t0 = dt("w2t0", [4096, D], F32, kind="ExternalInput").ap()
    n1w_bc = dt("n1w_bc", [128, D], F32, kind="ExternalInput").ap()
    n2w_bc = dt("n2w_bc", [128, 4096], F32, kind="ExternalInput").ap()
    actsb = dt("actsb", [128, 2], F32, kind="ExternalInput").ap()
    lnw_bc = dt("lnw_bc", [128, D], F32, kind="ExternalInput").ap()
    lnb_bc = dt("lnb_bc", [128, D], F32, kind="ExternalInput").ap()
    wq1 = dt("wq1", [D, 512], F32, kind="ExternalInput").ap()
    wk1 = dt("wk1", [D, 512], F32, kind="ExternalInput").ap()
    wv1 = dt("wv1", [D, 1024], F32, kind="ExternalInput").ap()
    wg1 = dt("wg1", [D, 1024], F32, kind="ExternalInput").ap()
    gkw1 = dt("gkw1", [D, 16], F32, kind="ExternalInput").ap()
    gkw2 = dt("gkw2", [16, 512], F32, kind="ExternalInput").ap()
    gkb2_bc = dt("gkb2_bc", [128, 512], F32, kind="ExternalInput").ap()
    rmsw_bc = dt("rmsw_bc", [128, D], F32, kind="ExternalInput").ap()
    wo1 = dt("wo1", [D, D], F32, kind="ExternalInput").ap()
    wmlp1 = dt("wmlp1", [D, 2048], F32, kind="ExternalInput").ap()
    bmlp1 = dt("bmlp1", [128, 16], F32, kind="ExternalInput").ap()
    wmlp2 = dt("wmlp2", [2048, D], F32, kind="ExternalInput").ap()
    bmlp2_bc = dt("bmlp2_bc", [128, D], F32, kind="ExternalInput").ap()
    w1t1 = dt("w1t1", [D, 4096], F32, kind="ExternalInput").ap()
    b1f = dt("b1f", [128, 32], F32, kind="ExternalInput").ap()
    w2t1 = dt("w2t1", [4096, D], F32, kind="ExternalInput").ap()
    b2f_bc = dt("b2f_bc", [128, D], F32, kind="ExternalInput").ap()

    c_ident = nc.inline_tensor(np.eye(128, dtype=np.float32), "c_ident")
    c_triu = nc.inline_tensor(np.triu(np.ones((128, 128), np.float32)), "c_triu")
    x2d = dt("x2d", [T, D], BF16, kind="Internal").ap()
    xn1td = dt("xn1td", [128, ND * T], BF16, kind="Internal").ap()

    P = Plan(nc)

    with (
        nc.sbuf_tensor("XRES", [128, NT * 1024], F32) as XRES,
        nc.sbuf_tensor("IDBF", [128, 128], BF16) as IDBF,
        nc.sbuf_tensor("ST", [128, 10 * NT], F32) as ST,
        nc.sbuf_tensor("COLS", [128, 64], F32) as COLS,
        nc.sbuf_tensor("ONESR", [1, 128], F32) as ONESR,
        nc.sbuf_tensor("CONSTC", [128, 16], F32) as CONSTC,
        nc.sbuf_tensor("SCRA", [128, 1024], BF16) as SCRA,
        nc.Block() as block,
    ):
        def st(i):
            return ST[:, i * NT:(i + 1) * NT]
        R0, NMR0, R2, NMR2, MX0, MX2 = (st(i) for i in range(6))
        R1, NMR1 = st(6), st(7)

        CVALS = [1e-5, 1e-8, 4096.0 * 1e-8, float(np.log(127.0 * 1e5)),
                 float(0.5 * np.log(4096.0)), 1.0,
                 float(-0.5 * np.log(32.0)), float(np.log(2.0)), 1e-35,
                 float(-3.0 * np.log(2.0)), float(3.0 * np.log(2.0)),
                 float(-10.0 * np.log(2.0)), float(10.0 * np.log(2.0))]
        CIDX = {v: i for i, v in enumerate(CVALS)}

        def getc(v):
            return CONSTC[:, CIDX[v]:CIDX[v] + 1]

        x_toks = [P.dma("y", lambda e, ti=ti: e.dma_start(out=XRES[:, ts(ti, 1024)],
                    in_=x_in[ts(ti), :])) for ti in range(NT)]
        id_tok = P.dma("g", lambda e: e.dma_start(out=IDBF[:], in_=c_ident.ap()[:, :]))
        P.cop("vector", lambda e: e.memset(ONESR[:], 1.0))
        for i, v in enumerate(CVALS):
            P.cop("vector", lambda e, i=i, v=v: e.memset(CONSTC[:, i:i + 1], v))
        ones_done = P.cop("vector", lambda e: e.memset(COLS[:, 63:64], 1.0))
        P.wait("scalar", "v", ones_done)

        def ln_stats(r_col, nmr_col, dep_v):
            SUM, SUM2 = COLS[:, 0:NT], COLS[:, 16:16 + NT]
            for tok in x_toks:
                P.wait_tok("vector", tok)
                P.wait_tok("scalar", tok)
            P.wait("vector", "v", dep_v)
            if dep_v:
                P.wait("scalar", "v", dep_v)
            for ti in range(NT):
                P.cop("vector", lambda e, ti=ti: e.tensor_reduce(
                    SUM[:, ti:ti + 1], XRES[:, ts(ti, 1024)], AX.X, ALU.add))
                P.wait("scalar", "s", P.cnt["s"])
                P.cop("scalar", lambda e, ti=ti: e.activation(
                    SCRA[:], XRES[:, ts(ti, 1024)], AF.Square, bias=0.0, scale=1.0,
                    accum_out=SUM2[:, ti:ti + 1]))
            P.wait("vector", "s", P.cnt["s"])
            P.wait("vector", "v", None)
            M = COLS[:, 32:32 + NT]
            VAR = COLS[:, 0:NT]
            P.cop("vector", lambda e: e.tensor_scalar(M, SUM, 1.0 / D, None, ALU.mult))
            P.cop("vector", lambda e: e.tensor_scalar(SUM2, SUM2, 1.0 / D, None, ALU.mult))
            P.cop("vector", lambda e: e.tensor_tensor(VAR, M, M, ALU.mult))
            vd = P.cop("vector", lambda e: e.tensor_tensor(VAR, SUM2, VAR, ALU.subtract))
            P.wait("scalar", "v", vd)
            P.cop("scalar", lambda e: e.activation(SUM2, VAR, AF.Ln, bias=getc(1e-5), scale=1.0))
            rd = P.cop("scalar", lambda e: e.activation(r_col, SUM2, AF.Exp, bias=0.0, scale=-0.5))
            P.wait("vector", "s", rd)
            P.cop("vector", lambda e: e.tensor_tensor(nmr_col, M, r_col, ALU.mult))
            nm = P.cop("vector", lambda e: e.tensor_scalar(nmr_col, nmr_col, -1.0, None, ALU.mult))
            return VAR, nm

        VAR0, nm0 = ln_stats(R0, NMR0, dep_v=0)
        TMP = COLS[:, 16:16 + NT]
        P.cop("vector", lambda e: e.tensor_tensor(TMP, VAR0, R0, ALU.mult))
        t2 = P.cop("vector", lambda e: e.tensor_tensor(TMP, TMP, R0, ALU.mult))
        P.wait("scalar", "v", t2)
        P.cop("scalar", lambda e: e.activation(TMP, TMP, AF.Ln, bias=getc(1e-8), scale=1.0))
        iv = P.cop("scalar", lambda e: e.activation(TMP, TMP, AF.Exp, bias=0.0, scale=-0.5))
        P.wait("vector", "s", iv)
        P.cop("vector", lambda e: e.tensor_tensor(R2, R0, TMP, ALU.mult))
        l0_cols = P.cop("vector", lambda e: e.tensor_tensor(NMR2, NMR0, TMP, ALU.mult))

        with nc.sbuf_tensor("XNT", [128, ND * T], BF16) as XNT:
            P.wait("vector", "v", l0_cols)
            P.wait_tok("tensor", id_tok)
            l0_ffn(nc, P, XRES, w1t0, w2t0, n1w_bc, n2w_bc, actsb, R0, NMR0,
                   R2, NMR2, MX0, COLS, ONESR, IDBF, SCRA, getc, XNT)
            if stop_after == "l0ffn":
                finish(nc, P, XRES, y_out)
                return P

            P.fence()
            l0_attn(nc, P, XRES, XNT, mq, mk, mv, COLS)
        if stop_after == "l0":
            finish(nc, P, XRES, y_out)
            return P

        P.fence()
        _, nm1 = ln_stats(R1, NMR1, dep_v=P.cnt["v"])
        l1_attn(nc, P, XRES, R1, NMR1, COLS, ST, IDBF, ONESR, c_ident, c_triu,
                lnw_bc, lnb_bc, wq1, wk1, wv1, wg1, gkw1, gkw2, gkb2_bc, rmsw_bc,
                wo1, wmlp1, bmlp1, wmlp2, bmlp2_bc, SCRA, getc, x2d, xn1td)
        if stop_after == "l1attn":
            finish(nc, P, XRES, y_out)
            return P
        P.fence()
        l1_ffn(nc, P, XRES, COLS, IDBF, w1t1, b1f, w2t1, b2f_bc, getc, xn1td)

        finish(nc, P, XRES, y_out)
    return P


def finish(nc, P, XRES, y_out):
    P.wait("gpsimd", "v", P.cnt["v"])
    P.wait("gpsimd", "s", P.cnt["s"])
    P.wait("gpsimd", "t", P.cnt["t"])
    for ti in range(NT):
        P.dma("g", lambda e, ti=ti: e.dma_start(
            out=y_out[ts(ti), :], in_=XRES[:, ts(ti, 1024)]))
    P.dma_barrier("gpsimd")
    P.dma_barrier("sync")



def l0_attn(nc, P, XRES, XNT, mq, mk, mv, COLS):
    with (
        nc.sbuf_tensor("QT2", [128, 4 * T], BF16) as QT2,
        nc.sbuf_tensor("KT2", [128, 4 * T], BF16) as KT2,
        nc.sbuf_tensor("VB", [128, 16384], BF16) as VB,
        nc.sbuf_tensor("EB", [128, 2 * T], BF16) as EB,
        nc.sbuf_tensor("VSC", [128, 256], BF16) as VSC,
        nc.sbuf_tensor("MW", [128, 8192], BF16) as MW,
        nc.psum_tensor("SC", [128, 2048], F32) as SC,
        nc.psum_tensor("OPS", [128, 2048], F32) as OPS,
    ):
        for wsrc, dstT in ((mq, QT2), (mk, KT2)):
            P.wait("gpsimd", "t", P.cnt["t"])
            wl = P.dma("g", lambda e, wsrc=wsrc: e.dma_start(
                out=MW[:, 0:4096].rearrange("p (k o) -> p k o", k=ND),
                in_=wsrc.rearrange("(k p) o -> p k o", p=128)))
            P.wait_tok("tensor", wl)
            P.wait("tensor", "v", P.cnt["v"])
            for m in range(4):
                h0 = 2 * m
                r0 = 64 * (h0 // 4)
                for sub in range(8):
                    g = (m * 8 + sub) % 4
                    reg = SC[r0:r0 + 64, 512 * g:512 * g + 512]
                    for k in range(ND):
                        mm = P.inc("tensor", lambda e, m=m, sub=sub, k=k, reg=reg: e.matmul(
                            reg,
                            MW[:, k * 512 + 64 * sub: k * 512 + 64 * (sub + 1)],
                            XNT[:, k * T + 512 * m: k * T + 512 * (m + 1)],
                            start=(k == 0), stop=(k == ND - 1)), "t")
                    P.wait("vector", "t", mm)
                    blk0 = h0 % 4
                    ev = P.cop("vector", lambda e, dstT=dstT, r0=r0, blk0=blk0, sub=sub,
                               reg=reg: e.tensor_copy(
                        dstT[r0:r0 + 64, blk0 * T:(blk0 + 2) * T].rearrange(
                            "p (hh tl rest) -> p hh tl rest", hh=2, tl=2)
                            [:, :, :, sub * 128:(sub + 1) * 128],
                        reg.rearrange("p (hh tl t) -> p hh tl t", hh=2, tl=2)))
                    if g == 3:
                        P.wait("tensor", "v", ev)
        P.wait("gpsimd", "t", P.cnt["t"])
        wl = P.dma("g", lambda e: e.dma_start(
            out=MW[:].rearrange("p (k o) -> p k o", k=ND),
            in_=mv.rearrange("(k p) o -> p k o", p=128)))
        P.wait_tok("tensor", wl)
        P.wait("tensor", "v", P.cnt["v"])
        for h in range(8):
            for tlc in range(2):
                for sg in range(2):
                    g = (h * 4 + tlc * 2 + sg) % 4
                    reg = OPS[:, 512 * g:512 * g + 512]
                    for k in range(ND):
                        mm = P.inc("tensor", lambda e, h=h, tlc=tlc, sg=sg, k=k, reg=reg:
                            e.matmul(
                                reg,
                                XNT[:, k * T + 256 * h + 128 * tlc:
                                    k * T + 256 * h + 128 * (tlc + 1)],
                                MW[:, k * 1024 + 512 * sg: k * 1024 + 512 * (sg + 1)],
                                start=(k == 0), stop=(k == ND - 1)), "t")
                    P.wait("vector", "t", mm)
                    t0_ = h * 16 + 8 * sg + tlc
                    ev = P.cop("vector", lambda e, t0_=t0_, reg=reg: e.tensor_copy(
                        VB[:].rearrange("p (n t) -> p n t", t=128)[:, t0_:t0_ + 7:2, :],
                        reg.rearrange("p (s t) -> p s t", s=4)))
                    if g == 3:
                        P.wait("tensor", "v", ev)
        exp_s = {}
        outmm_t = {}
        rv_v = {}
        vs_v = {}
        evac_v = None

        def emit_out(h, jt):
            par = jt % 2
            P.wait("tensor", "v", vs_v[jt])
            if jt == 0 and evac_v is not None:
                P.wait("tensor", "v", evac_v)
            for tau in range(2):
                for s_ in range(8):
                    outmm_t[par] = P.inc("tensor", lambda e, tau=tau, s_=s_, jt=jt,
                                         par=par: e.matmul(
                        OPS[:, ts(8 * tau + s_)],
                        EB[:, 2048 * par + 1024 * tau + 128 * s_:
                            2048 * par + 1024 * tau + 128 * (s_ + 1)],
                        VSC[:, 128 * par:128 * (par + 1)],
                        start=(jt == 0 and (8 * tau + s_) % 4 == 0),
                        stop=(jt == NT - 1 and (8 * tau + s_) % 4 == 3)), "t")

        for h in range(8):
            r0 = 64 * (h // 4)
            blk = h % 4
            for jt in range(NT):
                sub, tlc = jt // 2, jt % 2
                par = jt % 2
                jb = blk * T + 1024 * tlc + 128 * sub
                for half in range(2):
                    if half in exp_s:
                        P.wait("tensor", "s", exp_s[half])
                    for nn in range(2):
                        scd = P.inc("tensor", lambda e, r0=r0, blk=blk, jb=jb,
                                    half=half, nn=nn: e.matmul(
                            SC[:, 1024 * half + 512 * nn:1024 * half + 512 * (nn + 1)],
                            KT2[r0:r0 + 64, jb:jb + 128],
                            QT2[r0:r0 + 64, blk * T + 1024 * half + 512 * nn:
                                blk * T + 1024 * half + 512 * (nn + 1)],
                            start=True, stop=True), "t")
                    P.wait("scalar", "t", scd)
                    if half == 0:
                        if par in rv_v:
                            P.wait("scalar", "v", rv_v[par])
                        if par in outmm_t:
                            P.wait("scalar", "t", outmm_t[par])
                    exp_s[half] = P.inc("scalar", lambda e, par=par, half=half: e.activation(
                        EB[:, 2048 * par + 1024 * half:2048 * par + 1024 * (half + 1)],
                        SC[:, 1024 * half:1024 * (half + 1)],
                        AF.Exp, bias=0.0, scale=0.125,
                        accum_out=COLS[:, 44 + 2 * par + half:45 + 2 * par + half]), "s")
                P.wait("vector", "s", exp_s[1])
                P.cop("vector", lambda e, par=par: e.tensor_tensor(
                    COLS[:, 51 + par:52 + par], COLS[:, 44 + 2 * par:45 + 2 * par],
                    COLS[:, 45 + 2 * par:46 + 2 * par], ALU.add))
                rv_v[par] = P.cop("vector", lambda e, par=par: e.reciprocal(
                    COLS[:, 51 + par:52 + par], COLS[:, 51 + par:52 + par]))
                if par in outmm_t:
                    P.wait("vector", "t", outmm_t[par])
                vs_v[jt] = P.cop("vector", lambda e, h=h, jt=jt, par=par: e.tensor_scalar(
                    VSC[:, 128 * par:128 * (par + 1)],
                    VB[:, (h * 16 + jt) * 128:(h * 16 + jt) * 128 + 128],
                    COLS[:, 51 + par:52 + par], None, ALU.mult))
                if jt >= 1:
                    emit_out(h, jt - 1)
            emit_out(h, NT - 1)
            P.wait("vector", "t", outmm_t[(NT - 1) % 2])
            evac_v = P.cop("vector", lambda e, h=h: e.tensor_tensor(
                XRES[:, 2048 * h:2048 * (h + 1)], XRES[:, 2048 * h:2048 * (h + 1)],
                OPS[:, 0:2048], ALU.add))



def l0_ffn(nc, P, XRES, w1t0, w2t0, n1w_bc, n2w_bc, actsb, R0, NMR0, R2, NMR2,
           MX0, COLS, ONESR, IDBF, SCRA, getc, XNT):
    SY, MXC = 0, 8
    CS, CLN, CRSQ, CMZ, CMXP, CF, CD2 = 16, 17, 18, 19, 20, 21, 22
    CDSC = 24
    CPM, CPR = 26, 27
    CW1A, CW1T, CW1NT = 53, 54, 55
    CW2A, CW2T, CW2NT = 56, 49, 50
    T2A, R2A, T1A, R1A = 57, 58, 59, 60
    CCW1, CCW2 = 46, 48
    CA, CB = 61, 62
    CONES = 63
    with (
        nc.sbuf_tensor("IW1", [128, ND * 4096], FP8) as IW1,
        nc.sbuf_tensor("IW2", [128, 32 * 1024], FP8) as IW2,
        nc.sbuf_tensor("N1W", [128, D], F32) as N1W,
        nc.sbuf_tensor("N2W", [128, 4096], F32) as N2W,
    ):
        n1ld = P.dma("g", lambda e: e.dma_start(out=N1W[:], in_=n1w_bc[:, :]))
        n2ld = P.dma("g", lambda e: e.dma_start(out=N2W[:], in_=n2w_bc[:, :]))
        asld = P.dma("y", lambda e: e.dma_start(out=COLS[:, CA:CB + 1], in_=actsb[:, :]))
        iw2_v = {}
        with (
            nc.sbuf_tensor("WR0", [128, 1024], F32) as WR0,
            nc.sbuf_tensor("WR1", [128, 1024], F32) as WR1,
            nc.sbuf_tensor("WR2", [128, 1024], F32) as WR2,
            nc.sbuf_tensor("WR3", [128, 1024], F32) as WR3,
            nc.sbuf_tensor("PRA", [128, 2048], BF16) as PRA,
            nc.sbuf_tensor("TILE_A", [128, 1024], BF16) as TILE_A,
            nc.psum_tensor("PT", [128, 1024], BF16) as PT,
            nc.psum_tensor("RPS", [128, 512], F32) as RPS,
        ):
            WY, WG_ = (WR0, WR1), (WR2, WR3)

            def w1piece(i):
                k, c = i // 4, i % 4
                return w1t0[ts(k), c * 1024:(c + 1) * 1024]

            def w2piece(i):
                return w2t0[ts(i), :]

            streams = (("y", WY, w1piece, R1A, T1A, 0),
                       ("g", WG_, w2piece, R2A, T2A, 1024))
            P.cop("vector", lambda e: e.memset(COLS[:, R1A:R1A + 1], 0.0))
            P.cop("vector", lambda e: e.memset(COLS[:, R2A:R2A + 1], 0.0))
            abs_s = {}
            acc_v = {}
            for i in range(32):
                for (qn, Wp, piece, runc, tmpc, po) in streams:
                    eng = "sync" if qn == "y" else "gpsimd"
                    buf = Wp[i % 2]
                    if i >= 2:
                        P.wait(eng, "s", abs_s[(qn, i - 2)])
                    tok = P.dma(qn, lambda e, buf=buf, src=piece(i): e.dma_start(
                        out=buf[:], in_=src))
                    P.wait_tok("scalar", tok)
                    if i >= 1:
                        P.wait("scalar", "v", acc_v[(qn, i - 1)])
                    abs_s[(qn, i)] = P.cop("scalar", lambda e, buf=buf, po=po, tmpc=tmpc:
                        e.activation(PRA[:, po:po + 1024], buf[:], AF.Abs, bias=0.0,
                                     scale=1.0, accum_out=COLS[:, tmpc:tmpc + 1]))
                    P.wait("vector", "s", abs_s[(qn, i)])
                    acc_v[(qn, i)] = P.cop("vector", lambda e, runc=runc, tmpc=tmpc:
                        e.tensor_tensor(
                            COLS[:, runc:runc + 1], COLS[:, runc:runc + 1],
                            COLS[:, tmpc:tmpc + 1], ALU.add))
            acc = P.cnt["v"]
            P.wait("tensor", "v", acc)
            P.inc("tensor", lambda e: e.matmul(
                RPS[0:1, 0:1], COLS[:, R1A:R1A + 1], COLS[:, CONES:CONES + 1],
                start=True, stop=True), "t")
            mmc = P.inc("tensor", lambda e: e.matmul(
                RPS[0:1, 1:2], COLS[:, R2A:R2A + 1], COLS[:, CONES:CONES + 1],
                start=True, stop=True), "t")
            P.wait("vector", "t", mmc)
            P.cop("vector", lambda e: e.tensor_copy(COLS[0:1, T1A:T1A + 1], RPS[0:1, 0:1]))
            ev = P.cop("vector", lambda e: e.tensor_copy(COLS[0:1, T2A:T2A + 1], RPS[0:1, 1:2]))
            P.wait("tensor", "v", ev)
            P.inc("tensor", lambda e: e.matmul(
                RPS[:, 0:1], ONESR[:], COLS[0:1, T1A:T1A + 1], start=True, stop=True), "t")
            bc = P.inc("tensor", lambda e: e.matmul(
                RPS[:, 1:2], ONESR[:], COLS[0:1, T2A:T2A + 1], start=True, stop=True), "t")
            P.wait("vector", "t", bc)
            numel = float(D * 4096)
            P.cop("vector", lambda e: e.tensor_scalar(
                COLS[:, CW1A:CW1A + 1], RPS[:, 0:1], 1.0 / numel, 1e-5, ALU.mult, ALU.max))
            P.cop("vector", lambda e: e.tensor_scalar(
                COLS[:, CW2A:CW2A + 1], RPS[:, 1:2], 1.0 / numel, 1e-5, ALU.mult, ALU.max))
            for (ca, ct, cnt_) in ((CW1A, CW1T, CW1NT), (CW2A, CW2T, CW2NT)):
                P.cop("vector", lambda e, ca=ca, ct=ct: e.tensor_scalar(
                    COLS[:, ct:ct + 1], COLS[:, ca:ca + 1], 0.5, None, ALU.mult))
                P.cop("vector", lambda e, ct=ct, cnt_=cnt_: e.tensor_scalar(
                    COLS[:, cnt_:cnt_ + 1], COLS[:, ct:ct + 1], -1.0, None, ALU.mult))
            P.cop("vector", lambda e: e.tensor_scalar(
                COLS[:, CCW1:CCW1 + 1], COLS[:, CW1A:CW1A + 1], -1.0 / 127.0, None, ALU.mult))
            thd = P.cop("vector", lambda e: e.tensor_scalar(
                COLS[:, CCW2:CCW2 + 1], COLS[:, CW2A:CW2A + 1], -1.0 / 127.0, None, ALU.mult))
            P.wait("vector", "v", thd)
            stt_v = {}
            for i in range(32):
                for (qn, Wp, piece, runc, tmpc, po) in streams:
                    eng = "sync" if qn == "y" else "gpsimd"
                    tcol = CW1T if qn == "y" else CW2T
                    ncol = CW1NT if qn == "y" else CW2NT
                    buf = Wp[i % 2]
                    if i >= 2:
                        P.wait(eng, "v", stt_v[(qn, i - 2)])
                    else:
                        P.wait(eng, "s", abs_s[(qn, 30 + i)])
                    tok = P.dma(qn, lambda e, buf=buf, src=piece(i): e.dma_start(
                        out=buf[:], in_=src))
                    P.wait_tok("vector", tok)
                    if qn == "y":
                        k, c = i // 4, i % 4
                        dst = IW1[:, k * 4096 + c * 1024: k * 4096 + (c + 1) * 1024]
                    else:
                        dst = IW2[:, i * 1024:(i + 1) * 1024]
                    P.cop("vector", lambda e, buf=buf, po=po, tcol=tcol: e.tensor_scalar(
                        PRA[:, po:po + 1024], buf[:], COLS[:, tcol:tcol + 1], None, ALU.is_ge))
                    stt_v[(qn, i)] = P.cop("vector", lambda e, buf=buf, po=po, dst=dst, ncol=ncol:
                        e.scalar_tensor_tensor(
                            dst, buf[:], COLS[:, ncol:ncol + 1], PRA[:, po:po + 1024],
                            ALU.is_le, ALU.subtract))
                    if qn == "g":
                        iw2_v[i] = stt_v[(qn, i)]
            iw1_v = stt_v[("y", 31)]
            def pt(k):
                return PT[:, ts(k)]
            for ti in range(NT):
                xn = P.cop("vector", lambda e, ti=ti: e.tensor_scalar(
                    TILE_A[:], XRES[:, ts(ti, 1024)], R0[:, ti:ti + 1],
                    NMR0[:, ti:ti + 1], ALU.mult, ALU.add))
                P.wait("tensor", "v", xn)
                for k in range(ND):
                    trd = P.inc("tensor", lambda e, k=k: e.transpose(
                        pt(k), TILE_A[:, ts(k)], IDBF[:]), "t")
                P.wait("vector", "t", trd)
                for k in range(ND):
                    P.cop("vector", lambda e, ti=ti, k=k: e.tensor_copy(
                        XNT[:, k * T + 128 * ti: k * T + 128 * (ti + 1)], pt(k)))
                P.wait("tensor", "v", P.cnt["v"])
        with (
            nc.sbuf_tensor("YB", [128, 4096], F32) as YB,
            nc.sbuf_tensor("IXT2", [128, 2048], BF16) as IXT2,
            nc.sbuf_tensor("TILE_I", [128, 256], I32) as TILE_I,
            nc.sbuf_tensor("IHT", [128, 512], BF16) as IHT,
            nc.sbuf_tensor("QCB", [128, 512], BF16) as QCB,
            nc.psum_tensor("HPS", [128, 1024], F32) as HPS,
            nc.psum_tensor("TPS", [128, 1024], BF16) as TPS,
            nc.psum_tensor("OPS2", [128, 2048], F32) as OPS2,
        ):
            trc = [0]
            tps_cons = {}

            def tslot():
                n = trc[0]
                trc[0] += 1
                return n

            def pslot(n):
                return TPS[:, ts(n % 8)]

            cast_v = {}
            s1_s = {}
            s3_s = {}
            prep_pb_v = {}
            prep_t = {}
            prepvb_v = {}
            mm1_last = {}
            mm2_t = {}
            tr_t = {}
            add_v = {}
            dsc_v = {}

            def emit_prep(tj):
                PB = YB[:, 0:1024]
                P.wait_tok("vector", n1ld)
                P.cop("vector", lambda e, tj=tj: e.tensor_scalar(
                    PB, XRES[:, ts(tj, 1024)], R2[:, tj:tj + 1],
                    NMR2[:, tj:tj + 1], ALU.mult, ALU.add))
                P.cop("vector", lambda e: e.tensor_tensor(PB, PB, N1W[:], ALU.mult))
                P.cop("vector", lambda e: e.tensor_reduce(
                    COLS[:, CPM:CPM + 1], PB, AX.X, ALU.max, apply_absolute_value=True))
                P.cop("vector", lambda e: e.tensor_scalar(
                    COLS[:, CPM:CPM + 1], COLS[:, CPM:CPM + 1], 1e-5, None, ALU.max))
                P.cop("vector", lambda e: e.reciprocal(COLS[:, CPR:CPR + 1], COLS[:, CPM:CPM + 1]))
                P.cop("vector", lambda e: e.tensor_scalar(
                    PB, PB, COLS[:, CPR:CPR + 1], 127.0, ALU.mult, ALU.mult))
                if tj >= 1:
                    P.wait("vector", "s", s3_s[(tj - 1, 7)])
                for q in range(4):
                    P.cop("vector", lambda e, q=q: e.tensor_copy(
                        TILE_I[:], PB[:, 256 * q:256 * (q + 1)]))
                    cp = P.cop("vector", lambda e, q=q: e.tensor_copy(
                        SCRA[:, 256 * q:256 * (q + 1)], TILE_I[:]))
                prep_pb_v[tj] = cp
                P.wait("tensor", "v", cp)
                if tj >= 2:
                    P.wait("tensor", "t", mm1_last[tj - 2])
                for k in range(ND):
                    n = tslot()
                    if n >= 8:
                        P.wait("tensor", "v", tps_cons[n - 8])
                    trd = P.inc("tensor", lambda e, n=n, k=k: e.transpose(
                        pslot(n), SCRA[:, ts(k)], IDBF[:]), "t")
                prep_t[tj] = trd
                base = trc[0] - 8
                P.wait("vector", "t", trd)
                for g2 in range(2):
                    cc = P.cop("vector", lambda e, tj=tj, g2=g2, base=base: e.tensor_copy(
                        IXT2[:, (tj % 2) * 1024 + 512 * g2:(tj % 2) * 1024 + 512 * (g2 + 1)],
                        TPS[:, ((base + 4 * g2) % 8) * 128:((base + 4 * g2) % 8) * 128 + 512]))
                    for n in range(base + 4 * g2, base + 4 * g2 + 4):
                        tps_cons[n] = cc
                prepvb_v[tj] = cc
                if tj >= 2:
                    P.wait("vector", "s", s1_s[(tj - 2, 7)])
                dsc_v[tj] = P.cop("vector", lambda e, tj=tj: e.tensor_tensor(
                    COLS[:, CDSC + tj % 2:CDSC + tj % 2 + 1], COLS[:, CCW1:CCW1 + 1],
                    COLS[:, CPM:CPM + 1], ALU.mult))

            emit_prep(0)
            for ti in range(NT):
                for hc in range(8):
                    P.wait("tensor", "v", prepvb_v[ti])
                    if ti == 0 and hc == 0:
                        P.wait("tensor", "v", iw1_v)
                    if hc >= 2:
                        P.wait("tensor", "s", s1_s[(ti, hc - 2)])
                    elif ti > 0:
                        P.wait("tensor", "s", s1_s[(ti - 1, 6 + hc)])
                    for k in range(ND):
                        mm = P.inc("tensor", lambda e, ti=ti, hc=hc, k=k: e.matmul(
                            HPS[:, ts(hc % 2, 512)],
                            IXT2[:, (ti % 2) * 1024 + 128 * k:(ti % 2) * 1024 + 128 * (k + 1)],
                            IW1[:, k * 4096 + 512 * hc:k * 4096 + 512 * (hc + 1)],
                            start=(k == 0), stop=(k == ND - 1)), "t")
                    mm1_last[ti] = mm
                    YBc = YB[:, ts(hc, 512)]
                    P.wait("scalar", "t", mm)
                    P.wait("scalar", "v", dsc_v[ti])
                    if ti > 0:
                        P.wait("scalar", "v", cast_v[(ti - 1, hc)])
                    if hc < 2:
                        P.wait("scalar", "v", prep_pb_v[ti])
                    if hc == 0:
                        P.wait("scalar", "t", prep_t[ti])
                    s1_s[(ti, hc)] = P.cop("scalar", lambda e, ti=ti, hc=hc, YBc=YBc:
                        e.activation(YBc, HPS[:, ts(hc % 2, 512)], AF.Relu, bias=0.0,
                                     scale=COLS[:, CDSC + ti % 2:CDSC + ti % 2 + 1]))
                    sq2 = P.cop("scalar", lambda e, YBc=YBc: e.activation(
                        YBc, YBc, AF.Square, bias=0.0, scale=1.0))
                    P.wait("vector", "s", sq2)
                    P.wait_tok("vector", asld)
                    v1 = P.cop("vector", lambda e, YBc=YBc: e.tensor_scalar(
                        YBc, YBc, COLS[:, CA:CA + 1], COLS[:, CB:CB + 1], ALU.mult, ALU.add))
                    P.wait("scalar", "v", v1)
                    s3_s[(ti, hc)] = P.cop("scalar", lambda e, hc=hc, YBc=YBc: e.activation(
                        SCRA[:, 0:512], YBc, AF.Square, bias=0.0, scale=1.0,
                        accum_out=COLS[:, SY + hc:SY + hc + 1]))
                    P.wait("vector", "s", s3_s[(ti, hc)])
                    P.wait_tok("vector", n2ld)
                    P.cop("vector", lambda e, hc=hc, YBc=YBc: e.tensor_tensor(
                        YBc, YBc, N2W[:, ts(hc, 512)], ALU.mult))
                    P.cop("vector", lambda e, hc=hc, YBc=YBc: e.tensor_reduce(
                        COLS[:, MXC + hc:MXC + hc + 1], YBc, AX.X, ALU.max,
                        apply_absolute_value=True))
                sv = P.cop("vector", lambda e: e.tensor_reduce(
                    COLS[:, CS:CS + 1], COLS[:, SY:SY + 8], AX.X, ALU.add))
                P.cop("vector", lambda e: e.tensor_reduce(
                    COLS[:, CMZ:CMZ + 1], COLS[:, MXC:MXC + 8], AX.X, ALU.max))
                P.wait("scalar", "v", sv)
                P.cop("scalar", lambda e: e.activation(
                    COLS[:, CLN:CLN + 1], COLS[:, CS:CS + 1], AF.Ln,
                    bias=getc(4096.0 * 1e-8), scale=1.0))
                rq = P.cop("scalar", lambda e: e.activation(
                    COLS[:, CRSQ:CRSQ + 1], COLS[:, CLN:CLN + 1], AF.Exp,
                    bias=getc(float(0.5 * np.log(4096.0))), scale=-0.5))
                P.wait("vector", "s", rq)
                P.cop("vector", lambda e: e.tensor_tensor(
                    COLS[:, CMXP:CMXP + 1], COLS[:, CRSQ:CRSQ + 1],
                    COLS[:, CMZ:CMZ + 1], ALU.mult))
                P.cop("vector", lambda e: e.tensor_scalar(
                    COLS[:, CMXP:CMXP + 1], COLS[:, CMXP:CMXP + 1], 1e-5, None, ALU.max))
                P.cop("vector", lambda e: e.reciprocal(COLS[:, CF:CF + 1], COLS[:, CMXP:CMXP + 1]))
                P.cop("vector", lambda e: e.tensor_scalar(
                    COLS[:, CF:CF + 1], COLS[:, CF:CF + 1],
                    COLS[:, CRSQ:CRSQ + 1], 127.0, ALU.mult, ALU.mult))
                P.cop("vector", lambda e: e.tensor_tensor(
                    COLS[:, CD2:CD2 + 1], COLS[:, CCW2:CCW2 + 1],
                    COLS[:, CMXP:CMXP + 1], ALU.mult))
                for c8 in range(8):
                    YBc = YB[:, ts(c8, 512)]
                    P.cop("vector", lambda e, YBc=YBc: e.tensor_scalar(
                        YBc, YBc, COLS[:, CF:CF + 1], None, ALU.mult))
                    if c8 >= 1:
                        P.wait("vector", "t", tr_t[(ti, c8 - 1)])
                    for q in range(2):
                        P.cop("vector", lambda e, YBc=YBc, q=q: e.tensor_copy(
                            TILE_I[:], YBc[:, 256 * q:256 * (q + 1)]))
                        ct = P.cop("vector", lambda e, q=q: e.tensor_copy(
                            QCB[:, 256 * q:256 * (q + 1)], TILE_I[:]))
                    cast_v[(ti, c8)] = ct
                    P.wait("tensor", "v", ct)
                    base = trc[0]
                    for k4 in range(4):
                        n = tslot()
                        if n >= 8:
                            P.wait("tensor", "v", tps_cons[n - 8])
                        trd = P.inc("tensor", lambda e, n=n, k4=k4: e.transpose(
                            pslot(n), QCB[:, ts(k4)], IDBF[:]), "t")
                    tr_t[(ti, c8)] = trd
                    P.wait("vector", "t", trd)
                    if c8 >= 1:
                        P.wait("vector", "t", mm2_t[(ti, c8 - 1)])
                    cc = P.cop("vector", lambda e, base=base: e.tensor_copy(
                        IHT[:], TPS[:, (base % 8) * 128:(base % 8) * 128 + 512]))
                    for n in range(base, base + 4):
                        tps_cons[n] = cc
                    P.wait("tensor", "v", cc)
                    if ti == 0:
                        P.wait("tensor", "v", iw2_v[4 * c8 + 3])
                    if ti >= 2 and c8 == 0:
                        P.wait("tensor", "v", add_v[ti - 2])
                    for k4 in range(4):
                        k = 4 * c8 + k4
                        for nn2 in range(2):
                            mm2 = P.inc("tensor", lambda e, ti=ti, k=k, k4=k4, nn2=nn2: e.matmul(
                                OPS2[:, ts((ti % 2) * 2 + nn2, 512)],
                                IHT[:, ts(k4)],
                                IW2[:, k * 1024 + 512 * nn2:k * 1024 + 512 * (nn2 + 1)],
                                start=(k == 0), stop=(k == 31)), "t")
                    mm2_t[(ti, c8)] = mm2
                if ti + 1 < NT:
                    emit_prep(ti + 1)
                P.wait("vector", "t", mm2_t[(ti, 7)])
                add_v[ti] = P.cop("vector", lambda e, ti=ti: e.scalar_tensor_tensor(
                    XRES[:, ts(ti, 1024)],
                    OPS2[:, (ti % 2) * 1024:(ti % 2) * 1024 + 1024],
                    COLS[:, CD2:CD2 + 1], XRES[:, ts(ti, 1024)], ALU.mult, ALU.add))



def l1_attn(nc, P, XRES, R1, NMR1, COLS, ST, IDBF, ONESR, c_ident, c_triu,
            lnw_bc, lnb_bc, wq1, wk1, wv1, wg1, gkw1, gkw2, gkb2_bc, rmsw_bc,
            wo1, wmlp1, bmlp1, wmlp2, bmlp2_bc, SCRA, getc, x2d, xn1td):
    HEADS = 16
    with (
        nc.sbuf_tensor("HT", [128, ND * T], BF16) as HT,
        nc.sbuf_tensor("TRIUF", [128, 128], F32) as TRIUF,
        nc.sbuf_tensor("MSK4", [128, 512], BF16) as MSK4,
        nc.sbuf_tensor("LNW", [128, D], BF16) as LNW,
        nc.sbuf_tensor("LNB", [128, D], BF16) as LNB,
        nc.sbuf_tensor("LNWI", [128, D], BF16) as LNWI,
        nc.sbuf_tensor("GK2S", [16, 512], BF16) as GK2S,
        nc.sbuf_tensor("GK1S", [128, ND * 16], BF16) as GK1S,
    ):
        trl = P.dma("y", lambda e: e.dma_start(out=TRIUF[:], in_=c_triu.ap()[:, :]))
        for q4 in range(4):
            mkl = P.dma("g", lambda e, q4=q4: e.dma_start(
                out=MSK4[:, 128 * q4:128 * (q4 + 1)], in_=c_triu.ap()[:, :]))
        lwl = P.dma("g", lambda e: e.dma_start(out=LNW[:], in_=lnw_bc[:, :]))
        lbl = P.dma("g", lambda e: e.dma_start(out=LNB[:], in_=lnb_bc[:, :]))
        g2l = P.dma("g", lambda e: e.dma_start(out=GK2S[:], in_=gkw2[:, :]))
        g1l = P.dma("g", lambda e: e.dma_start(
            out=GK1S[:].rearrange("p (k o) -> p k o", k=ND),
            in_=gkw1.rearrange("(k p) o -> p k o", p=128)))
        for tok in (lwl, lbl, mkl, g2l):
            P.wait_tok("vector", tok)
        P.wait("vector", "v", None)

        def _recip_lnw(e, nc=nc):
            with nc.allow_low_precision("bf16 reciprocal of ln gamma (xn1 recovery)"):
                return e.reciprocal(LNWI[:], LNW[:])
        lwi = P.cop("vector", _recip_lnw)

        es = contextlib.ExitStack()
        WQ = es.enter_context(nc.sbuf_tensor("WQ", [128, ND * 512], BF16))
        WK = es.enter_context(nc.sbuf_tensor("WK", [128, ND * 512], BF16))
        WV = es.enter_context(nc.sbuf_tensor("WV", [128, ND * 1024], BF16))
        WG = es.enter_context(nc.sbuf_tensor("WG", [128, ND * 1024], BF16))
        WO = es.enter_context(nc.sbuf_tensor("WO", [128, ND * 1024], BF16))
        GKB = es.enter_context(nc.sbuf_tensor("GKB", [128, 512], BF16))
        RMSW = es.enter_context(nc.sbuf_tensor("RMSW", [128, D], BF16))
        QC = es.enter_context(nc.sbuf_tensor("QC", [128, 1024], BF16))
        KC = es.enter_context(nc.sbuf_tensor("KC", [128, 1024], BF16))
        VC = es.enter_context(nc.sbuf_tensor("VC", [128, 2048], BF16))
        GC = es.enter_context(nc.sbuf_tensor("GC", [128, 2048], BF16))
        SPF = es.enter_context(nc.sbuf_tensor("SPF", [128, 512], F32))
        ASB = es.enter_context(nc.sbuf_tensor("ASB1", [128, 512], F32))
        E1 = es.enter_context(nc.sbuf_tensor("E1", [128, 512], BF16))
        E2 = es.enter_context(nc.sbuf_tensor("E2", [128, 512], BF16))
        ALC = es.enter_context(nc.sbuf_tensor("ALC", [128, 4], F32))
        EALC = es.enter_context(nc.sbuf_tensor("EALC", [128, 8], F32))
        QPT = es.enter_context(nc.sbuf_tensor("QPT", [128, 1024], BF16))
        KPT = es.enter_context(nc.sbuf_tensor("KPT", [128, 1024], BF16))
        Z1C = es.enter_context(nc.sbuf_tensor("Z1C", [128, 16], BF16))
        Z1TC = es.enter_context(nc.sbuf_tensor("Z1TC", [16, 128], BF16))
        PB2 = es.enter_context(nc.sbuf_tensor("PB2", [128, 512], BF16))
        OG = es.enter_context(nc.sbuf_tensor("OG", [128, 1024], BF16))
        OGT = es.enter_context(nc.sbuf_tensor("OGT", [128, 1024], BF16))
        SSB = es.enter_context(nc.sbuf_tensor("SSB", [128, 256], F32))
        SBB = es.enter_context(nc.sbuf_tensor("SBB", [128, 256], BF16))
        OCOL = es.enter_context(nc.sbuf_tensor("OCOL", [128, 16], F32))
        INVC = es.enter_context(nc.sbuf_tensor("INVC", [128, 16], F32))
        SQB = es.enter_context(nc.sbuf_tensor("SQB", [128, 1024], BF16))
        PRJ = es.enter_context(nc.psum_tensor("PRJ", [128, 1024], F32))
        TPB = es.enter_context(nc.psum_tensor("TPB", [128, 1024], BF16))
        PPS = es.enter_context(nc.psum_tensor("PPS", [128, 512], F32))
        OP2 = es.enter_context(nc.psum_tensor("OP2", [128, 1024], F32))
        XP = es.enter_context(nc.psum_tensor("XP", [128, 1024], F32))
        for wsrc, wdst in ((wq1, WQ), (wk1, WK), (wv1, WV), (wg1, WG), (wo1, WO)):
            P.dma("g", lambda e, wsrc=wsrc, wdst=wdst: e.dma_start(
                out=wdst[:].rearrange("p (k o) -> p k o", k=ND),
                in_=wsrc.rearrange("(k p) o -> p k o", p=128)))
        P.dma("g", lambda e: e.dma_start(out=GKB[:], in_=gkb2_bc[:, :]))
        P.dma("g", lambda e: e.dma_start(out=RMSW[:], in_=rmsw_bc[:, :]))

        btr = [0]
        bcons = {}

        def bslot():
            n = btr[0]
            btr[0] += 1
            return n

        tb_t = {}
        tc_tok = {}
        for ti in range(NT):
            par = ti % 2
            TAb = OG if par == 0 else OGT
            TBb = QC if par == 0 else KC
            TCb = VC[:, 1024 * par:1024 * (par + 1)]
            if par in tb_t:
                P.wait("vector", "t", tb_t[par])
            P.cop("vector", lambda e, ti=ti, TAb=TAb: e.tensor_scalar(
                TAb[:], XRES[:, ts(ti, 1024)], R1[:, ti:ti + 1],
                NMR1[:, ti:ti + 1], ALU.mult, ALU.add))
            P.cop("vector", lambda e, TAb=TAb, TBb=TBb: e.tensor_tensor(
                TBb[:], TAb[:], LNW[:], ALU.mult))
            hb = P.cop("vector", lambda e, TBb=TBb: e.tensor_tensor(
                TBb[:], TBb[:], LNB[:], ALU.add))
            P.wait("tensor", "v", hb)
            if btr[0] > 0:
                P.wait("tensor", "v", bcons[btr[0] - 1])
            for k in range(ND):
                n = bslot()
                trd = P.inc("tensor", lambda e, n=n, k=k, TAb=TAb: e.transpose(
                    TPB[:, ts(n % 8)], TAb[:, ts(k)], IDBF[:]), "t")
            base = btr[0] - 8
            P.wait("vector", "t", trd)
            if par in tc_tok:
                P.wait_tok("vector", tc_tok[par])
            for g2 in range(2):
                cc = P.cop("vector", lambda e, TCb=TCb, g2=g2, base=base: e.tensor_copy(
                    TCb[:, 512 * g2:512 * (g2 + 1)], TPSLICE(TPB, base + 4 * g2)))
                for n in range(base + 4 * g2, base + 4 * g2 + 4):
                    bcons[n] = cc
            P.wait("sync", "v", cc)
            tc_tok[par] = P.dma("y", lambda e, ti=ti, TCb=TCb: e.dma_start(
                out=xn1td.rearrange("p (k t) -> p k t", k=ND)[:, :, 128 * ti:128 * (ti + 1)],
                in_=TCb.rearrange("p (k t) -> p k t", k=ND)))
            if btr[0] > 0:
                P.wait("tensor", "v", bcons[btr[0] - 1])
            for k in range(ND):
                n = bslot()
                trd = P.inc("tensor", lambda e, n=n, k=k, TBb=TBb: e.transpose(
                    TPB[:, ts(n % 8)], TBb[:, ts(k)], IDBF[:]), "t")
            tb_t[par] = trd
            base = btr[0] - 8
            P.wait("vector", "t", trd)
            for k in range(ND):
                cc = P.cop("vector", lambda e, ti=ti, k=k, base=base: e.tensor_copy(
                    HT[:, k * T + 128 * ti:k * T + 128 * (ti + 1)],
                    TPB[:, ((base + k) % 8) * 128:((base + k) % 8) * 128 + 128]))
                bcons[base + k] = cc

        qk_v = {}
        ealc_s = {}
        vc_v = {}
        gc_v = {}
        sbb_v = {}
        st_t = {}
        og_v = {}
        sq_s = {}
        oc_v = {}
        x2add_v = {}
        x2_tok = {}
        prj_free_v = [0]

        def emit_prep(c):
            par = c % 2
            qo, vo = 512 * par, 1024 * par

            def hsl(k, c=c):
                return HT[:, k * T + 128 * c:k * T + 128 * (c + 1)]
            P.wait("tensor", "v", prj_free_v[0])
            if c >= 2:
                P.wait("tensor", "t", st_t[c - 2])
            for p0, wbuf in ((0, WQ), (512, WK)):
                for k in range(ND):
                    mm = P.inc("tensor", lambda e, wbuf=wbuf, p0=p0, k=k, hk=hsl(k): e.matmul(
                        PRJ[:, p0:p0 + 512], hk,
                        wbuf[:, 512 * k:512 * (k + 1)],
                        start=(k == 0), stop=(k == ND - 1)), "t")
            P.wait("vector", "t", mm)
            P.cop("vector", lambda e, qo=qo: e.tensor_copy(QC[:, qo:qo + 512], PRJ[:, 0:512]))
            ev = P.cop("vector", lambda e, qo=qo: e.tensor_copy(KC[:, qo:qo + 512], PRJ[:, 512:1024]))
            P.wait("tensor", "v", ev)
            for k in range(ND):
                for nn in range(2):
                    mm = P.inc("tensor", lambda e, k=k, nn=nn, hk=hsl(k): e.matmul(
                        PRJ[:, ts(nn, 512)], hk,
                        WV[:, 1024 * k + 512 * nn:1024 * k + 512 * (nn + 1)],
                        start=(k == 0), stop=(k == ND - 1)), "t")
            P.wait("vector", "t", mm)
            if c < 2 and par in tc_tok:
                P.wait_tok("vector", tc_tok[par])
            ev = P.cop("vector", lambda e, vo=vo: e.tensor_copy(VC[:, vo:vo + 1024], PRJ[:, 0:1024]))
            vc_v[c] = ev
            P.wait("tensor", "v", ev)
            for k in range(ND):
                for nn in range(2):
                    mm = P.inc("tensor", lambda e, k=k, nn=nn, hk=hsl(k): e.matmul(
                        PRJ[:, ts(nn, 512)], hk,
                        WG[:, 1024 * k + 512 * nn:1024 * k + 512 * (nn + 1)],
                        start=(k == 0), stop=(k == ND - 1)), "t")
            P.wait("scalar", "t", mm)
            if c >= 2:
                P.wait("scalar", "v", oc_v[c - 2])
            P.wait("scalar", "s", P.cnt["s"])
            P.inc("scalar", lambda e: e.activation(
                SQB[:], PRJ[:, 0:1024], AF.Exp, bias=0.0, scale=-1.0), "s")
            P.wait("scalar", "s", P.cnt["s"])
            P.inc("scalar", lambda e: e.activation(
                SQB[:], SQB[:], AF.Ln, bias=getc(1.0), scale=1.0), "s")
            P.wait("scalar", "s", P.cnt["s"])
            sgd = P.inc("scalar", lambda e: e.activation(
                SQB[:], SQB[:], AF.Exp, bias=0.0, scale=-1.0), "s")
            P.wait("vector", "s", sgd)
            P.cop("vector", lambda e, vo=vo: e.tensor_tensor(
                GC[:, vo:vo + 1024], PRJ[:, 0:1024], SQB[:], ALU.mult))
            ev = P.cop("vector", lambda e, vo=vo: e.tensor_tensor(
                GC[:, vo:vo + 1024], GC[:, vo:vo + 1024], RMSW[:], ALU.mult))
            gc_v[c] = ev
            P.wait("tensor", "v", ev)
            for k in range(ND):
                mm = P.inc("tensor", lambda e, k=k, hk=hsl(k): e.matmul(
                    PRJ[:, 0:16], hk, GK1S[:, 16 * k:16 * (k + 1)],
                    start=(k == 0), stop=(k == ND - 1)), "t")
            P.wait("vector", "t", mm)
            zv = P.cop("vector", lambda e: e.tensor_copy(Z1C[:], PRJ[:, 0:16]))
            P.wait("tensor", "v", zv)
            zbase = btr[0]
            if btr[0] > 0:
                P.wait("tensor", "v", bcons[btr[0] - 1])
            for _ in range(4):
                n = bslot()
            trd = P.inc("tensor", lambda e, zbase=zbase: e.transpose(
                TPB[0:16, (zbase % 8) * 128:(zbase % 8) * 128 + 128], Z1C[:], IDBF[:]), "t")
            P.wait("vector", "t", trd)
            ztc = P.cop("vector", lambda e, zbase=zbase: e.tensor_copy(
                Z1TC[:], TPB[0:16, (zbase % 8) * 128:(zbase % 8) * 128 + 128]))
            for n in range(zbase, zbase + 4):
                bcons[n] = ztc
            P.wait("tensor", "v", ztc)
            mm = P.inc("tensor", lambda e: e.matmul(
                PRJ[:, 0:512], Z1TC[:], GK2S[:], start=True, stop=True), "t")
            P.wait("vector", "t", mm)
            zb = P.cop("vector", lambda e: e.tensor_tensor(
                SPF[:], PRJ[:, 0:512], GKB[:], ALU.add))
            P.wait("scalar", "v", zb)
            P.wait("scalar", "s", P.cnt["s"])
            P.inc("scalar", lambda e: e.activation(SPF[:], SPF[:], AF.Exp, bias=0.0, scale=-1.0), "s")
            P.wait("scalar", "s", P.cnt["s"])
            spd = P.inc("scalar", lambda e: e.activation(
                SPF[:], SPF[:], AF.Ln, bias=getc(1.0), scale=1.0), "s")
            P.wait("tensor", "s", spd)
            mm = P.inc("tensor", lambda e: e.matmul(
                PRJ[:, 0:512], TRIUF[:], SPF[:], start=True, stop=True), "t")
            P.wait("vector", "t", mm)
            asd = P.cop("vector", lambda e: e.tensor_copy(ASB[:], PRJ[:, 0:512]))
            P.wait("scalar", "v", asd)
            P.wait("scalar", "s", P.cnt["s"])
            P.inc("scalar", lambda e: e.activation(
                E1[:], ASB[:], AF.Exp, bias=getc(float(-0.5 * np.log(32.0))),
                scale=-1.0 / 16.0), "s")
            P.wait("scalar", "s", P.cnt["s"])
            e2d = P.inc("scalar", lambda e: e.activation(
                E2[:], ASB[:], AF.Exp, bias=0.0, scale=1.0 / 16.0), "s")
            P.wait("tensor", "v", asd)
            for b in range(4):
                alm = P.inc("tensor", lambda e, b=b: e.matmul(
                    PRJ[:, 512 + b:513 + b], SPF[:, ts(b)], COLS[:, 63:64],
                    start=True, stop=True), "t")
            P.wait("vector", "t", alm)
            P.cop("vector", lambda e: e.tensor_copy(ALC[:, 0:4], PRJ[:, 512:516]))
            alc = P.cnt["v"]
            prj_free_v[0] = alc
            P.wait("scalar", "v", alc)
            if c >= 2:
                P.wait("scalar", "v", sbb_v[(c - 2, 3)])
            P.wait("scalar", "s", P.cnt["s"])
            ealc_s[c] = P.inc("scalar", lambda e, par=par: e.activation(
                EALC[:, 4 * par:4 * (par + 1)], ALC[:], AF.Exp, bias=0.0,
                scale=-1.0 / 16.0), "s")
            P.wait("vector", "s", e2d)
            P.cop("vector", lambda e, qo=qo: e.tensor_tensor(
                QC[:, qo:qo + 512], QC[:, qo:qo + 512], E1[:], ALU.mult))
            qpd = P.cop("vector", lambda e, qo=qo: e.tensor_tensor(
                KC[:, qo:qo + 512], KC[:, qo:qo + 512], E2[:], ALU.mult))
            P.wait("tensor", "v", qpd)
            for src_, dst in ((QC, QPT), (KC, KPT)):
                if btr[0] > 0:
                    P.wait("tensor", "v", bcons[btr[0] - 1])
                base = btr[0]
                for b in range(4):
                    n = bslot()
                    trd = P.inc("tensor", lambda e, n=n, b=b, src_=src_, qo=qo: e.transpose(
                        TPB[:, ts(n % 8)], src_[:, qo + 128 * b:qo + 128 * (b + 1)],
                        IDBF[:]), "t")
                P.wait("vector", "t", trd)
                cc = P.cop("vector", lambda e, dst=dst, qo=qo, base=base: e.tensor_copy(
                    dst[:, qo:qo + 512], TPSLICE(TPB, base)))
                for n in range(base, base + 4):
                    bcons[n] = cc
            qk_v[c] = cc

        def emit_state(c):
            par = c % 2
            qo, vo = 512 * par, 1024 * par
            P.wait("tensor", "v", qk_v[c])
            P.wait("tensor", "v", vc_v[c])
            if c >= 1:
                P.wait("tensor", "s", sq_s[c - 1])
                P.wait("tensor", "v", og_v[c - 1])
                P.wait("tensor", "v", x2add_v[c - 1])
            def emit_pt(g, m):
                r0 = 32 * m
                cb = 128 * g
                return P.inc("tensor", lambda e, r0=r0, cb=cb, qo=qo, m=m: e.matmul(
                    PPS[:, 128 * m:128 * (m + 1)],
                    KPT[r0:r0 + 32, qo + cb:qo + cb + 128],
                    QPT[r0:r0 + 32, qo + cb:qo + cb + 128],
                    start=True, stop=True, tile_position=(r0, 0)), "t")

            for m in range(4):
                if m > 0:
                    P.wait("tensor", "t", P.cnt["t"])
                pt_mm = emit_pt(0, m)
            for g in range(4):
                P.wait("vector", "t", pt_mm)
                if c == 0 and g == 0:
                    P.wait("vector", "v", sbb_init)
                msk = P.cop("vector", lambda e: e.tensor_tensor(
                    PB2[:], PPS[:, 0:512], MSK4[:], ALU.mult))
                P.wait("tensor", "v", msk)
                if c >= 1:
                    P.wait("tensor", "v", sbb_v[(c - 1, g)])
                for m in range(4):
                    h = 4 * g + m
                    r0 = 32 * m
                    P.inc("tensor", lambda e, h=h, m=m, vo=vo: e.matmul(
                        OP2[:, 64 * h:64 * h + 64], PB2[:, 128 * m:128 * (m + 1)],
                        VC[:, vo + 64 * h:vo + 64 * h + 64],
                        start=(h % 8 == 0), stop=False), "t")
                    P.inc("tensor", lambda e, h=h, r0=r0, g=g, qo=qo: e.matmul(
                        OP2[:, 64 * h:64 * h + 64],
                        QPT[r0:r0 + 32, qo + 128 * g:qo + 128 * (g + 1)],
                        SBB[r0:r0 + 32, 64 * g:64 * g + 64],
                        start=False, stop=(h % 8 == 7), tile_position=(r0, 0)), "t")
                    if g + 1 < 4:
                        pt_mm = emit_pt(g + 1, m)
                for m in range(4):
                    h = 4 * g + m
                    r0 = 32 * m
                    ds = P.inc("tensor", lambda e, h=h, r0=r0, g=g, qo=qo, vo=vo: e.matmul(
                        XP[r0:r0 + 32, 64 * g:64 * g + 64],
                        KC[:, qo + 32 * h:qo + 32 * h + 32],
                        VC[:, vo + 64 * h:vo + 64 * h + 64],
                        start=True, stop=True, tile_position=(0, r0)), "t")
                st_t[c] = ds
            P.wait("vector", "t", st_t[c])
            P.wait("vector", "s", ealc_s[c])
            P.cop("vector", lambda e: e.tensor_tensor(
                SSB[:], SSB[:], XP[:, 0:256], ALU.add))
            for g in range(4):
                P.cop("vector", lambda e, g=g, par=par: e.tensor_scalar(
                    SSB[:, 64 * g:64 * g + 64], SSB[:, 64 * g:64 * g + 64],
                    EALC[:, 4 * par + g:4 * par + g + 1], None, ALU.mult))
            cpall = P.cop("vector", lambda e: e.tensor_copy(SBB[:], SSB[:]))
            for g in range(4):
                sbb_v[(c, g)] = cpall

        def emit_epi(c):
            par = c % 2
            vo = 1024 * par
            P.wait("scalar", "t", st_t[c])
            P.wait("scalar", "v", gc_v[min(c + 1, NT - 1)])
            P.wait("scalar", "s", P.cnt["s"])
            sq_s[c] = P.inc("scalar", lambda e: e.activation(
                SQB[:], OP2[:, 0:1024], AF.Square, bias=0.0, scale=1.0), "s")
            P.wait("vector", "s", sq_s[c])
            for h in range(HEADS):
                P.cop("vector", lambda e, h=h: e.tensor_reduce(
                    OCOL[:, h:h + 1], SQB[:, 64 * h:64 * (h + 1)], AX.X, ALU.add))
            oc = P.cnt["v"]
            oc_v[c] = oc
            P.wait("scalar", "v", oc)
            P.wait("scalar", "s", P.cnt["s"])
            P.inc("scalar", lambda e: e.activation(
                INVC[:], OCOL[:], AF.Ln, bias=getc(1e-5), scale=1.0 / 64.0), "s")
            P.wait("scalar", "s", P.cnt["s"])
            ivd = P.inc("scalar", lambda e: e.activation(
                INVC[:], INVC[:], AF.Exp, bias=0.0, scale=-0.5), "s")
            P.wait("vector", "s", ivd)
            for pp in (0, 1):
                if pp in x2_tok:
                    P.wait_tok("vector", x2_tok[pp])
            for h in range(HEADS):
                P.cop("vector", lambda e, h=h, vo=vo: e.scalar_tensor_tensor(
                    OG[:, 64 * h:64 * (h + 1)], OP2[:, 64 * h:64 * (h + 1)],
                    INVC[:, h:h + 1], GC[:, vo + 64 * h:vo + 64 * (h + 1)],
                    ALU.mult, ALU.mult))
            og_v[c] = P.cnt["v"]
            P.wait("tensor", "v", og_v[c])
            if btr[0] > 0:
                P.wait("tensor", "v", bcons[btr[0] - 1])
            base0 = btr[0]
            for b in range(ND):
                n = bslot()
                trd = P.inc("tensor", lambda e, n=n, b=b: e.transpose(
                    TPB[:, ts(n % 8)], OG[:, ts(b)], IDBF[:]), "t")
            P.wait("vector", "t", trd)
            for g2 in range(2):
                cc = P.cop("vector", lambda e, g2=g2, base0=base0: e.tensor_copy(
                    OGT[:, 512 * g2:512 * (g2 + 1)], TPSLICE(TPB, base0 + 4 * g2)))
                for n in range(base0 + 4 * g2, base0 + 4 * g2 + 4):
                    bcons[n] = cc
            P.wait("tensor", "v", cc)
            if c >= 1:
                P.wait("tensor", "v", x2add_v[c - 1])
            for k in range(ND):
                for nn in range(2):
                    wo_mm = P.inc("tensor", lambda e, k=k, nn=nn: e.matmul(
                        XP[:, ts(nn, 512)], OGT[:, ts(k)],
                        WO[:, 1024 * k + 512 * nn:1024 * k + 512 * (nn + 1)],
                        start=(k == 0), stop=(k == ND - 1)), "t")
            if btr[0] > 0:
                P.wait("tensor", "v", bcons[btr[0] - 1])
            base1 = btr[0]
            for k in range(ND):
                n = bslot()
                trd = P.inc("tensor", lambda e, n=n, k=k, c=c: e.transpose(
                    TPB[:, ts(n % 8)], HT[:, k * T + 128 * c:k * T + 128 * (c + 1)],
                    IDBF[:]), "t")
            P.wait("vector", "t", trd)
            P.wait("vector", "t", wo_mm)
            P.wait("vector", "s", ealc_s[min(c + 1, NT - 1)])
            for g2 in range(2):
                cc = P.cop("vector", lambda e, g2=g2, base1=base1: e.tensor_copy(
                    OGT[:, 512 * g2:512 * (g2 + 1)], TPSLICE(TPB, base1 + 4 * g2)))
                for n in range(base1 + 4 * g2, base1 + 4 * g2 + 4):
                    bcons[n] = cc
            for BUFH, ho in ((SPF, 0), (ASB, 512)):
                P.cop("vector", lambda e, BUFH=BUFH, ho=ho: e.tensor_tensor(
                    BUFH[:], OGT[:, ho:ho + 512], LNB[:, ho:ho + 512], ALU.subtract))
                P.cop("vector", lambda e, BUFH=BUFH, ho=ho: e.tensor_tensor(
                    BUFH[:], BUFH[:], LNWI[:, ho:ho + 512], ALU.mult))
                P.cop("vector", lambda e, BUFH=BUFH, ho=ho: e.tensor_tensor(
                    BUFH[:], BUFH[:], XP[:, ho:ho + 512], ALU.add))
                x2add_v[c] = P.cop("vector", lambda e, BUFH=BUFH, ho=ho, c=c: e.tensor_tensor(
                    XRES[:, 1024 * c + ho:1024 * c + ho + 512],
                    XRES[:, 1024 * c + ho:1024 * c + ho + 512], BUFH[:], ALU.add))
                x2b = P.cop("vector", lambda e, BUFH=BUFH, ho=ho: e.tensor_copy(
                    OG[:, ho:ho + 512], BUFH[:]))
            P.wait("sync", "v", x2b)
            x2_tok[par] = P.dma("y", lambda e, c=c: e.dma_start(
                out=x2d[ts(c), :], in_=OG[:]))

        P.wait_tok("tensor", g1l)
        P.wait_tok("tensor", trl)
        P.dma_barrier("tensor")
        P.dma_barrier("vector")
        P.wait("vector", "v", None)
        P.cop("vector", lambda e: e.memset(SSB[:], 0.0))
        sbb_init = P.cop("vector", lambda e: e.memset(SBB[:], 0.0))

        emit_prep(0)
        for c in range(NT):
            emit_state(c)
            if c + 1 < NT:
                emit_prep(c + 1)
            emit_epi(c)
        es.close()
    P.fence()
    l1_mlp(nc, P, XRES, COLS, ST, IDBF, SCRA, wmlp1, bmlp1, wmlp2, bmlp2_bc,
           getc, x2d, lnw_bc, lnb_bc)


def l1_mlp(nc, P, XRES, COLS, ST, IDBF, SCRA, wmlp1, bmlp1, wmlp2, bmlp2_bc,
           getc, x2d, lnw_bc, lnb_bc):
    R2C, NMR2C = ST[:, 8 * NT:9 * NT], ST[:, 9 * NT:10 * NT]
    with (
        nc.sbuf_tensor("X2G", [128, 4 * 1024], BF16) as X2G,
        nc.sbuf_tensor("X2T", [128, ND * 512], BF16) as X2T,
        nc.sbuf_tensor("HMT", [128, 16 * 512], BF16) as HMT,
        nc.sbuf_tensor("MW1", [128, ND * 2048], BF16) as MW1,
        nc.sbuf_tensor("MW2", [128, 16 * 1024], BF16) as MW2,
        nc.sbuf_tensor("BM1", [128, 16], F32) as BM1,
        nc.sbuf_tensor("BM2", [128, D], F32) as BM2,
        nc.sbuf_tensor("TD", [128, 1024], BF16) as TD,
        nc.sbuf_tensor("LNW2", [128, D], BF16) as LNW2,
        nc.sbuf_tensor("LNB2", [128, D], BF16) as LNB2,
        nc.psum_tensor("MPS", [128, 1024], F32) as MPS,
        nc.psum_tensor("TPM", [128, 1024], BF16) as TPM,
        nc.psum_tensor("OPM", [128, 2048], F32) as OPM,
    ):
        w1tok = P.dma("g", lambda e: e.dma_start(
            out=MW1[:].rearrange("p (k o) -> p k o", k=ND),
            in_=wmlp1.rearrange("(k p) o -> p k o", p=128)))
        w2tok = P.dma("g", lambda e: e.dma_start(
            out=MW2[:].rearrange("p (k o) -> p k o", k=16),
            in_=wmlp2.rearrange("(k p) o -> p k o", p=128)))
        b1tok = P.dma("y", lambda e: e.dma_start(out=BM1[:], in_=bmlp1[:, :]))
        lw2 = P.dma("g", lambda e: e.dma_start(out=LNW2[:], in_=lnw_bc[:, :]))
        lb2 = P.dma("g", lambda e: e.dma_start(out=LNB2[:], in_=lnb_bc[:, :]))
        P.wait_tok("vector", lw2)
        P.wait_tok("vector", lb2)
        b2tok = P.dma("y", lambda e: e.dma_start(out=BM2[:], in_=bmlp2_bc[:, :]))
        P.wait_tok("tensor", w1tok)
        P.wait_tok("tensor", w2tok)
        P.wait_tok("scalar", b1tok)
        P.wait_tok("vector", b2tok)
        x2toks = []
        for gb in range(4):
            x2toks.append(P.dma("y", lambda e, gb=gb: e.dma_start(
                out=X2G[:, ts(gb, 1024)], in_=x2d[ts(gb), :])))
        SUM, SUM2 = COLS[:, 0:NT], COLS[:, 16:16 + NT]
        for ti in range(NT):
            gslot = ti % 4
            P.wait_tok("vector", x2toks[ti])
            P.wait_tok("scalar", x2toks[ti])
            P.cop("vector", lambda e, gslot=gslot, ti=ti: e.tensor_reduce(
                SUM[:, ti:ti + 1], X2G[:, ts(gslot, 1024)], AX.X, ALU.add))
            P.wait("scalar", "s", P.cnt["s"])
            sq = P.inc("scalar", lambda e, gslot=gslot, ti=ti: e.activation(
                SCRA[:], X2G[:, ts(gslot, 1024)], AF.Square, bias=0.0, scale=1.0,
                accum_out=SUM2[:, ti:ti + 1]), "s")
            if ti + 4 < NT:
                P.wait("sync", "v", P.cnt["v"])
                P.wait("sync", "s", P.cnt["s"])
                x2toks.append(P.dma("y", lambda e, ti=ti: e.dma_start(
                    out=X2G[:, ts((ti + 4) % 4, 1024)], in_=x2d[ts(ti + 4), :])))
        P.wait("vector", "s", P.cnt["s"])
        P.wait("vector", "v", None)
        M = COLS[:, 32:32 + NT]
        VAR = COLS[:, 0:NT]
        P.cop("vector", lambda e: e.tensor_scalar(M, SUM, 1.0 / D, None, ALU.mult))
        P.cop("vector", lambda e: e.tensor_scalar(SUM2, SUM2, 1.0 / D, None, ALU.mult))
        P.cop("vector", lambda e: e.tensor_tensor(VAR, M, M, ALU.mult))
        vd = P.cop("vector", lambda e: e.tensor_tensor(VAR, SUM2, VAR, ALU.subtract))
        P.wait("scalar", "v", vd)
        P.wait("scalar", "s", P.cnt["s"])
        P.inc("scalar", lambda e: e.activation(SUM2, VAR, AF.Ln, bias=getc(1e-5), scale=1.0), "s")
        P.wait("scalar", "s", P.cnt["s"])
        rd = P.inc("scalar", lambda e: e.activation(R2C, SUM2, AF.Exp, bias=0.0, scale=-0.5), "s")
        P.wait("vector", "s", rd)
        P.cop("vector", lambda e: e.tensor_tensor(NMR2C, M, R2C, ALU.mult))
        nm = P.cop("vector", lambda e: e.tensor_scalar(NMR2C, NMR2C, -1.0, None, ALU.mult))
        P.wait("sync", "v", P.cnt["v"])
        P.wait("sync", "s", P.cnt["s"])
        for g in range(4):
            for gb in range(4):
                ti = 4 * g + gb
                P.wait("sync", "v", P.cnt["v"])
                tok = P.dma("y", lambda e, gb=gb, ti=ti: e.dma_start(
                    out=X2G[:, ts(gb, 1024)], in_=x2d[ts(ti), :]))
                P.wait_tok("vector", tok)
                P.wait("vector", "v", nm)
                P.cop("vector", lambda e, gb=gb, ti=ti: e.tensor_scalar(
                    TD[:], X2G[:, ts(gb, 1024)], R2C[:, ti:ti + 1],
                    NMR2C[:, ti:ti + 1], ALU.mult, ALU.add))
                P.cop("vector", lambda e: e.tensor_tensor(TD[:], TD[:], LNW2[:], ALU.mult))
                xn = P.cop("vector", lambda e: e.tensor_tensor(TD[:], TD[:], LNB2[:], ALU.add))
                P.wait("tensor", "v", xn)
                for k in range(ND):
                    trd = P.inc("tensor", lambda e, k=k: e.transpose(
                        TPM[:, ts(k)], TD[:, ts(k)], IDBF[:]), "t")
                P.wait("vector", "t", trd)
                for k in range(ND):
                    P.cop("vector", lambda e, k=k, gb=gb: e.tensor_copy(
                        X2T[:, k * 512 + 128 * gb: k * 512 + 128 * (gb + 1)], TPM[:, ts(k)]))
                P.wait("tensor", "v", P.cnt["v"])
            evs = 0
            for ht in range(16):
                if ht >= 2:
                    P.wait("tensor", "s", evs - 1)
                for k in range(ND):
                    mm = P.inc("tensor", lambda e, ht=ht, k=k: e.matmul(
                        MPS[:, ts(ht % 2, 512)],
                        MW1[:, 2048 * k + 128 * ht: 2048 * k + 128 * (ht + 1)],
                        X2T[:, ts(k, 512)],
                        start=(k == 0), stop=(k == ND - 1)), "t")
                P.wait("scalar", "t", mm)
                P.wait("scalar", "s", P.cnt["s"])
                evs = P.inc("scalar", lambda e, ht=ht: e.activation(
                    HMT[:, ts(ht, 512)], MPS[:, ts(ht % 2, 512)], AF.Gelu,
                    bias=BM1[:, ht:ht + 1], scale=1.0), "s")
            P.wait("tensor", "s", evs)
            for tc in range(4):
                ti = 4 * g + tc
                for k in range(16):
                    for nn in range(2):
                        mm = P.inc("tensor", lambda e, tc=tc, k=k, nn=nn: e.matmul(
                            OPM[:, ts((tc % 2) * 2 + nn, 512)],
                            HMT[:, 512 * k + 128 * tc: 512 * k + 128 * (tc + 1)],
                            MW2[:, 1024 * k + 512 * nn: 1024 * k + 512 * (nn + 1)],
                            start=(k == 0), stop=(k == 15)), "t")
                P.wait("vector", "t", mm)
                P.cop("vector", lambda e, ti=ti, tc=tc: e.tensor_tensor(
                    XRES[:, ts(ti, 1024)], XRES[:, ts(ti, 1024)],
                    OPM[:, (tc % 2) * 1024: (tc % 2) * 1024 + 1024], ALU.add))
                ad = P.cop("vector", lambda e, ti=ti: e.tensor_tensor(
                    XRES[:, ts(ti, 1024)], XRES[:, ts(ti, 1024)], BM2[:], ALU.add))
                P.wait("tensor", "v", ad)



def l1_ffn(nc, P, XRES, COLS, IDBF, w1t1, b1f, w2t1, b2f_bc, getc, xn1td):
    with (
        nc.sbuf_tensor("XN1F", [128, ND * T], BF16) as XN1F,
        nc.sbuf_tensor("HQT", [128, 32 * T], FP8) as HQT,
        nc.sbuf_tensor("QW2", [128, 32 * 1024], FP8) as QW2,
        nc.sbuf_tensor("MWS", [128, 2 * ND * 128], BF16) as MWS,
        nc.sbuf_tensor("B1F", [128, 32], F32) as B1F,
        nc.sbuf_tensor("BQ16", [128, D], mybir.dt.float16) as BQ16,
        nc.sbuf_tensor("HF", [128, 1024], F32) as HF,
        nc.psum_tensor("HP1", [128, 2048], F32) as HP1,
        nc.psum_tensor("OPF", [128, 2048], F32) as OPF,
    ):
        OT16 = HF[:, 0:512].bitcast(mybir.dt.float16)
        xl = P.dma("y", lambda e: e.dma_start(out=XN1F[:], in_=xn1td[:, :]))
        ql = P.dma("g", lambda e: e.dma_start(
            out=QW2[:].rearrange("p (k o) -> p k o", k=32),
            in_=w2t1.rearrange("(k p) o -> p k o", p=128)))
        bl = P.dma("y", lambda e: e.dma_start(out=B1F[:], in_=b1f[:, :]))
        b2l = P.dma("g", lambda e: e.dma_start(out=BQ16[:], in_=b2f_bc[:, :]))
        P.wait_tok("tensor", xl)
        P.wait_tok("scalar", bl)
        mm_t = {}
        hr_s = {}
        hq_v = {}
        for ht in range(32):
            PS = HP1 if ht % 2 == 0 else OPF
            if ht >= 2:
                P.wait("gpsimd", "t", mm_t[ht - 2])
            wtok = P.dma("g", lambda e, ht=ht: e.dma_start(
                out=MWS[:, 1024 * (ht % 2):1024 * (ht % 2 + 1)]
                    .rearrange("p (k o) -> p k o", k=ND),
                in_=w1t1.rearrange("(k p) o -> p k o", p=128)[:, :, 128 * ht:128 * (ht + 1)]))
            P.wait_tok("tensor", wtok)
            if ht >= 2:
                P.wait("tensor", "s", hr_s[ht - 2])
            for k in range(ND):
                for nn in range(4):
                    mm = P.inc("tensor", lambda e, ht=ht, k=k, nn=nn, PS=PS: e.matmul(
                        PS[:, ts(nn, 512)],
                        MWS[:, 1024 * (ht % 2) + 128 * k:1024 * (ht % 2) + 128 * (k + 1)],
                        XN1F[:, k * T + 512 * nn: k * T + 512 * (nn + 1)],
                        start=(k == 0), stop=(k == ND - 1)), "t")
            mm_t[ht] = mm
            for hh in range(2):
                P.wait("scalar", "t", mm)
                if ht >= 1 or hh == 1:
                    P.wait("scalar", "v", hq_v[(ht, hh - 1) if hh else (ht - 1, 1)])
                P.wait("scalar", "s", P.cnt["s"])
                hr_s[ht] = P.inc("scalar", lambda e, ht=ht, hh=hh, PS=PS: e.activation(
                    HF[:], PS[:, 1024 * hh:1024 * (hh + 1)], AF.Relu,
                    bias=B1F[:, ht:ht + 1], scale=1.0), "s")
                P.wait("vector", "s", hr_s[ht])
                hq_v[(ht, hh)] = P.cop("vector", lambda e, ht=ht, hh=hh: e.tensor_copy(
                    HQT[:, 2048 * ht + 1024 * hh:2048 * ht + 1024 * (hh + 1)], HF[:]))
        P.wait_tok("tensor", ql)
        add_v = {}
        for ti in range(NT):
            if ti == 0:
                P.wait("tensor", "s", hr_s[31])
                P.wait("tensor", "v", hq_v[(31, 1)])
            if ti >= 2:
                P.wait("tensor", "v", add_v[ti - 2])
            for k in range(32):
                for nn in range(2):
                    mm = P.inc("tensor", lambda e, ti=ti, k=k, nn=nn: e.matmul(
                        OPF[:, ts((ti % 2) * 2 + nn, 512)],
                        HQT[:, 2048 * k + 128 * ti: 2048 * k + 128 * (ti + 1)],
                        QW2[:, 1024 * k + 512 * nn: 1024 * k + 512 * (nn + 1)],
                        start=(k == 0), stop=(k == 31)), "t")
            P.wait("vector", "t", mm)
            P.wait_tok("vector", b2l)
            P.cop("vector", lambda e, ti=ti: e.tensor_tensor(
                OT16[:], OPF[:, (ti % 2) * 1024: (ti % 2) * 1024 + 1024], BQ16[:], ALU.add))
            add_v[ti] = P.cop("vector", lambda e, ti=ti: e.tensor_tensor(
                XRES[:, ts(ti, 1024)], XRES[:, ts(ti, 1024)], OT16[:], ALU.add))



_CACHE = {}


def _build_program(stop_after="full"):
    if stop_after not in _CACHE:
        nc = bass.Bass("TRN2", target_bir_lowering=False, debug=False)
        P = build(nc, stop_after)
        with nc.Block() as blk:
            P.run(blk)
        _CACHE[stop_after] = nc
    return _CACHE[stop_after]


def _in_maps(inputs):
    d = inputs
    f32 = np.float32

    def asc(a):
        return np.ascontiguousarray(np.asarray(a, dtype=f32))

    shared = {
        "mq": asc(d["L0_attn_Mq"]),
        "mk": asc(d["L0_attn_Mk"]),
        "mv": asc(d["L0_attn_Mv"]),
        "w1t0": asc(np.asarray(d["L0_ffn_w1"], dtype=f32).T),
        "w2t0": asc(np.asarray(d["L0_ffn_w2"], dtype=f32).T),
        "n1w_bc": asc(np.broadcast_to(np.asarray(d["L0_ffn_norm1_w"], f32), (128, D))),
        "n2w_bc": asc(np.broadcast_to(np.asarray(d["L0_ffn_norm2_w"], f32), (128, 4096))),
        "actsb": asc(np.broadcast_to(np.concatenate([
            np.asarray(d["L0_ffn_act_scale"], f32).reshape(1),
            np.asarray(d["L0_ffn_act_bias"], f32).reshape(1)]), (128, 2))),
        "lnw_bc": asc(np.broadcast_to(np.asarray(d["L1_attn_ln_w"], f32), (128, D))),
        "lnb_bc": asc(np.broadcast_to(np.asarray(d["L1_attn_ln_b"], f32), (128, D))),
        "wq1": asc(d["L1_attn_Wq"]),
        "wk1": asc(d["L1_attn_Wk"]),
        "wv1": asc(d["L1_attn_Wv"]),
        "wg1": asc(d["L1_attn_Wg"]),
        "gkw1": asc(d["L1_attn_gk_w1"]),
        "gkw2": asc(d["L1_attn_gk_w2"]),
        "gkb2_bc": asc(np.broadcast_to(np.asarray(d["L1_attn_gk_b2"], f32), (128, 512))),
        "rmsw_bc": asc(np.broadcast_to(np.tile(np.asarray(d["L1_attn_rms_w"], f32), 16), (128, D))),
        "wo1": asc(d["L1_attn_Wo"]),
        "wmlp1": asc(d["L1_attn_W1"]),
        "bmlp1": asc(np.asarray(d["L1_attn_b1"], f32).reshape(16, 128).T),
        "wmlp2": asc(d["L1_attn_W2"]),
        "bmlp2_bc": asc(np.broadcast_to(np.asarray(d["L1_attn_b2"], f32), (128, D))),
        "w1t1": asc(np.asarray(d["L1_ffn_w1"], dtype=f32).T),
        "b1f": asc(np.asarray(d["L1_ffn_b1"], f32).reshape(32, 128).T),
        "w2t1": asc(np.asarray(d["L1_ffn_w2"], dtype=f32).T),
        "b2f_bc": asc(np.broadcast_to(np.asarray(d["L1_ffn_b2"], f32), (128, D))),
    }
    x = np.asarray(d["L0_attn_x"], dtype=f32)
    return [{**shared, "x": np.ascontiguousarray(x[b])} for b in range(x.shape[0])]


def kernel(**inputs):
    nc = _build_program("full")
    maps = _in_maps(inputs)
    res = run_bass_kernel_spmd(nc, maps, core_ids=list(range(8)))
    return np.stack([r["y"] for r in res.results], axis=0)


def kernel_dev(stop_after, inputs, cores=1):
    nc = _build_program(stop_after)
    maps = _in_maps(inputs)[:cores]
    res = run_bass_kernel_spmd(nc, maps, core_ids=list(range(cores)))
    return np.stack([r["y"] for r in res.results], axis=0)


# version 11
# speedup vs baseline: 1.2250x; 1.0381x over previous
import contextlib
import numpy as np
import concourse.bass as bass
import concourse.mybir as mybir
from concourse.bass_utils import run_bass_kernel_spmd

F32 = mybir.dt.float32
F32R = mybir.dt.float32r
BF16 = mybir.dt.bfloat16
FP8 = mybir.dt.float8e4
I32 = mybir.dt.int32
AF = mybir.ActivationFunctionType
ALU = mybir.AluOpType
AX = mybir.AxisListType

T = 2048
D = 1024
NT = 16
ND = 8


def ts(i, n=128):
    return slice(i * n, (i + 1) * n)


def TPSLICE(PT, base):
    return PT[:, (base % 8) * 128:(base % 8) * 128 + 512]


class Plan:

    ENGS = ("sync", "gpsimd", "vector", "scalar", "tensor")

    NLANE = 4

    def __init__(self, nc):
        self.nc = nc
        self.q = {e: [] for e in self.ENGS}
        self.h = {}
        self.cnt = {}
        self.last_wait = {}
        names = ["t", "v", "s"]
        for qn in ("y", "g"):
            for i in range(self.NLANE):
                names.append(qn + str(i))
        for n in names:
            self.h[n] = nc.alloc_semaphore("sem_" + n)
            self.cnt[n] = 0
        self.rr = {"y": 0, "g": 0}

    def dma(self, queue, fn):
        eng = "sync" if queue == "y" else "gpsimd"
        i = self.rr[queue]
        self.rr[queue] = (i + 1) % self.NLANE
        lane = queue + str(i)
        if self.cnt[lane] > 0:
            self.wait(eng, lane, self.cnt[lane])
        tok = (lane, self.inc(eng, fn, lane, 16))
        return tok

    def wait_tok(self, eng, tok):
        self.wait(eng, tok[0], tok[1])

    def cop(self, eng, fn):
        sem = {"vector": "v", "scalar": "s", "tensor": "t"}[eng]
        self.wait(eng, sem, self.cnt[sem])
        return self.inc(eng, fn, sem)

    def fence(self):
        sems = [n for n in self.cnt if self.cnt[n] > 0]
        for eng in self.ENGS:
            for n in sems:
                self.wait(eng, n, self.cnt[n])

    def dma_barrier(self, eng):
        for lane in list(self.cnt):
            if lane[0] in "yg" and len(lane) == 2 and self.cnt[lane] > 0:
                self.wait(eng, lane, self.cnt[lane])

    def emit(self, eng, fn):
        self.q[eng].append(fn)

    def inc(self, eng, fn, sem, k=1):
        h = self.h[sem]
        self.q[eng].append(lambda e, fn=fn, h=h, k=k: fn(e).then_inc(h, k))
        self.cnt[sem] += k
        return self.cnt[sem]

    def wait(self, eng, sem, val=None):
        if val is None:
            val = self.cnt[sem]
        assert val <= self.cnt[sem], f"wait on future value of {sem}"
        key = (eng, sem)
        if self.last_wait.get(key, -1) >= val:
            return
        self.last_wait[key] = val
        h = self.h[sem]
        self.q[eng].append(lambda e, h=h, v=val: e.wait_ge(h, v))

    def run(self, block):
        for eng in self.ENGS:
            block.__getattribute__(eng)(lambda e, eng=eng: [f(e) for f in self.q[eng]])


def build(nc, stop_after="full"):
    dt = nc.dram_tensor
    x_in = dt("x", [T, D], F32, kind="ExternalInput").ap()
    y_out = dt("y", [T, D], F32, kind="ExternalOutput").ap()
    mq = dt("mq", [D, 512], F32, kind="ExternalInput").ap()
    mk = dt("mk", [D, 512], F32, kind="ExternalInput").ap()
    mv = dt("mv", [D, 1024], F32, kind="ExternalInput").ap()
    w1t0 = dt("w1t0", [D, 4096], F32, kind="ExternalInput").ap()
    w2t0 = dt("w2t0", [4096, D], F32, kind="ExternalInput").ap()
    n1w_bc = dt("n1w_bc", [128, D], F32, kind="ExternalInput").ap()
    n2w_bc = dt("n2w_bc", [128, 4096], F32, kind="ExternalInput").ap()
    actsb = dt("actsb", [128, 2], F32, kind="ExternalInput").ap()
    lnw_bc = dt("lnw_bc", [128, D], F32, kind="ExternalInput").ap()
    lnb_bc = dt("lnb_bc", [128, D], F32, kind="ExternalInput").ap()
    wq1 = dt("wq1", [D, 512], F32, kind="ExternalInput").ap()
    wk1 = dt("wk1", [D, 512], F32, kind="ExternalInput").ap()
    wv1 = dt("wv1", [D, 1024], F32, kind="ExternalInput").ap()
    wg1 = dt("wg1", [D, 1024], F32, kind="ExternalInput").ap()
    gkw1 = dt("gkw1", [D, 16], F32, kind="ExternalInput").ap()
    gkw2 = dt("gkw2", [16, 512], F32, kind="ExternalInput").ap()
    gkb2_bc = dt("gkb2_bc", [128, 512], F32, kind="ExternalInput").ap()
    rmsw_bc = dt("rmsw_bc", [128, D], F32, kind="ExternalInput").ap()
    wo1 = dt("wo1", [D, D], F32, kind="ExternalInput").ap()
    wmlp1 = dt("wmlp1", [D, 2048], F32, kind="ExternalInput").ap()
    bmlp1 = dt("bmlp1", [128, 16], F32, kind="ExternalInput").ap()
    wmlp2 = dt("wmlp2", [2048, D], F32, kind="ExternalInput").ap()
    bmlp2_bc = dt("bmlp2_bc", [128, D], F32, kind="ExternalInput").ap()
    w1t1 = dt("w1t1", [D, 4096], F32, kind="ExternalInput").ap()
    b1f = dt("b1f", [128, 32], F32, kind="ExternalInput").ap()
    w2t1 = dt("w2t1", [4096, D], F32, kind="ExternalInput").ap()
    b2f_bc = dt("b2f_bc", [128, D], F32, kind="ExternalInput").ap()

    c_ident = nc.inline_tensor(np.eye(128, dtype=np.float32), "c_ident")
    c_triu = nc.inline_tensor(np.triu(np.ones((128, 128), np.float32)), "c_triu")
    x2d = dt("x2d", [T, D], BF16, kind="Internal").ap()
    xn1td = dt("xn1td", [128, ND * T], BF16, kind="Internal").ap()

    P = Plan(nc)

    with (
        nc.sbuf_tensor("XRES", [128, NT * 1024], F32) as XRES,
        nc.sbuf_tensor("IDBF", [128, 128], BF16) as IDBF,
        nc.sbuf_tensor("ST", [128, 10 * NT], F32) as ST,
        nc.sbuf_tensor("COLS", [128, 64], F32) as COLS,
        nc.sbuf_tensor("ONESR", [1, 128], F32) as ONESR,
        nc.sbuf_tensor("CONSTC", [128, 16], F32) as CONSTC,
        nc.sbuf_tensor("SCRA", [128, 1024], BF16) as SCRA,
        nc.Block() as block,
    ):
        def st(i):
            return ST[:, i * NT:(i + 1) * NT]
        R0, NMR0, R2, NMR2, MX0, MX2 = (st(i) for i in range(6))
        R1, NMR1 = st(6), st(7)

        CVALS = [1e-5, 1e-8, 4096.0 * 1e-8, float(np.log(127.0 * 1e5)),
                 float(0.5 * np.log(4096.0)), 1.0,
                 float(-0.5 * np.log(32.0)), float(np.log(2.0)), 1e-35,
                 float(-3.0 * np.log(2.0)), float(3.0 * np.log(2.0)),
                 float(-10.0 * np.log(2.0)), float(10.0 * np.log(2.0))]
        CIDX = {v: i for i, v in enumerate(CVALS)}

        def getc(v):
            return CONSTC[:, CIDX[v]:CIDX[v] + 1]

        x_toks = [P.dma("y", lambda e, ti=ti: e.dma_start(out=XRES[:, ts(ti, 1024)],
                    in_=x_in[ts(ti), :])) for ti in range(NT)]
        id_tok = P.dma("g", lambda e: e.dma_start(out=IDBF[:], in_=c_ident.ap()[:, :]))
        P.cop("vector", lambda e: e.memset(ONESR[:], 1.0))
        for i, v in enumerate(CVALS):
            P.cop("vector", lambda e, i=i, v=v: e.memset(CONSTC[:, i:i + 1], v))
        ones_done = P.cop("vector", lambda e: e.memset(COLS[:, 63:64], 1.0))
        P.wait("scalar", "v", ones_done)

        def ln_stats(r_col, nmr_col, dep_v):
            SUM, SUM2 = COLS[:, 0:NT], COLS[:, 16:16 + NT]
            for tok in x_toks:
                P.wait_tok("vector", tok)
                P.wait_tok("scalar", tok)
            P.wait("vector", "v", dep_v)
            if dep_v:
                P.wait("scalar", "v", dep_v)
            for ti in range(NT):
                P.cop("vector", lambda e, ti=ti: e.tensor_reduce(
                    SUM[:, ti:ti + 1], XRES[:, ts(ti, 1024)], AX.X, ALU.add))
                P.wait("scalar", "s", P.cnt["s"])
                P.cop("scalar", lambda e, ti=ti: e.activation(
                    SCRA[:], XRES[:, ts(ti, 1024)], AF.Square, bias=0.0, scale=1.0,
                    accum_out=SUM2[:, ti:ti + 1]))
            P.wait("vector", "s", P.cnt["s"])
            P.wait("vector", "v", None)
            M = COLS[:, 32:32 + NT]
            VAR = COLS[:, 0:NT]
            P.cop("vector", lambda e: e.tensor_scalar(M, SUM, 1.0 / D, None, ALU.mult))
            P.cop("vector", lambda e: e.tensor_scalar(SUM2, SUM2, 1.0 / D, None, ALU.mult))
            P.cop("vector", lambda e: e.tensor_tensor(VAR, M, M, ALU.mult))
            vd = P.cop("vector", lambda e: e.tensor_tensor(VAR, SUM2, VAR, ALU.subtract))
            P.wait("scalar", "v", vd)
            P.cop("scalar", lambda e: e.activation(SUM2, VAR, AF.Ln, bias=getc(1e-5), scale=1.0))
            rd = P.cop("scalar", lambda e: e.activation(r_col, SUM2, AF.Exp, bias=0.0, scale=-0.5))
            P.wait("vector", "s", rd)
            P.cop("vector", lambda e: e.tensor_tensor(nmr_col, M, r_col, ALU.mult))
            nm = P.cop("vector", lambda e: e.tensor_scalar(nmr_col, nmr_col, -1.0, None, ALU.mult))
            return VAR, nm

        VAR0, nm0 = ln_stats(R0, NMR0, dep_v=0)
        TMP = COLS[:, 16:16 + NT]
        P.cop("vector", lambda e: e.tensor_tensor(TMP, VAR0, R0, ALU.mult))
        t2 = P.cop("vector", lambda e: e.tensor_tensor(TMP, TMP, R0, ALU.mult))
        P.wait("scalar", "v", t2)
        P.cop("scalar", lambda e: e.activation(TMP, TMP, AF.Ln, bias=getc(1e-8), scale=1.0))
        iv = P.cop("scalar", lambda e: e.activation(TMP, TMP, AF.Exp, bias=0.0, scale=-0.5))
        P.wait("vector", "s", iv)
        P.cop("vector", lambda e: e.tensor_tensor(R2, R0, TMP, ALU.mult))
        l0_cols = P.cop("vector", lambda e: e.tensor_tensor(NMR2, NMR0, TMP, ALU.mult))

        with nc.sbuf_tensor("XNT", [128, ND * T], BF16) as XNT:
            P.wait("vector", "v", l0_cols)
            P.wait_tok("tensor", id_tok)
            l0_ffn(nc, P, XRES, w1t0, w2t0, n1w_bc, n2w_bc, actsb, R0, NMR0,
                   R2, NMR2, MX0, COLS, ONESR, IDBF, SCRA, getc, XNT)
            if stop_after == "l0ffn":
                finish(nc, P, XRES, y_out)
                return P

            P.fence()
            l0_attn(nc, P, XRES, XNT, mq, mk, mv, COLS)
        if stop_after == "l0":
            finish(nc, P, XRES, y_out)
            return P

        P.fence()
        _, nm1 = ln_stats(R1, NMR1, dep_v=P.cnt["v"])
        l1_attn(nc, P, XRES, R1, NMR1, COLS, ST, IDBF, ONESR, c_ident, c_triu,
                lnw_bc, lnb_bc, wq1, wk1, wv1, wg1, gkw1, gkw2, gkb2_bc, rmsw_bc,
                wo1, wmlp1, bmlp1, wmlp2, bmlp2_bc, SCRA, getc, x2d, xn1td)
        if stop_after == "l1attn":
            finish(nc, P, XRES, y_out)
            return P
        P.fence()
        l1_ffn(nc, P, XRES, COLS, IDBF, w1t1, b1f, w2t1, b2f_bc, getc, xn1td)

        finish(nc, P, XRES, y_out)
    return P


def finish(nc, P, XRES, y_out):
    P.wait("gpsimd", "v", P.cnt["v"])
    P.wait("gpsimd", "s", P.cnt["s"])
    P.wait("gpsimd", "t", P.cnt["t"])
    for ti in range(NT):
        P.dma("g", lambda e, ti=ti: e.dma_start(
            out=y_out[ts(ti), :], in_=XRES[:, ts(ti, 1024)]))
    P.dma_barrier("gpsimd")
    P.dma_barrier("sync")



def l0_attn(nc, P, XRES, XNT, mq, mk, mv, COLS):
    with (
        nc.sbuf_tensor("QT2", [128, 4 * T], BF16) as QT2,
        nc.sbuf_tensor("KT2", [128, 4 * T], BF16) as KT2,
        nc.sbuf_tensor("VB", [128, 16384], BF16) as VB,
        nc.sbuf_tensor("EB", [128, 2 * T], BF16) as EB,
        nc.sbuf_tensor("VSC", [128, 256], BF16) as VSC,
        nc.sbuf_tensor("MW", [128, 8192], BF16) as MW,
        nc.psum_tensor("SC", [128, 2048], F32) as SC,
        nc.psum_tensor("OPS", [128, 2048], F32) as OPS,
    ):
        for wsrc, dstT in ((mq, QT2), (mk, KT2)):
            P.wait("gpsimd", "t", P.cnt["t"])
            wl = P.dma("g", lambda e, wsrc=wsrc: e.dma_start(
                out=MW[:, 0:4096].rearrange("p (k o) -> p k o", k=ND),
                in_=wsrc.rearrange("(k p) o -> p k o", p=128)))
            P.wait_tok("tensor", wl)
            P.wait("tensor", "v", P.cnt["v"])
            for m in range(4):
                h0 = 2 * m
                r0 = 64 * (h0 // 4)
                for sub in range(8):
                    g = (m * 8 + sub) % 4
                    reg = SC[r0:r0 + 64, 512 * g:512 * g + 512]
                    for k in range(ND):
                        mm = P.inc("tensor", lambda e, m=m, sub=sub, k=k, reg=reg: e.matmul(
                            reg,
                            MW[:, k * 512 + 64 * sub: k * 512 + 64 * (sub + 1)],
                            XNT[:, k * T + 512 * m: k * T + 512 * (m + 1)],
                            start=(k == 0), stop=(k == ND - 1)), "t")
                    P.wait("vector", "t", mm)
                    blk0 = h0 % 4
                    ev = P.cop("vector", lambda e, dstT=dstT, r0=r0, blk0=blk0, sub=sub,
                               reg=reg: e.tensor_copy(
                        dstT[r0:r0 + 64, blk0 * T:(blk0 + 2) * T].rearrange(
                            "p (hh tl rest) -> p hh tl rest", hh=2, tl=2)
                            [:, :, :, sub * 128:(sub + 1) * 128],
                        reg.rearrange("p (hh tl t) -> p hh tl t", hh=2, tl=2)))
                    if g == 3:
                        P.wait("tensor", "v", ev)
        P.wait("gpsimd", "t", P.cnt["t"])
        wl = P.dma("g", lambda e: e.dma_start(
            out=MW[:].rearrange("p (k o) -> p k o", k=ND),
            in_=mv.rearrange("(k p) o -> p k o", p=128)))
        P.wait_tok("tensor", wl)
        P.wait("tensor", "v", P.cnt["v"])
        for h in range(8):
            for tlc in range(2):
                for sg in range(2):
                    g = (h * 4 + tlc * 2 + sg) % 4
                    reg = OPS[:, 512 * g:512 * g + 512]
                    for k in range(ND):
                        mm = P.inc("tensor", lambda e, h=h, tlc=tlc, sg=sg, k=k, reg=reg:
                            e.matmul(
                                reg,
                                XNT[:, k * T + 256 * h + 128 * tlc:
                                    k * T + 256 * h + 128 * (tlc + 1)],
                                MW[:, k * 1024 + 512 * sg: k * 1024 + 512 * (sg + 1)],
                                start=(k == 0), stop=(k == ND - 1)), "t")
                    P.wait("vector", "t", mm)
                    t0_ = h * 16 + 8 * sg + tlc
                    ev = P.cop("vector", lambda e, t0_=t0_, reg=reg: e.tensor_copy(
                        VB[:].rearrange("p (n t) -> p n t", t=128)[:, t0_:t0_ + 7:2, :],
                        reg.rearrange("p (s t) -> p s t", s=4)))
                    if g == 3:
                        P.wait("tensor", "v", ev)
        exp_s = {}
        outmm_t = {}
        rv_v = {}
        vs_v = {}
        evac_v = None

        def emit_out(h, jt):
            par = jt % 2
            P.wait("tensor", "v", vs_v[jt])
            if jt == 0 and evac_v is not None:
                P.wait("tensor", "v", evac_v)
            for tau in range(2):
                for s_ in range(8):
                    outmm_t[par] = P.inc("tensor", lambda e, tau=tau, s_=s_, jt=jt,
                                         par=par: e.matmul(
                        OPS[:, ts(8 * tau + s_)],
                        EB[:, 2048 * par + 1024 * tau + 128 * s_:
                            2048 * par + 1024 * tau + 128 * (s_ + 1)],
                        VSC[:, 128 * par:128 * (par + 1)],
                        start=(jt == 0 and (8 * tau + s_) % 4 == 0),
                        stop=(jt == NT - 1 and (8 * tau + s_) % 4 == 3)), "t")

        for h in range(8):
            r0 = 64 * (h // 4)
            blk = h % 4
            for jt in range(NT):
                sub, tlc = jt // 2, jt % 2
                par = jt % 2
                jb = blk * T + 1024 * tlc + 128 * sub
                for half in range(2):
                    if half in exp_s:
                        P.wait("tensor", "s", exp_s[half])
                    for nn in range(2):
                        scd = P.inc("tensor", lambda e, r0=r0, blk=blk, jb=jb,
                                    half=half, nn=nn: e.matmul(
                            SC[:, 1024 * half + 512 * nn:1024 * half + 512 * (nn + 1)],
                            KT2[r0:r0 + 64, jb:jb + 128],
                            QT2[r0:r0 + 64, blk * T + 1024 * half + 512 * nn:
                                blk * T + 1024 * half + 512 * (nn + 1)],
                            start=True, stop=True), "t")
                    P.wait("scalar", "t", scd)
                    if half == 0:
                        if par in rv_v:
                            P.wait("scalar", "v", rv_v[par])
                        if par in outmm_t:
                            P.wait("scalar", "t", outmm_t[par])
                    exp_s[half] = P.inc("scalar", lambda e, par=par, half=half: e.activation(
                        EB[:, 2048 * par + 1024 * half:2048 * par + 1024 * (half + 1)],
                        SC[:, 1024 * half:1024 * (half + 1)],
                        AF.Exp, bias=0.0, scale=0.125,
                        accum_out=COLS[:, 44 + 2 * par + half:45 + 2 * par + half]), "s")
                P.wait("vector", "s", exp_s[1])
                P.cop("vector", lambda e, par=par: e.tensor_tensor(
                    COLS[:, 51 + par:52 + par], COLS[:, 44 + 2 * par:45 + 2 * par],
                    COLS[:, 45 + 2 * par:46 + 2 * par], ALU.add))
                rv_v[par] = P.cop("vector", lambda e, par=par: e.reciprocal(
                    COLS[:, 51 + par:52 + par], COLS[:, 51 + par:52 + par]))
                if par in outmm_t:
                    P.wait("vector", "t", outmm_t[par])
                vs_v[jt] = P.cop("vector", lambda e, h=h, jt=jt, par=par: e.tensor_scalar(
                    VSC[:, 128 * par:128 * (par + 1)],
                    VB[:, (h * 16 + jt) * 128:(h * 16 + jt) * 128 + 128],
                    COLS[:, 51 + par:52 + par], None, ALU.mult))
                if jt >= 1:
                    emit_out(h, jt - 1)
            emit_out(h, NT - 1)
            P.wait("vector", "t", outmm_t[(NT - 1) % 2])
            evac_v = P.cop("vector", lambda e, h=h: e.tensor_tensor(
                XRES[:, 2048 * h:2048 * (h + 1)], XRES[:, 2048 * h:2048 * (h + 1)],
                OPS[:, 0:2048], ALU.add))



def l0_ffn(nc, P, XRES, w1t0, w2t0, n1w_bc, n2w_bc, actsb, R0, NMR0, R2, NMR2,
           MX0, COLS, ONESR, IDBF, SCRA, getc, XNT):
    SY, MXC = 0, 8
    CS, CLN, CRSQ, CMZ, CMXP, CF, CD2 = 16, 17, 18, 19, 20, 21, 22
    CDSC = 24
    CPM, CPR = 26, 27
    CW1A, CW1T, CW1NT = 53, 54, 55
    CW2A, CW2T, CW2NT = 56, 49, 50
    T2A, R2A, T1A, R1A = 57, 58, 59, 60
    CCW1, CCW2 = 46, 48
    CA, CB = 61, 62
    CONES = 63
    with (
        nc.sbuf_tensor("IW1", [128, ND * 4096], FP8) as IW1,
        nc.sbuf_tensor("IW2", [128, 32 * 1024], FP8) as IW2,
        nc.sbuf_tensor("N1W", [128, D], F32) as N1W,
        nc.sbuf_tensor("N2W", [128, 4096], F32) as N2W,
    ):
        n1ld = P.dma("g", lambda e: e.dma_start(out=N1W[:], in_=n1w_bc[:, :]))
        n2ld = P.dma("g", lambda e: e.dma_start(out=N2W[:], in_=n2w_bc[:, :]))
        asld = P.dma("y", lambda e: e.dma_start(out=COLS[:, CA:CB + 1], in_=actsb[:, :]))
        iw2_v = {}
        with (
            nc.sbuf_tensor("WR0", [128, 1024], F32) as WR0,
            nc.sbuf_tensor("WR1", [128, 1024], F32) as WR1,
            nc.sbuf_tensor("WR2", [128, 1024], F32) as WR2,
            nc.sbuf_tensor("WR3", [128, 1024], F32) as WR3,
            nc.sbuf_tensor("PRA", [128, 2048], BF16) as PRA,
            nc.sbuf_tensor("TILE_A", [128, 1024], BF16) as TILE_A,
            nc.psum_tensor("PT", [128, 1024], BF16) as PT,
            nc.psum_tensor("RPS", [128, 512], F32) as RPS,
        ):
            WY, WG_ = (WR0, WR1), (WR2, WR3)

            def w1piece(i):
                k, c = i // 4, i % 4
                return w1t0[ts(k), c * 1024:(c + 1) * 1024]

            def w2piece(i):
                return w2t0[ts(i), :]

            streams = (("y", WY, w1piece, R1A, T1A, 0),
                       ("g", WG_, w2piece, R2A, T2A, 1024))
            P.cop("vector", lambda e: e.memset(COLS[:, R1A:R1A + 1], 0.0))
            P.cop("vector", lambda e: e.memset(COLS[:, R2A:R2A + 1], 0.0))
            abs_s = {}
            acc_v = {}
            for i in range(32):
                for (qn, Wp, piece, runc, tmpc, po) in streams:
                    eng = "sync" if qn == "y" else "gpsimd"
                    buf = Wp[i % 2]
                    if i >= 2:
                        P.wait(eng, "s", abs_s[(qn, i - 2)])
                    tok = P.dma(qn, lambda e, buf=buf, src=piece(i): e.dma_start(
                        out=buf[:], in_=src))
                    P.wait_tok("scalar", tok)
                    if i >= 1:
                        P.wait("scalar", "v", acc_v[(qn, i - 1)])
                    abs_s[(qn, i)] = P.cop("scalar", lambda e, buf=buf, po=po, tmpc=tmpc:
                        e.activation(PRA[:, po:po + 1024], buf[:], AF.Abs, bias=0.0,
                                     scale=1.0, accum_out=COLS[:, tmpc:tmpc + 1]))
                    P.wait("vector", "s", abs_s[(qn, i)])
                    acc_v[(qn, i)] = P.cop("vector", lambda e, runc=runc, tmpc=tmpc:
                        e.tensor_tensor(
                            COLS[:, runc:runc + 1], COLS[:, runc:runc + 1],
                            COLS[:, tmpc:tmpc + 1], ALU.add))
            acc = P.cnt["v"]
            P.wait("tensor", "v", acc)
            P.inc("tensor", lambda e: e.matmul(
                RPS[0:1, 0:1], COLS[:, R1A:R1A + 1], COLS[:, CONES:CONES + 1],
                start=True, stop=True), "t")
            mmc = P.inc("tensor", lambda e: e.matmul(
                RPS[0:1, 1:2], COLS[:, R2A:R2A + 1], COLS[:, CONES:CONES + 1],
                start=True, stop=True), "t")
            P.wait("vector", "t", mmc)
            P.cop("vector", lambda e: e.tensor_copy(COLS[0:1, T1A:T1A + 1], RPS[0:1, 0:1]))
            ev = P.cop("vector", lambda e: e.tensor_copy(COLS[0:1, T2A:T2A + 1], RPS[0:1, 1:2]))
            P.wait("tensor", "v", ev)
            P.inc("tensor", lambda e: e.matmul(
                RPS[:, 0:1], ONESR[:], COLS[0:1, T1A:T1A + 1], start=True, stop=True), "t")
            bc = P.inc("tensor", lambda e: e.matmul(
                RPS[:, 1:2], ONESR[:], COLS[0:1, T2A:T2A + 1], start=True, stop=True), "t")
            P.wait("vector", "t", bc)
            numel = float(D * 4096)
            P.cop("vector", lambda e: e.tensor_scalar(
                COLS[:, CW1A:CW1A + 1], RPS[:, 0:1], 1.0 / numel, 1e-5, ALU.mult, ALU.max))
            P.cop("vector", lambda e: e.tensor_scalar(
                COLS[:, CW2A:CW2A + 1], RPS[:, 1:2], 1.0 / numel, 1e-5, ALU.mult, ALU.max))
            for (ca, ct, cnt_) in ((CW1A, CW1T, CW1NT), (CW2A, CW2T, CW2NT)):
                P.cop("vector", lambda e, ca=ca, ct=ct: e.tensor_scalar(
                    COLS[:, ct:ct + 1], COLS[:, ca:ca + 1], 0.5, None, ALU.mult))
                P.cop("vector", lambda e, ct=ct, cnt_=cnt_: e.tensor_scalar(
                    COLS[:, cnt_:cnt_ + 1], COLS[:, ct:ct + 1], -1.0, None, ALU.mult))
            P.cop("vector", lambda e: e.tensor_scalar(
                COLS[:, CCW1:CCW1 + 1], COLS[:, CW1A:CW1A + 1], -1.0 / 127.0, None, ALU.mult))
            thd = P.cop("vector", lambda e: e.tensor_scalar(
                COLS[:, CCW2:CCW2 + 1], COLS[:, CW2A:CW2A + 1], -1.0 / 127.0, None, ALU.mult))
            P.wait("vector", "v", thd)
            stt_v = {}
            for i in range(32):
                for (qn, Wp, piece, runc, tmpc, po) in streams:
                    eng = "sync" if qn == "y" else "gpsimd"
                    tcol = CW1T if qn == "y" else CW2T
                    ncol = CW1NT if qn == "y" else CW2NT
                    buf = Wp[i % 2]
                    if i >= 2:
                        P.wait(eng, "v", stt_v[(qn, i - 2)])
                    else:
                        P.wait(eng, "s", abs_s[(qn, 30 + i)])
                    tok = P.dma(qn, lambda e, buf=buf, src=piece(i): e.dma_start(
                        out=buf[:], in_=src))
                    P.wait_tok("vector", tok)
                    if qn == "y":
                        k, c = i // 4, i % 4
                        dst = IW1[:, k * 4096 + c * 1024: k * 4096 + (c + 1) * 1024]
                    else:
                        dst = IW2[:, i * 1024:(i + 1) * 1024]
                    P.cop("vector", lambda e, buf=buf, po=po, tcol=tcol: e.tensor_scalar(
                        PRA[:, po:po + 1024], buf[:], COLS[:, tcol:tcol + 1], None, ALU.is_ge))
                    stt_v[(qn, i)] = P.cop("vector", lambda e, buf=buf, po=po, dst=dst, ncol=ncol:
                        e.scalar_tensor_tensor(
                            dst, buf[:], COLS[:, ncol:ncol + 1], PRA[:, po:po + 1024],
                            ALU.is_le, ALU.subtract))
                    if qn == "g":
                        iw2_v[i] = stt_v[(qn, i)]
            iw1_v = stt_v[("y", 31)]
            def pt(k):
                return PT[:, ts(k)]
            for ti in range(NT):
                xn = P.cop("vector", lambda e, ti=ti: e.tensor_scalar(
                    TILE_A[:], XRES[:, ts(ti, 1024)], R0[:, ti:ti + 1],
                    NMR0[:, ti:ti + 1], ALU.mult, ALU.add))
                P.wait("tensor", "v", xn)
                for k in range(ND):
                    trd = P.inc("tensor", lambda e, k=k: e.transpose(
                        pt(k), TILE_A[:, ts(k)], IDBF[:]), "t")
                P.wait("vector", "t", trd)
                for k in range(ND):
                    P.cop("vector", lambda e, ti=ti, k=k: e.tensor_copy(
                        XNT[:, k * T + 128 * ti: k * T + 128 * (ti + 1)], pt(k)))
                P.wait("tensor", "v", P.cnt["v"])
        with (
            nc.sbuf_tensor("YB", [128, 4096], F32) as YB,
            nc.sbuf_tensor("IXT2", [128, 2048], BF16) as IXT2,
            nc.sbuf_tensor("TILE_I", [128, 512], I32) as TILE_I,
            nc.sbuf_tensor("IHT", [128, 512], BF16) as IHT,
            nc.sbuf_tensor("QCB", [128, 512], BF16) as QCB,
            nc.psum_tensor("HPS", [128, 1024], F32) as HPS,
            nc.psum_tensor("TPS", [128, 1024], BF16) as TPS,
            nc.psum_tensor("OPS2", [128, 2048], F32) as OPS2,
        ):
            trc = [0]
            tps_cons = {}

            def tslot():
                n = trc[0]
                trc[0] += 1
                return n

            def pslot(n):
                return TPS[:, ts(n % 8)]

            cast_v = {}
            s1_s = {}
            s3_s = {}
            prep_pb_v = {}
            prep_t = {}
            prepvb_v = {}
            mm1_last = {}
            mm2_t = {}
            tr_t = {}
            add_v = {}
            dsc_v = {}

            def emit_prep(tj):
                PB = YB[:, 0:1024]
                P.wait_tok("vector", n1ld)
                P.cop("vector", lambda e, tj=tj: e.tensor_scalar(
                    PB, XRES[:, ts(tj, 1024)], R2[:, tj:tj + 1],
                    NMR2[:, tj:tj + 1], ALU.mult, ALU.add))
                P.cop("vector", lambda e: e.tensor_tensor(PB, PB, N1W[:], ALU.mult))
                P.cop("vector", lambda e: e.tensor_reduce(
                    COLS[:, CPM:CPM + 1], PB, AX.X, ALU.max, apply_absolute_value=True))
                P.cop("vector", lambda e: e.tensor_scalar(
                    COLS[:, CPM:CPM + 1], COLS[:, CPM:CPM + 1], 1e-5, None, ALU.max))
                P.cop("vector", lambda e: e.reciprocal(COLS[:, CPR:CPR + 1], COLS[:, CPM:CPM + 1]))
                P.cop("vector", lambda e: e.tensor_scalar(
                    PB, PB, COLS[:, CPR:CPR + 1], 127.0, ALU.mult, ALU.mult))
                if tj >= 1:
                    P.wait("vector", "s", s3_s[(tj - 1, 3)])
                for q in range(2):
                    P.cop("vector", lambda e, q=q: e.tensor_copy(
                        TILE_I[:], PB[:, 512 * q:512 * (q + 1)]))
                    cp = P.cop("vector", lambda e, q=q: e.tensor_copy(
                        SCRA[:, 512 * q:512 * (q + 1)], TILE_I[:]))
                prep_pb_v[tj] = cp
                P.wait("tensor", "v", cp)
                if tj >= 2:
                    P.wait("tensor", "t", mm1_last[tj - 2])
                for k in range(ND):
                    n = tslot()
                    if n >= 8:
                        P.wait("tensor", "v", tps_cons[n - 8])
                    trd = P.inc("tensor", lambda e, n=n, k=k: e.transpose(
                        pslot(n), SCRA[:, ts(k)], IDBF[:]), "t")
                prep_t[tj] = trd
                base = trc[0] - 8
                P.wait("vector", "t", trd)
                for g2 in range(2):
                    cc = P.cop("vector", lambda e, tj=tj, g2=g2, base=base: e.tensor_copy(
                        IXT2[:, (tj % 2) * 1024 + 512 * g2:(tj % 2) * 1024 + 512 * (g2 + 1)],
                        TPS[:, ((base + 4 * g2) % 8) * 128:((base + 4 * g2) % 8) * 128 + 512]))
                    for n in range(base + 4 * g2, base + 4 * g2 + 4):
                        tps_cons[n] = cc
                prepvb_v[tj] = cc
                if tj >= 2:
                    P.wait("vector", "s", s1_s[(tj - 2, 3)])
                dsc_v[tj] = P.cop("vector", lambda e, tj=tj: e.tensor_tensor(
                    COLS[:, CDSC + tj % 2:CDSC + tj % 2 + 1], COLS[:, CCW1:CCW1 + 1],
                    COLS[:, CPM:CPM + 1], ALU.mult))

            emit_prep(0)
            for ti in range(NT):
                for hp in range(4):
                    for hc in (2 * hp, 2 * hp + 1):
                        P.wait("tensor", "v", prepvb_v[ti])
                        if ti == 0 and hc == 0:
                            P.wait("tensor", "v", iw1_v)
                        if hp >= 1:
                            P.wait("tensor", "s", s1_s[(ti, hp - 1)])
                        elif ti > 0:
                            P.wait("tensor", "s", s1_s[(ti - 1, 3)])
                        for k in range(ND):
                            mm = P.inc("tensor", lambda e, ti=ti, hc=hc, k=k: e.matmul(
                                HPS[:, ts(hc % 2, 512)],
                                IXT2[:, (ti % 2) * 1024 + 128 * k:(ti % 2) * 1024 + 128 * (k + 1)],
                                IW1[:, k * 4096 + 512 * hc:k * 4096 + 512 * (hc + 1)],
                                start=(k == 0), stop=(k == ND - 1)), "t")
                    mm1_last[ti] = mm
                    YBc = YB[:, 1024 * hp:1024 * (hp + 1)]
                    P.wait("scalar", "t", mm)
                    P.wait("scalar", "v", dsc_v[ti])
                    if ti > 0:
                        P.wait("scalar", "v", cast_v[(ti - 1, 2 * hp + 1)])
                    if hp == 0:
                        P.wait("scalar", "v", prep_pb_v[ti])
                        P.wait("scalar", "t", prep_t[ti])
                    s1_s[(ti, hp)] = P.cop("scalar", lambda e, ti=ti, YBc=YBc:
                        e.activation(YBc, HPS[:, 0:1024], AF.Relu, bias=0.0,
                                     scale=COLS[:, CDSC + ti % 2:CDSC + ti % 2 + 1]))
                    sq2 = P.cop("scalar", lambda e, YBc=YBc: e.activation(
                        YBc, YBc, AF.Square, bias=0.0, scale=1.0))
                    P.wait("vector", "s", sq2)
                    P.wait_tok("vector", asld)
                    v1 = P.cop("vector", lambda e, YBc=YBc: e.tensor_scalar(
                        YBc, YBc, COLS[:, CA:CA + 1], COLS[:, CB:CB + 1], ALU.mult, ALU.add))
                    P.wait("scalar", "v", v1)
                    s3_s[(ti, hp)] = P.cop("scalar", lambda e, hp=hp, YBc=YBc: e.activation(
                        SCRA[:], YBc, AF.Square, bias=0.0, scale=1.0,
                        accum_out=COLS[:, SY + hp:SY + hp + 1]))
                    P.wait("vector", "s", s3_s[(ti, hp)])
                    P.wait_tok("vector", n2ld)
                    P.cop("vector", lambda e, hp=hp, YBc=YBc: e.tensor_tensor(
                        YBc, YBc, N2W[:, 1024 * hp:1024 * (hp + 1)], ALU.mult))
                    P.cop("vector", lambda e, hp=hp, YBc=YBc: e.tensor_reduce(
                        COLS[:, MXC + hp:MXC + hp + 1], YBc, AX.X, ALU.max,
                        apply_absolute_value=True))
                sv = P.cop("vector", lambda e: e.tensor_reduce(
                    COLS[:, CS:CS + 1], COLS[:, SY:SY + 4], AX.X, ALU.add))
                P.cop("vector", lambda e: e.tensor_reduce(
                    COLS[:, CMZ:CMZ + 1], COLS[:, MXC:MXC + 4], AX.X, ALU.max))
                P.wait("scalar", "v", sv)
                P.cop("scalar", lambda e: e.activation(
                    COLS[:, CLN:CLN + 1], COLS[:, CS:CS + 1], AF.Ln,
                    bias=getc(4096.0 * 1e-8), scale=1.0))
                rq = P.cop("scalar", lambda e: e.activation(
                    COLS[:, CRSQ:CRSQ + 1], COLS[:, CLN:CLN + 1], AF.Exp,
                    bias=getc(float(0.5 * np.log(4096.0))), scale=-0.5))
                P.wait("vector", "s", rq)
                P.cop("vector", lambda e: e.tensor_tensor(
                    COLS[:, CMXP:CMXP + 1], COLS[:, CRSQ:CRSQ + 1],
                    COLS[:, CMZ:CMZ + 1], ALU.mult))
                P.cop("vector", lambda e: e.tensor_scalar(
                    COLS[:, CMXP:CMXP + 1], COLS[:, CMXP:CMXP + 1], 1e-5, None, ALU.max))
                P.cop("vector", lambda e: e.reciprocal(COLS[:, CF:CF + 1], COLS[:, CMXP:CMXP + 1]))
                P.cop("vector", lambda e: e.tensor_scalar(
                    COLS[:, CF:CF + 1], COLS[:, CF:CF + 1],
                    COLS[:, CRSQ:CRSQ + 1], 127.0, ALU.mult, ALU.mult))
                P.cop("vector", lambda e: e.tensor_tensor(
                    COLS[:, CD2:CD2 + 1], COLS[:, CCW2:CCW2 + 1],
                    COLS[:, CMXP:CMXP + 1], ALU.mult))
                for c8 in range(8):
                    YBc = YB[:, ts(c8, 512)]
                    P.cop("vector", lambda e, YBc=YBc: e.tensor_scalar(
                        YBc, YBc, COLS[:, CF:CF + 1], None, ALU.mult))
                    if c8 >= 1:
                        P.wait("vector", "t", tr_t[(ti, c8 - 1)])
                    P.cop("vector", lambda e, YBc=YBc: e.tensor_copy(TILE_I[:], YBc))
                    ct = P.cop("vector", lambda e: e.tensor_copy(QCB[:], TILE_I[:]))
                    cast_v[(ti, c8)] = ct
                    P.wait("tensor", "v", ct)
                    base = trc[0]
                    for k4 in range(4):
                        n = tslot()
                        if n >= 8:
                            P.wait("tensor", "v", tps_cons[n - 8])
                        trd = P.inc("tensor", lambda e, n=n, k4=k4: e.transpose(
                            pslot(n), QCB[:, ts(k4)], IDBF[:]), "t")
                    tr_t[(ti, c8)] = trd
                    P.wait("vector", "t", trd)
                    if c8 >= 1:
                        P.wait("vector", "t", mm2_t[(ti, c8 - 1)])
                    cc = P.cop("vector", lambda e, base=base: e.tensor_copy(
                        IHT[:], TPS[:, (base % 8) * 128:(base % 8) * 128 + 512]))
                    for n in range(base, base + 4):
                        tps_cons[n] = cc
                    P.wait("tensor", "v", cc)
                    if ti == 0:
                        P.wait("tensor", "v", iw2_v[4 * c8 + 3])
                    if ti >= 2 and c8 == 0:
                        P.wait("tensor", "v", add_v[ti - 2])
                    for k4 in range(4):
                        k = 4 * c8 + k4
                        for nn2 in range(2):
                            mm2 = P.inc("tensor", lambda e, ti=ti, k=k, k4=k4, nn2=nn2: e.matmul(
                                OPS2[:, ts((ti % 2) * 2 + nn2, 512)],
                                IHT[:, ts(k4)],
                                IW2[:, k * 1024 + 512 * nn2:k * 1024 + 512 * (nn2 + 1)],
                                start=(k == 0), stop=(k == 31)), "t")
                    mm2_t[(ti, c8)] = mm2
                if ti + 1 < NT:
                    emit_prep(ti + 1)
                P.wait("vector", "t", mm2_t[(ti, 7)])
                add_v[ti] = P.cop("vector", lambda e, ti=ti: e.scalar_tensor_tensor(
                    XRES[:, ts(ti, 1024)],
                    OPS2[:, (ti % 2) * 1024:(ti % 2) * 1024 + 1024],
                    COLS[:, CD2:CD2 + 1], XRES[:, ts(ti, 1024)], ALU.mult, ALU.add))



def l1_attn(nc, P, XRES, R1, NMR1, COLS, ST, IDBF, ONESR, c_ident, c_triu,
            lnw_bc, lnb_bc, wq1, wk1, wv1, wg1, gkw1, gkw2, gkb2_bc, rmsw_bc,
            wo1, wmlp1, bmlp1, wmlp2, bmlp2_bc, SCRA, getc, x2d, xn1td):
    HEADS = 16
    with (
        nc.sbuf_tensor("HT", [128, ND * T], BF16) as HT,
        nc.sbuf_tensor("TRIUF", [128, 128], F32) as TRIUF,
        nc.sbuf_tensor("MSK4", [128, 512], BF16) as MSK4,
        nc.sbuf_tensor("LNW", [128, D], BF16) as LNW,
        nc.sbuf_tensor("LNB", [128, D], BF16) as LNB,
        nc.sbuf_tensor("LNWI", [128, D], BF16) as LNWI,
        nc.sbuf_tensor("GK2S", [16, 512], BF16) as GK2S,
        nc.sbuf_tensor("GK1S", [128, ND * 16], BF16) as GK1S,
    ):
        trl = P.dma("y", lambda e: e.dma_start(out=TRIUF[:], in_=c_triu.ap()[:, :]))
        for q4 in range(4):
            mkl = P.dma("g", lambda e, q4=q4: e.dma_start(
                out=MSK4[:, 128 * q4:128 * (q4 + 1)], in_=c_triu.ap()[:, :]))
        lwl = P.dma("g", lambda e: e.dma_start(out=LNW[:], in_=lnw_bc[:, :]))
        lbl = P.dma("g", lambda e: e.dma_start(out=LNB[:], in_=lnb_bc[:, :]))
        g2l = P.dma("g", lambda e: e.dma_start(out=GK2S[:], in_=gkw2[:, :]))
        g1l = P.dma("g", lambda e: e.dma_start(
            out=GK1S[:].rearrange("p (k o) -> p k o", k=ND),
            in_=gkw1.rearrange("(k p) o -> p k o", p=128)))
        for tok in (lwl, lbl, mkl, g2l):
            P.wait_tok("vector", tok)
        P.wait("vector", "v", None)

        def _recip_lnw(e, nc=nc):
            with nc.allow_low_precision("bf16 reciprocal of ln gamma (xn1 recovery)"):
                return e.reciprocal(LNWI[:], LNW[:])
        lwi = P.cop("vector", _recip_lnw)

        es = contextlib.ExitStack()
        WQ = es.enter_context(nc.sbuf_tensor("WQ", [128, ND * 512], BF16))
        WK = es.enter_context(nc.sbuf_tensor("WK", [128, ND * 512], BF16))
        WV = es.enter_context(nc.sbuf_tensor("WV", [128, ND * 1024], BF16))
        WG = es.enter_context(nc.sbuf_tensor("WG", [128, ND * 1024], BF16))
        WO = es.enter_context(nc.sbuf_tensor("WO", [128, ND * 1024], BF16))
        GKB = es.enter_context(nc.sbuf_tensor("GKB", [128, 512], BF16))
        RMSW = es.enter_context(nc.sbuf_tensor("RMSW", [128, D], BF16))
        QC = es.enter_context(nc.sbuf_tensor("QC", [128, 1024], BF16))
        KC = es.enter_context(nc.sbuf_tensor("KC", [128, 1024], BF16))
        VC = es.enter_context(nc.sbuf_tensor("VC", [128, 2048], BF16))
        GC = es.enter_context(nc.sbuf_tensor("GC", [128, 2048], BF16))
        SPF = es.enter_context(nc.sbuf_tensor("SPF", [128, 512], F32))
        ASB = es.enter_context(nc.sbuf_tensor("ASB1", [128, 512], F32))
        E1 = es.enter_context(nc.sbuf_tensor("E1", [128, 512], BF16))
        E2 = es.enter_context(nc.sbuf_tensor("E2", [128, 512], BF16))
        ALC = es.enter_context(nc.sbuf_tensor("ALC", [128, 4], F32))
        EALC = es.enter_context(nc.sbuf_tensor("EALC", [128, 8], F32))
        QPT = es.enter_context(nc.sbuf_tensor("QPT", [128, 1024], BF16))
        KPT = es.enter_context(nc.sbuf_tensor("KPT", [128, 1024], BF16))
        Z1C = es.enter_context(nc.sbuf_tensor("Z1C", [128, 16], BF16))
        Z1TC = es.enter_context(nc.sbuf_tensor("Z1TC", [16, 128], BF16))
        PB2 = es.enter_context(nc.sbuf_tensor("PB2", [128, 512], BF16))
        OG = es.enter_context(nc.sbuf_tensor("OG", [128, 1024], BF16))
        OGT = es.enter_context(nc.sbuf_tensor("OGT", [128, 1024], BF16))
        SSB = es.enter_context(nc.sbuf_tensor("SSB", [128, 256], F32))
        SBB = es.enter_context(nc.sbuf_tensor("SBB", [128, 256], BF16))
        OCOL = es.enter_context(nc.sbuf_tensor("OCOL", [128, 16], F32))
        INVC = es.enter_context(nc.sbuf_tensor("INVC", [128, 16], F32))
        SQB = es.enter_context(nc.sbuf_tensor("SQB", [128, 1024], BF16))
        PRJ = es.enter_context(nc.psum_tensor("PRJ", [128, 1024], F32))
        TPB = es.enter_context(nc.psum_tensor("TPB", [128, 1024], BF16))
        PPS = es.enter_context(nc.psum_tensor("PPS", [128, 512], F32))
        OP2 = es.enter_context(nc.psum_tensor("OP2", [128, 1024], F32))
        XP = es.enter_context(nc.psum_tensor("XP", [128, 1024], F32))
        for wsrc, wdst in ((wq1, WQ), (wk1, WK), (wv1, WV), (wg1, WG), (wo1, WO)):
            P.dma("g", lambda e, wsrc=wsrc, wdst=wdst: e.dma_start(
                out=wdst[:].rearrange("p (k o) -> p k o", k=ND),
                in_=wsrc.rearrange("(k p) o -> p k o", p=128)))
        P.dma("g", lambda e: e.dma_start(out=GKB[:], in_=gkb2_bc[:, :]))
        P.dma("g", lambda e: e.dma_start(out=RMSW[:], in_=rmsw_bc[:, :]))

        btr = [0]
        bcons = {}

        def bslot():
            n = btr[0]
            btr[0] += 1
            return n

        tb_t = {}
        tc_tok = {}
        for ti in range(NT):
            par = ti % 2
            TAb = OG if par == 0 else OGT
            TBb = QC if par == 0 else KC
            TCb = VC[:, 1024 * par:1024 * (par + 1)]
            if par in tb_t:
                P.wait("vector", "t", tb_t[par])
            P.cop("vector", lambda e, ti=ti, TAb=TAb: e.tensor_scalar(
                TAb[:], XRES[:, ts(ti, 1024)], R1[:, ti:ti + 1],
                NMR1[:, ti:ti + 1], ALU.mult, ALU.add))
            P.cop("vector", lambda e, TAb=TAb, TBb=TBb: e.tensor_tensor(
                TBb[:], TAb[:], LNW[:], ALU.mult))
            hb = P.cop("vector", lambda e, TBb=TBb: e.tensor_tensor(
                TBb[:], TBb[:], LNB[:], ALU.add))
            P.wait("tensor", "v", hb)
            if btr[0] > 0:
                P.wait("tensor", "v", bcons[btr[0] - 1])
            for k in range(ND):
                n = bslot()
                trd = P.inc("tensor", lambda e, n=n, k=k, TAb=TAb: e.transpose(
                    TPB[:, ts(n % 8)], TAb[:, ts(k)], IDBF[:]), "t")
            base = btr[0] - 8
            P.wait("vector", "t", trd)
            if par in tc_tok:
                P.wait_tok("vector", tc_tok[par])
            for g2 in range(2):
                cc = P.cop("vector", lambda e, TCb=TCb, g2=g2, base=base: e.tensor_copy(
                    TCb[:, 512 * g2:512 * (g2 + 1)], TPSLICE(TPB, base + 4 * g2)))
                for n in range(base + 4 * g2, base + 4 * g2 + 4):
                    bcons[n] = cc
            P.wait("sync", "v", cc)
            tc_tok[par] = P.dma("y", lambda e, ti=ti, TCb=TCb: e.dma_start(
                out=xn1td.rearrange("p (k t) -> p k t", k=ND)[:, :, 128 * ti:128 * (ti + 1)],
                in_=TCb.rearrange("p (k t) -> p k t", k=ND)))
            if btr[0] > 0:
                P.wait("tensor", "v", bcons[btr[0] - 1])
            for k in range(ND):
                n = bslot()
                trd = P.inc("tensor", lambda e, n=n, k=k, TBb=TBb: e.transpose(
                    TPB[:, ts(n % 8)], TBb[:, ts(k)], IDBF[:]), "t")
            tb_t[par] = trd
            base = btr[0] - 8
            P.wait("vector", "t", trd)
            for k in range(ND):
                cc = P.cop("vector", lambda e, ti=ti, k=k, base=base: e.tensor_copy(
                    HT[:, k * T + 128 * ti:k * T + 128 * (ti + 1)],
                    TPB[:, ((base + k) % 8) * 128:((base + k) % 8) * 128 + 128]))
                bcons[base + k] = cc

        qk_v = {}
        ealc_s = {}
        vc_v = {}
        gc_v = {}
        sbb_v = {}
        st_t = {}
        og_v = {}
        sq_s = {}
        oc_v = {}
        x2add_v = {}
        x2_tok = {}
        prj_free_v = [0]

        def emit_prep(c):
            par = c % 2
            qo, vo = 512 * par, 1024 * par

            def hsl(k, c=c):
                return HT[:, k * T + 128 * c:k * T + 128 * (c + 1)]
            P.wait("tensor", "v", prj_free_v[0])
            if c >= 2:
                P.wait("tensor", "t", st_t[c - 2])
            for p0, wbuf in ((0, WQ), (512, WK)):
                for k in range(ND):
                    mm = P.inc("tensor", lambda e, wbuf=wbuf, p0=p0, k=k, hk=hsl(k): e.matmul(
                        PRJ[:, p0:p0 + 512], hk,
                        wbuf[:, 512 * k:512 * (k + 1)],
                        start=(k == 0), stop=(k == ND - 1)), "t")
            P.wait("vector", "t", mm)
            P.cop("vector", lambda e, qo=qo: e.tensor_copy(QC[:, qo:qo + 512], PRJ[:, 0:512]))
            ev = P.cop("vector", lambda e, qo=qo: e.tensor_copy(KC[:, qo:qo + 512], PRJ[:, 512:1024]))
            P.wait("tensor", "v", ev)
            for k in range(ND):
                for nn in range(2):
                    mm = P.inc("tensor", lambda e, k=k, nn=nn, hk=hsl(k): e.matmul(
                        PRJ[:, ts(nn, 512)], hk,
                        WV[:, 1024 * k + 512 * nn:1024 * k + 512 * (nn + 1)],
                        start=(k == 0), stop=(k == ND - 1)), "t")
            P.wait("vector", "t", mm)
            if c < 2 and par in tc_tok:
                P.wait_tok("vector", tc_tok[par])
            ev = P.cop("vector", lambda e, vo=vo: e.tensor_copy(VC[:, vo:vo + 1024], PRJ[:, 0:1024]))
            vc_v[c] = ev
            P.wait("tensor", "v", ev)
            for k in range(ND):
                for nn in range(2):
                    mm = P.inc("tensor", lambda e, k=k, nn=nn, hk=hsl(k): e.matmul(
                        PRJ[:, ts(nn, 512)], hk,
                        WG[:, 1024 * k + 512 * nn:1024 * k + 512 * (nn + 1)],
                        start=(k == 0), stop=(k == ND - 1)), "t")
            P.wait("scalar", "t", mm)
            if c >= 2:
                P.wait("scalar", "v", oc_v[c - 2])
            P.wait("scalar", "s", P.cnt["s"])
            P.inc("scalar", lambda e: e.activation(
                SQB[:], PRJ[:, 0:1024], AF.Exp, bias=0.0, scale=-1.0), "s")
            P.wait("scalar", "s", P.cnt["s"])
            P.inc("scalar", lambda e: e.activation(
                SQB[:], SQB[:], AF.Ln, bias=getc(1.0), scale=1.0), "s")
            P.wait("scalar", "s", P.cnt["s"])
            sgd = P.inc("scalar", lambda e: e.activation(
                SQB[:], SQB[:], AF.Exp, bias=0.0, scale=-1.0), "s")
            P.wait("vector", "s", sgd)
            P.cop("vector", lambda e, vo=vo: e.tensor_tensor(
                GC[:, vo:vo + 1024], PRJ[:, 0:1024], SQB[:], ALU.mult))
            ev = P.cop("vector", lambda e, vo=vo: e.tensor_tensor(
                GC[:, vo:vo + 1024], GC[:, vo:vo + 1024], RMSW[:], ALU.mult))
            gc_v[c] = ev
            P.wait("tensor", "v", ev)
            for k in range(ND):
                mm = P.inc("tensor", lambda e, k=k, hk=hsl(k): e.matmul(
                    PRJ[:, 0:16], hk, GK1S[:, 16 * k:16 * (k + 1)],
                    start=(k == 0), stop=(k == ND - 1)), "t")
            P.wait("vector", "t", mm)
            zv = P.cop("vector", lambda e: e.tensor_copy(Z1C[:], PRJ[:, 0:16]))
            P.wait("tensor", "v", zv)
            zbase = btr[0]
            if btr[0] > 0:
                P.wait("tensor", "v", bcons[btr[0] - 1])
            for _ in range(4):
                n = bslot()
            trd = P.inc("tensor", lambda e, zbase=zbase: e.transpose(
                TPB[0:16, (zbase % 8) * 128:(zbase % 8) * 128 + 128], Z1C[:], IDBF[:]), "t")
            P.wait("vector", "t", trd)
            ztc = P.cop("vector", lambda e, zbase=zbase: e.tensor_copy(
                Z1TC[:], TPB[0:16, (zbase % 8) * 128:(zbase % 8) * 128 + 128]))
            for n in range(zbase, zbase + 4):
                bcons[n] = ztc
            P.wait("tensor", "v", ztc)
            mm = P.inc("tensor", lambda e: e.matmul(
                PRJ[:, 0:512], Z1TC[:], GK2S[:], start=True, stop=True), "t")
            P.wait("vector", "t", mm)
            zb = P.cop("vector", lambda e: e.tensor_tensor(
                SPF[:], PRJ[:, 0:512], GKB[:], ALU.add))
            P.wait("scalar", "v", zb)
            P.wait("scalar", "s", P.cnt["s"])
            P.inc("scalar", lambda e: e.activation(SPF[:], SPF[:], AF.Exp, bias=0.0, scale=-1.0), "s")
            P.wait("scalar", "s", P.cnt["s"])
            spd = P.inc("scalar", lambda e: e.activation(
                SPF[:], SPF[:], AF.Ln, bias=getc(1.0), scale=1.0), "s")
            P.wait("tensor", "s", spd)
            mm = P.inc("tensor", lambda e: e.matmul(
                PRJ[:, 0:512], TRIUF[:], SPF[:], start=True, stop=True), "t")
            P.wait("vector", "t", mm)
            asd = P.cop("vector", lambda e: e.tensor_copy(ASB[:], PRJ[:, 0:512]))
            P.wait("scalar", "v", asd)
            P.wait("scalar", "s", P.cnt["s"])
            P.inc("scalar", lambda e: e.activation(
                E1[:], ASB[:], AF.Exp, bias=getc(float(-0.5 * np.log(32.0))),
                scale=-1.0 / 16.0), "s")
            P.wait("scalar", "s", P.cnt["s"])
            e2d = P.inc("scalar", lambda e: e.activation(
                E2[:], ASB[:], AF.Exp, bias=0.0, scale=1.0 / 16.0), "s")
            P.wait("tensor", "v", asd)
            for b in range(4):
                alm = P.inc("tensor", lambda e, b=b: e.matmul(
                    PRJ[:, 512 + b:513 + b], SPF[:, ts(b)], COLS[:, 63:64],
                    start=True, stop=True), "t")
            P.wait("vector", "t", alm)
            P.cop("vector", lambda e: e.tensor_copy(ALC[:, 0:4], PRJ[:, 512:516]))
            alc = P.cnt["v"]
            prj_free_v[0] = alc
            P.wait("scalar", "v", alc)
            if c >= 2:
                P.wait("scalar", "v", sbb_v[(c - 2, 3)])
            P.wait("scalar", "s", P.cnt["s"])
            ealc_s[c] = P.inc("scalar", lambda e, par=par: e.activation(
                EALC[:, 4 * par:4 * (par + 1)], ALC[:], AF.Exp, bias=0.0,
                scale=-1.0 / 16.0), "s")
            P.wait("vector", "s", e2d)
            P.cop("vector", lambda e, qo=qo: e.tensor_tensor(
                QC[:, qo:qo + 512], QC[:, qo:qo + 512], E1[:], ALU.mult))
            qpd = P.cop("vector", lambda e, qo=qo: e.tensor_tensor(
                KC[:, qo:qo + 512], KC[:, qo:qo + 512], E2[:], ALU.mult))
            P.wait("tensor", "v", qpd)
            for src_, dst in ((QC, QPT), (KC, KPT)):
                if btr[0] > 0:
                    P.wait("tensor", "v", bcons[btr[0] - 1])
                base = btr[0]
                for b in range(4):
                    n = bslot()
                    trd = P.inc("tensor", lambda e, n=n, b=b, src_=src_, qo=qo: e.transpose(
                        TPB[:, ts(n % 8)], src_[:, qo + 128 * b:qo + 128 * (b + 1)],
                        IDBF[:]), "t")
                P.wait("vector", "t", trd)
                cc = P.cop("vector", lambda e, dst=dst, qo=qo, base=base: e.tensor_copy(
                    dst[:, qo:qo + 512], TPSLICE(TPB, base)))
                for n in range(base, base + 4):
                    bcons[n] = cc
            qk_v[c] = cc

        def emit_state(c):
            par = c % 2
            qo, vo = 512 * par, 1024 * par
            P.wait("tensor", "v", qk_v[c])
            P.wait("tensor", "v", vc_v[c])
            if c >= 1:
                P.wait("tensor", "s", sq_s[c - 1])
                P.wait("tensor", "v", og_v[c - 1])
                P.wait("tensor", "v", x2add_v[c - 1])
            def emit_pt(g, m):
                r0 = 32 * m
                cb = 128 * g
                return P.inc("tensor", lambda e, r0=r0, cb=cb, qo=qo, m=m: e.matmul(
                    PPS[:, 128 * m:128 * (m + 1)],
                    KPT[r0:r0 + 32, qo + cb:qo + cb + 128],
                    QPT[r0:r0 + 32, qo + cb:qo + cb + 128],
                    start=True, stop=True, tile_position=(r0, 0)), "t")

            for m in range(4):
                if m > 0:
                    P.wait("tensor", "t", P.cnt["t"])
                pt_mm = emit_pt(0, m)
            for g in range(4):
                P.wait("vector", "t", pt_mm)
                if c == 0 and g == 0:
                    P.wait("vector", "v", sbb_init)
                msk = P.cop("vector", lambda e: e.tensor_tensor(
                    PB2[:], PPS[:, 0:512], MSK4[:], ALU.mult))
                P.wait("tensor", "v", msk)
                if c >= 1:
                    P.wait("tensor", "v", sbb_v[(c - 1, g)])
                for m in range(4):
                    h = 4 * g + m
                    r0 = 32 * m
                    P.inc("tensor", lambda e, h=h, m=m, vo=vo: e.matmul(
                        OP2[:, 64 * h:64 * h + 64], PB2[:, 128 * m:128 * (m + 1)],
                        VC[:, vo + 64 * h:vo + 64 * h + 64],
                        start=(h % 8 == 0), stop=False), "t")
                    P.inc("tensor", lambda e, h=h, r0=r0, g=g, qo=qo: e.matmul(
                        OP2[:, 64 * h:64 * h + 64],
                        QPT[r0:r0 + 32, qo + 128 * g:qo + 128 * (g + 1)],
                        SBB[r0:r0 + 32, 64 * g:64 * g + 64],
                        start=False, stop=(h % 8 == 7), tile_position=(r0, 0)), "t")
                    if g + 1 < 4:
                        pt_mm = emit_pt(g + 1, m)
                for m in range(4):
                    h = 4 * g + m
                    r0 = 32 * m
                    ds = P.inc("tensor", lambda e, h=h, r0=r0, g=g, qo=qo, vo=vo: e.matmul(
                        XP[r0:r0 + 32, 64 * g:64 * g + 64],
                        KC[:, qo + 32 * h:qo + 32 * h + 32],
                        VC[:, vo + 64 * h:vo + 64 * h + 64],
                        start=True, stop=True, tile_position=(0, r0)), "t")
                st_t[c] = ds
            P.wait("vector", "t", st_t[c])
            P.wait("vector", "s", ealc_s[c])
            P.cop("vector", lambda e: e.tensor_tensor(
                SSB[:], SSB[:], XP[:, 0:256], ALU.add))
            for g in range(4):
                P.cop("vector", lambda e, g=g, par=par: e.tensor_scalar(
                    SSB[:, 64 * g:64 * g + 64], SSB[:, 64 * g:64 * g + 64],
                    EALC[:, 4 * par + g:4 * par + g + 1], None, ALU.mult))
            cpall = P.cop("vector", lambda e: e.tensor_copy(SBB[:], SSB[:]))
            for g in range(4):
                sbb_v[(c, g)] = cpall

        def emit_epi(c):
            par = c % 2
            vo = 1024 * par
            P.wait("scalar", "t", st_t[c])
            P.wait("scalar", "v", gc_v[min(c + 1, NT - 1)])
            P.wait("scalar", "s", P.cnt["s"])
            sq_s[c] = P.inc("scalar", lambda e: e.activation(
                SQB[:], OP2[:, 0:1024], AF.Square, bias=0.0, scale=1.0), "s")
            P.wait("vector", "s", sq_s[c])
            for h in range(HEADS):
                P.cop("vector", lambda e, h=h: e.tensor_reduce(
                    OCOL[:, h:h + 1], SQB[:, 64 * h:64 * (h + 1)], AX.X, ALU.add))
            oc = P.cnt["v"]
            oc_v[c] = oc
            P.wait("scalar", "v", oc)
            P.wait("scalar", "s", P.cnt["s"])
            P.inc("scalar", lambda e: e.activation(
                INVC[:], OCOL[:], AF.Ln, bias=getc(1e-5), scale=1.0 / 64.0), "s")
            P.wait("scalar", "s", P.cnt["s"])
            ivd = P.inc("scalar", lambda e: e.activation(
                INVC[:], INVC[:], AF.Exp, bias=0.0, scale=-0.5), "s")
            P.wait("vector", "s", ivd)
            for pp in (0, 1):
                if pp in x2_tok:
                    P.wait_tok("vector", x2_tok[pp])
            for h in range(HEADS):
                P.cop("vector", lambda e, h=h, vo=vo: e.scalar_tensor_tensor(
                    OG[:, 64 * h:64 * (h + 1)], OP2[:, 64 * h:64 * (h + 1)],
                    INVC[:, h:h + 1], GC[:, vo + 64 * h:vo + 64 * (h + 1)],
                    ALU.mult, ALU.mult))
            og_v[c] = P.cnt["v"]
            P.wait("tensor", "v", og_v[c])
            if btr[0] > 0:
                P.wait("tensor", "v", bcons[btr[0] - 1])
            base0 = btr[0]
            for b in range(ND):
                n = bslot()
                trd = P.inc("tensor", lambda e, n=n, b=b: e.transpose(
                    TPB[:, ts(n % 8)], OG[:, ts(b)], IDBF[:]), "t")
            P.wait("vector", "t", trd)
            for g2 in range(2):
                cc = P.cop("vector", lambda e, g2=g2, base0=base0: e.tensor_copy(
                    OGT[:, 512 * g2:512 * (g2 + 1)], TPSLICE(TPB, base0 + 4 * g2)))
                for n in range(base0 + 4 * g2, base0 + 4 * g2 + 4):
                    bcons[n] = cc
            P.wait("tensor", "v", cc)
            if c >= 1:
                P.wait("tensor", "v", x2add_v[c - 1])
            for k in range(ND):
                for nn in range(2):
                    wo_mm = P.inc("tensor", lambda e, k=k, nn=nn: e.matmul(
                        XP[:, ts(nn, 512)], OGT[:, ts(k)],
                        WO[:, 1024 * k + 512 * nn:1024 * k + 512 * (nn + 1)],
                        start=(k == 0), stop=(k == ND - 1)), "t")
            if btr[0] > 0:
                P.wait("tensor", "v", bcons[btr[0] - 1])
            base1 = btr[0]
            for k in range(ND):
                n = bslot()
                trd = P.inc("tensor", lambda e, n=n, k=k, c=c: e.transpose(
                    TPB[:, ts(n % 8)], HT[:, k * T + 128 * c:k * T + 128 * (c + 1)],
                    IDBF[:]), "t")
            P.wait("vector", "t", trd)
            P.wait("vector", "t", wo_mm)
            P.wait("vector", "s", ealc_s[min(c + 1, NT - 1)])
            for g2 in range(2):
                cc = P.cop("vector", lambda e, g2=g2, base1=base1: e.tensor_copy(
                    OGT[:, 512 * g2:512 * (g2 + 1)], TPSLICE(TPB, base1 + 4 * g2)))
                for n in range(base1 + 4 * g2, base1 + 4 * g2 + 4):
                    bcons[n] = cc
            for BUFH, ho in ((SPF, 0), (ASB, 512)):
                P.cop("vector", lambda e, BUFH=BUFH, ho=ho: e.tensor_tensor(
                    BUFH[:], OGT[:, ho:ho + 512], LNB[:, ho:ho + 512], ALU.subtract))
                P.cop("vector", lambda e, BUFH=BUFH, ho=ho: e.tensor_tensor(
                    BUFH[:], BUFH[:], LNWI[:, ho:ho + 512], ALU.mult))
                P.cop("vector", lambda e, BUFH=BUFH, ho=ho: e.tensor_tensor(
                    BUFH[:], BUFH[:], XP[:, ho:ho + 512], ALU.add))
                x2add_v[c] = P.cop("vector", lambda e, BUFH=BUFH, ho=ho, c=c: e.tensor_tensor(
                    XRES[:, 1024 * c + ho:1024 * c + ho + 512],
                    XRES[:, 1024 * c + ho:1024 * c + ho + 512], BUFH[:], ALU.add))
                x2b = P.cop("vector", lambda e, BUFH=BUFH, ho=ho: e.tensor_copy(
                    OG[:, ho:ho + 512], BUFH[:]))
            P.wait("sync", "v", x2b)
            x2_tok[par] = P.dma("y", lambda e, c=c: e.dma_start(
                out=x2d[ts(c), :], in_=OG[:]))

        P.wait_tok("tensor", g1l)
        P.wait_tok("tensor", trl)
        P.dma_barrier("tensor")
        P.dma_barrier("vector")
        P.wait("vector", "v", None)
        P.cop("vector", lambda e: e.memset(SSB[:], 0.0))
        sbb_init = P.cop("vector", lambda e: e.memset(SBB[:], 0.0))

        emit_prep(0)
        for c in range(NT):
            emit_state(c)
            if c + 1 < NT:
                emit_prep(c + 1)
            emit_epi(c)
        es.close()
    P.fence()
    l1_mlp(nc, P, XRES, COLS, ST, IDBF, SCRA, wmlp1, bmlp1, wmlp2, bmlp2_bc,
           getc, x2d, lnw_bc, lnb_bc)


def l1_mlp(nc, P, XRES, COLS, ST, IDBF, SCRA, wmlp1, bmlp1, wmlp2, bmlp2_bc,
           getc, x2d, lnw_bc, lnb_bc):
    R2C, NMR2C = ST[:, 8 * NT:9 * NT], ST[:, 9 * NT:10 * NT]
    with (
        nc.sbuf_tensor("X2G", [128, 4 * 1024], BF16) as X2G,
        nc.sbuf_tensor("X2T", [128, ND * 512], BF16) as X2T,
        nc.sbuf_tensor("HMT", [128, 16 * 512], BF16) as HMT,
        nc.sbuf_tensor("MW1", [128, ND * 2048], BF16) as MW1,
        nc.sbuf_tensor("MW2", [128, 16 * 1024], BF16) as MW2,
        nc.sbuf_tensor("BM1", [128, 16], F32) as BM1,
        nc.sbuf_tensor("BM2", [128, D], F32) as BM2,
        nc.sbuf_tensor("TD", [128, 1024], BF16) as TD,
        nc.sbuf_tensor("LNW2", [128, D], BF16) as LNW2,
        nc.sbuf_tensor("LNB2", [128, D], BF16) as LNB2,
        nc.psum_tensor("MPS", [128, 1024], F32) as MPS,
        nc.psum_tensor("TPM", [128, 1024], BF16) as TPM,
        nc.psum_tensor("OPM", [128, 2048], F32) as OPM,
    ):
        w1tok = P.dma("g", lambda e: e.dma_start(
            out=MW1[:].rearrange("p (k o) -> p k o", k=ND),
            in_=wmlp1.rearrange("(k p) o -> p k o", p=128)))
        w2tok = P.dma("g", lambda e: e.dma_start(
            out=MW2[:].rearrange("p (k o) -> p k o", k=16),
            in_=wmlp2.rearrange("(k p) o -> p k o", p=128)))
        b1tok = P.dma("y", lambda e: e.dma_start(out=BM1[:], in_=bmlp1[:, :]))
        lw2 = P.dma("g", lambda e: e.dma_start(out=LNW2[:], in_=lnw_bc[:, :]))
        lb2 = P.dma("g", lambda e: e.dma_start(out=LNB2[:], in_=lnb_bc[:, :]))
        P.wait_tok("vector", lw2)
        P.wait_tok("vector", lb2)
        b2tok = P.dma("y", lambda e: e.dma_start(out=BM2[:], in_=bmlp2_bc[:, :]))
        P.wait_tok("tensor", w1tok)
        P.wait_tok("tensor", w2tok)
        P.wait_tok("scalar", b1tok)
        P.wait_tok("vector", b2tok)
        x2toks = []
        for gb in range(4):
            x2toks.append(P.dma("y", lambda e, gb=gb: e.dma_start(
                out=X2G[:, ts(gb, 1024)], in_=x2d[ts(gb), :])))
        SUM, SUM2 = COLS[:, 0:NT], COLS[:, 16:16 + NT]
        for ti in range(NT):
            gslot = ti % 4
            P.wait_tok("vector", x2toks[ti])
            P.wait_tok("scalar", x2toks[ti])
            P.cop("vector", lambda e, gslot=gslot, ti=ti: e.tensor_reduce(
                SUM[:, ti:ti + 1], X2G[:, ts(gslot, 1024)], AX.X, ALU.add))
            P.wait("scalar", "s", P.cnt["s"])
            sq = P.inc("scalar", lambda e, gslot=gslot, ti=ti: e.activation(
                SCRA[:], X2G[:, ts(gslot, 1024)], AF.Square, bias=0.0, scale=1.0,
                accum_out=SUM2[:, ti:ti + 1]), "s")
            if ti + 4 < NT:
                P.wait("sync", "v", P.cnt["v"])
                P.wait("sync", "s", P.cnt["s"])
                x2toks.append(P.dma("y", lambda e, ti=ti: e.dma_start(
                    out=X2G[:, ts((ti + 4) % 4, 1024)], in_=x2d[ts(ti + 4), :])))
        P.wait("vector", "s", P.cnt["s"])
        P.wait("vector", "v", None)
        M = COLS[:, 32:32 + NT]
        VAR = COLS[:, 0:NT]
        P.cop("vector", lambda e: e.tensor_scalar(M, SUM, 1.0 / D, None, ALU.mult))
        P.cop("vector", lambda e: e.tensor_scalar(SUM2, SUM2, 1.0 / D, None, ALU.mult))
        P.cop("vector", lambda e: e.tensor_tensor(VAR, M, M, ALU.mult))
        vd = P.cop("vector", lambda e: e.tensor_tensor(VAR, SUM2, VAR, ALU.subtract))
        P.wait("scalar", "v", vd)
        P.wait("scalar", "s", P.cnt["s"])
        P.inc("scalar", lambda e: e.activation(SUM2, VAR, AF.Ln, bias=getc(1e-5), scale=1.0), "s")
        P.wait("scalar", "s", P.cnt["s"])
        rd = P.inc("scalar", lambda e: e.activation(R2C, SUM2, AF.Exp, bias=0.0, scale=-0.5), "s")
        P.wait("vector", "s", rd)
        P.cop("vector", lambda e: e.tensor_tensor(NMR2C, M, R2C, ALU.mult))
        nm = P.cop("vector", lambda e: e.tensor_scalar(NMR2C, NMR2C, -1.0, None, ALU.mult))
        P.wait("sync", "v", P.cnt["v"])
        P.wait("sync", "s", P.cnt["s"])
        for g in range(4):
            for gb in range(4):
                ti = 4 * g + gb
                P.wait("sync", "v", P.cnt["v"])
                tok = P.dma("y", lambda e, gb=gb, ti=ti: e.dma_start(
                    out=X2G[:, ts(gb, 1024)], in_=x2d[ts(ti), :]))
                P.wait_tok("vector", tok)
                P.wait("vector", "v", nm)
                P.cop("vector", lambda e, gb=gb, ti=ti: e.tensor_scalar(
                    TD[:], X2G[:, ts(gb, 1024)], R2C[:, ti:ti + 1],
                    NMR2C[:, ti:ti + 1], ALU.mult, ALU.add))
                P.cop("vector", lambda e: e.tensor_tensor(TD[:], TD[:], LNW2[:], ALU.mult))
                xn = P.cop("vector", lambda e: e.tensor_tensor(TD[:], TD[:], LNB2[:], ALU.add))
                P.wait("tensor", "v", xn)
                for k in range(ND):
                    trd = P.inc("tensor", lambda e, k=k: e.transpose(
                        TPM[:, ts(k)], TD[:, ts(k)], IDBF[:]), "t")
                P.wait("vector", "t", trd)
                for k in range(ND):
                    P.cop("vector", lambda e, k=k, gb=gb: e.tensor_copy(
                        X2T[:, k * 512 + 128 * gb: k * 512 + 128 * (gb + 1)], TPM[:, ts(k)]))
                P.wait("tensor", "v", P.cnt["v"])
            evs = 0
            for ht in range(16):
                if ht >= 2:
                    P.wait("tensor", "s", evs - 1)
                for k in range(ND):
                    mm = P.inc("tensor", lambda e, ht=ht, k=k: e.matmul(
                        MPS[:, ts(ht % 2, 512)],
                        MW1[:, 2048 * k + 128 * ht: 2048 * k + 128 * (ht + 1)],
                        X2T[:, ts(k, 512)],
                        start=(k == 0), stop=(k == ND - 1)), "t")
                P.wait("scalar", "t", mm)
                P.wait("scalar", "s", P.cnt["s"])
                evs = P.inc("scalar", lambda e, ht=ht: e.activation(
                    HMT[:, ts(ht, 512)], MPS[:, ts(ht % 2, 512)], AF.Gelu,
                    bias=BM1[:, ht:ht + 1], scale=1.0), "s")
            P.wait("tensor", "s", evs)
            for tc in range(4):
                ti = 4 * g + tc
                for k in range(16):
                    for nn in range(2):
                        mm = P.inc("tensor", lambda e, tc=tc, k=k, nn=nn: e.matmul(
                            OPM[:, ts((tc % 2) * 2 + nn, 512)],
                            HMT[:, 512 * k + 128 * tc: 512 * k + 128 * (tc + 1)],
                            MW2[:, 1024 * k + 512 * nn: 1024 * k + 512 * (nn + 1)],
                            start=(k == 0), stop=(k == 15)), "t")
                P.wait("vector", "t", mm)
                P.cop("vector", lambda e, ti=ti, tc=tc: e.tensor_tensor(
                    XRES[:, ts(ti, 1024)], XRES[:, ts(ti, 1024)],
                    OPM[:, (tc % 2) * 1024: (tc % 2) * 1024 + 1024], ALU.add))
                ad = P.cop("vector", lambda e, ti=ti: e.tensor_tensor(
                    XRES[:, ts(ti, 1024)], XRES[:, ts(ti, 1024)], BM2[:], ALU.add))
                P.wait("tensor", "v", ad)



def l1_ffn(nc, P, XRES, COLS, IDBF, w1t1, b1f, w2t1, b2f_bc, getc, xn1td):
    with (
        nc.sbuf_tensor("XN1F", [128, ND * T], BF16) as XN1F,
        nc.sbuf_tensor("HQT", [128, 32 * T], FP8) as HQT,
        nc.sbuf_tensor("QW2", [128, 32 * 1024], FP8) as QW2,
        nc.sbuf_tensor("MWS", [128, 2 * ND * 128], BF16) as MWS,
        nc.sbuf_tensor("B1F", [128, 32], F32) as B1F,
        nc.sbuf_tensor("BQ16", [128, D], mybir.dt.float16) as BQ16,
        nc.sbuf_tensor("HF", [128, 1024], F32) as HF,
        nc.psum_tensor("HP1", [128, 2048], F32) as HP1,
        nc.psum_tensor("OPF", [128, 2048], F32) as OPF,
    ):
        OT16 = HF[:, 0:512].bitcast(mybir.dt.float16)
        xl = P.dma("y", lambda e: e.dma_start(out=XN1F[:], in_=xn1td[:, :]))
        ql = P.dma("g", lambda e: e.dma_start(
            out=QW2[:].rearrange("p (k o) -> p k o", k=32),
            in_=w2t1.rearrange("(k p) o -> p k o", p=128)))
        bl = P.dma("y", lambda e: e.dma_start(out=B1F[:], in_=b1f[:, :]))
        b2l = P.dma("g", lambda e: e.dma_start(out=BQ16[:], in_=b2f_bc[:, :]))
        P.wait_tok("tensor", xl)
        P.wait_tok("scalar", bl)
        mm_t = {}
        hr_s = {}
        hq_v = {}
        for ht in range(32):
            PS = HP1 if ht % 2 == 0 else OPF
            if ht >= 2:
                P.wait("gpsimd", "t", mm_t[ht - 2])
            wtok = P.dma("g", lambda e, ht=ht: e.dma_start(
                out=MWS[:, 1024 * (ht % 2):1024 * (ht % 2 + 1)]
                    .rearrange("p (k o) -> p k o", k=ND),
                in_=w1t1.rearrange("(k p) o -> p k o", p=128)[:, :, 128 * ht:128 * (ht + 1)]))
            P.wait_tok("tensor", wtok)
            if ht >= 2:
                P.wait("tensor", "s", hr_s[ht - 2])
            for k in range(ND):
                for nn in range(4):
                    mm = P.inc("tensor", lambda e, ht=ht, k=k, nn=nn, PS=PS: e.matmul(
                        PS[:, ts(nn, 512)],
                        MWS[:, 1024 * (ht % 2) + 128 * k:1024 * (ht % 2) + 128 * (k + 1)],
                        XN1F[:, k * T + 512 * nn: k * T + 512 * (nn + 1)],
                        start=(k == 0), stop=(k == ND - 1)), "t")
            mm_t[ht] = mm
            for hh in range(2):
                P.wait("scalar", "t", mm)
                if ht >= 1 or hh == 1:
                    P.wait("scalar", "v", hq_v[(ht, hh - 1) if hh else (ht - 1, 1)])
                P.wait("scalar", "s", P.cnt["s"])
                hr_s[ht] = P.inc("scalar", lambda e, ht=ht, hh=hh, PS=PS: e.activation(
                    HF[:], PS[:, 1024 * hh:1024 * (hh + 1)], AF.Relu,
                    bias=B1F[:, ht:ht + 1], scale=1.0), "s")
                P.wait("vector", "s", hr_s[ht])
                hq_v[(ht, hh)] = P.cop("vector", lambda e, ht=ht, hh=hh: e.tensor_copy(
                    HQT[:, 2048 * ht + 1024 * hh:2048 * ht + 1024 * (hh + 1)], HF[:]))
        P.wait_tok("tensor", ql)
        add_v = {}
        for ti in range(NT):
            if ti == 0:
                P.wait("tensor", "s", hr_s[31])
                P.wait("tensor", "v", hq_v[(31, 1)])
            if ti >= 2:
                P.wait("tensor", "v", add_v[ti - 2])
            for k in range(32):
                for nn in range(2):
                    mm = P.inc("tensor", lambda e, ti=ti, k=k, nn=nn: e.matmul(
                        OPF[:, ts((ti % 2) * 2 + nn, 512)],
                        HQT[:, 2048 * k + 128 * ti: 2048 * k + 128 * (ti + 1)],
                        QW2[:, 1024 * k + 512 * nn: 1024 * k + 512 * (nn + 1)],
                        start=(k == 0), stop=(k == 31)), "t")
            P.wait("vector", "t", mm)
            P.wait_tok("vector", b2l)
            P.cop("vector", lambda e, ti=ti: e.tensor_tensor(
                OT16[:], OPF[:, (ti % 2) * 1024: (ti % 2) * 1024 + 1024], BQ16[:], ALU.add))
            add_v[ti] = P.cop("vector", lambda e, ti=ti: e.tensor_tensor(
                XRES[:, ts(ti, 1024)], XRES[:, ts(ti, 1024)], OT16[:], ALU.add))



_CACHE = {}


def _build_program(stop_after="full"):
    if stop_after not in _CACHE:
        nc = bass.Bass("TRN2", target_bir_lowering=False, debug=False)
        P = build(nc, stop_after)
        with nc.Block() as blk:
            P.run(blk)
        _CACHE[stop_after] = nc
    return _CACHE[stop_after]


def _in_maps(inputs):
    d = inputs
    f32 = np.float32

    def asc(a):
        return np.ascontiguousarray(np.asarray(a, dtype=f32))

    shared = {
        "mq": asc(d["L0_attn_Mq"]),
        "mk": asc(d["L0_attn_Mk"]),
        "mv": asc(d["L0_attn_Mv"]),
        "w1t0": asc(np.asarray(d["L0_ffn_w1"], dtype=f32).T),
        "w2t0": asc(np.asarray(d["L0_ffn_w2"], dtype=f32).T),
        "n1w_bc": asc(np.broadcast_to(np.asarray(d["L0_ffn_norm1_w"], f32), (128, D))),
        "n2w_bc": asc(np.broadcast_to(np.asarray(d["L0_ffn_norm2_w"], f32), (128, 4096))),
        "actsb": asc(np.broadcast_to(np.concatenate([
            np.asarray(d["L0_ffn_act_scale"], f32).reshape(1),
            np.asarray(d["L0_ffn_act_bias"], f32).reshape(1)]), (128, 2))),
        "lnw_bc": asc(np.broadcast_to(np.asarray(d["L1_attn_ln_w"], f32), (128, D))),
        "lnb_bc": asc(np.broadcast_to(np.asarray(d["L1_attn_ln_b"], f32), (128, D))),
        "wq1": asc(d["L1_attn_Wq"]),
        "wk1": asc(d["L1_attn_Wk"]),
        "wv1": asc(d["L1_attn_Wv"]),
        "wg1": asc(d["L1_attn_Wg"]),
        "gkw1": asc(d["L1_attn_gk_w1"]),
        "gkw2": asc(d["L1_attn_gk_w2"]),
        "gkb2_bc": asc(np.broadcast_to(np.asarray(d["L1_attn_gk_b2"], f32), (128, 512))),
        "rmsw_bc": asc(np.broadcast_to(np.tile(np.asarray(d["L1_attn_rms_w"], f32), 16), (128, D))),
        "wo1": asc(d["L1_attn_Wo"]),
        "wmlp1": asc(d["L1_attn_W1"]),
        "bmlp1": asc(np.asarray(d["L1_attn_b1"], f32).reshape(16, 128).T),
        "wmlp2": asc(d["L1_attn_W2"]),
        "bmlp2_bc": asc(np.broadcast_to(np.asarray(d["L1_attn_b2"], f32), (128, D))),
        "w1t1": asc(np.asarray(d["L1_ffn_w1"], dtype=f32).T),
        "b1f": asc(np.asarray(d["L1_ffn_b1"], f32).reshape(32, 128).T),
        "w2t1": asc(np.asarray(d["L1_ffn_w2"], dtype=f32).T),
        "b2f_bc": asc(np.broadcast_to(np.asarray(d["L1_ffn_b2"], f32), (128, D))),
    }
    x = np.asarray(d["L0_attn_x"], dtype=f32)
    return [{**shared, "x": np.ascontiguousarray(x[b])} for b in range(x.shape[0])]


def kernel(**inputs):
    nc = _build_program("full")
    maps = _in_maps(inputs)
    res = run_bass_kernel_spmd(nc, maps, core_ids=list(range(8)))
    return np.stack([r["y"] for r in res.results], axis=0)


def kernel_dev(stop_after, inputs, cores=1):
    nc = _build_program(stop_after)
    maps = _in_maps(inputs)[:cores]
    res = run_bass_kernel_spmd(nc, maps, core_ids=list(range(cores)))
    return np.stack([r["y"] for r in res.results], axis=0)


# version 12
# speedup vs baseline: 1.2590x; 1.0278x over previous
import contextlib
import numpy as np
import concourse.bass as bass
import concourse.mybir as mybir
from concourse.bass_utils import run_bass_kernel_spmd

F32 = mybir.dt.float32
F32R = mybir.dt.float32r
BF16 = mybir.dt.bfloat16
FP8 = mybir.dt.float8e4
I32 = mybir.dt.int32
AF = mybir.ActivationFunctionType
ALU = mybir.AluOpType
AX = mybir.AxisListType

T = 2048
D = 1024
NT = 16
ND = 8


def ts(i, n=128):
    return slice(i * n, (i + 1) * n)


def TPSLICE(PT, base):
    return PT[:, (base % 8) * 128:(base % 8) * 128 + 512]


class Plan:

    ENGS = ("sync", "gpsimd", "vector", "scalar", "tensor")

    NLANE = 4

    def __init__(self, nc):
        self.nc = nc
        self.q = {e: [] for e in self.ENGS}
        self.h = {}
        self.cnt = {}
        self.last_wait = {}
        names = ["t", "v", "s"]
        for qn in ("y", "g"):
            for i in range(self.NLANE):
                names.append(qn + str(i))
        for n in names:
            self.h[n] = nc.alloc_semaphore("sem_" + n)
            self.cnt[n] = 0
        self.rr = {"y": 0, "g": 0}

    def dma(self, queue, fn):
        eng = "sync" if queue == "y" else "gpsimd"
        i = self.rr[queue]
        self.rr[queue] = (i + 1) % self.NLANE
        lane = queue + str(i)
        if self.cnt[lane] > 0:
            self.wait(eng, lane, self.cnt[lane])
        tok = (lane, self.inc(eng, fn, lane, 16))
        return tok

    def wait_tok(self, eng, tok):
        self.wait(eng, tok[0], tok[1])

    def cop(self, eng, fn):
        sem = {"vector": "v", "scalar": "s", "tensor": "t"}[eng]
        self.wait(eng, sem, self.cnt[sem])
        return self.inc(eng, fn, sem)

    def fence(self):
        sems = [n for n in self.cnt if self.cnt[n] > 0]
        for eng in self.ENGS:
            for n in sems:
                self.wait(eng, n, self.cnt[n])

    def dma_barrier(self, eng):
        for lane in list(self.cnt):
            if lane[0] in "yg" and len(lane) == 2 and self.cnt[lane] > 0:
                self.wait(eng, lane, self.cnt[lane])

    def emit(self, eng, fn):
        self.q[eng].append(fn)

    def inc(self, eng, fn, sem, k=1):
        h = self.h[sem]
        self.q[eng].append(lambda e, fn=fn, h=h, k=k: fn(e).then_inc(h, k))
        self.cnt[sem] += k
        return self.cnt[sem]

    def wait(self, eng, sem, val=None):
        if val is None:
            val = self.cnt[sem]
        assert val <= self.cnt[sem], f"wait on future value of {sem}"
        key = (eng, sem)
        if self.last_wait.get(key, -1) >= val:
            return
        self.last_wait[key] = val
        h = self.h[sem]
        self.q[eng].append(lambda e, h=h, v=val: e.wait_ge(h, v))

    def run(self, block):
        for eng in self.ENGS:
            block.__getattribute__(eng)(lambda e, eng=eng: [f(e) for f in self.q[eng]])


def build(nc, stop_after="full"):
    dt = nc.dram_tensor
    x_in = dt("x", [T, D], F32, kind="ExternalInput").ap()
    y_out = dt("y", [T, D], F32, kind="ExternalOutput").ap()
    mq = dt("mq", [D, 512], F32, kind="ExternalInput").ap()
    mk = dt("mk", [D, 512], F32, kind="ExternalInput").ap()
    mv = dt("mv", [D, 1024], F32, kind="ExternalInput").ap()
    w1t0 = dt("w1t0", [D, 4096], F32, kind="ExternalInput").ap()
    w2t0 = dt("w2t0", [4096, D], F32, kind="ExternalInput").ap()
    n1w_bc = dt("n1w_bc", [128, D], F32, kind="ExternalInput").ap()
    n2w_bc = dt("n2w_bc", [128, 4096], F32, kind="ExternalInput").ap()
    actsb = dt("actsb", [128, 2], F32, kind="ExternalInput").ap()
    lnw_bc = dt("lnw_bc", [128, D], F32, kind="ExternalInput").ap()
    lnb_bc = dt("lnb_bc", [128, D], F32, kind="ExternalInput").ap()
    wq1 = dt("wq1", [D, 512], F32, kind="ExternalInput").ap()
    wk1 = dt("wk1", [D, 512], F32, kind="ExternalInput").ap()
    wv1 = dt("wv1", [D, 1024], F32, kind="ExternalInput").ap()
    wg1 = dt("wg1", [D, 1024], F32, kind="ExternalInput").ap()
    gkw1 = dt("gkw1", [D, 16], F32, kind="ExternalInput").ap()
    gkw2 = dt("gkw2", [16, 512], F32, kind="ExternalInput").ap()
    gkb2_bc = dt("gkb2_bc", [128, 512], F32, kind="ExternalInput").ap()
    rmsw_bc = dt("rmsw_bc", [128, D], F32, kind="ExternalInput").ap()
    wo1 = dt("wo1", [D, D], F32, kind="ExternalInput").ap()
    wmlp1 = dt("wmlp1", [D, 2048], F32, kind="ExternalInput").ap()
    bmlp1 = dt("bmlp1", [128, 16], F32, kind="ExternalInput").ap()
    wmlp2 = dt("wmlp2", [2048, D], F32, kind="ExternalInput").ap()
    bmlp2_bc = dt("bmlp2_bc", [128, D], F32, kind="ExternalInput").ap()
    w1t1 = dt("w1t1", [D, 4096], F32, kind="ExternalInput").ap()
    b1f = dt("b1f", [128, 32], F32, kind="ExternalInput").ap()
    w2t1 = dt("w2t1", [4096, D], F32, kind="ExternalInput").ap()
    b2f_bc = dt("b2f_bc", [128, D], F32, kind="ExternalInput").ap()

    c_ident = nc.inline_tensor(np.eye(128, dtype=np.float32), "c_ident")
    c_triu = nc.inline_tensor(np.triu(np.ones((128, 128), np.float32)), "c_triu")
    x2d = dt("x2d", [T, D], BF16, kind="Internal").ap()
    xn1td = dt("xn1td", [128, ND * T], BF16, kind="Internal").ap()

    P = Plan(nc)

    with (
        nc.sbuf_tensor("XRES", [128, NT * 1024], F32) as XRES,
        nc.sbuf_tensor("IDBF", [128, 128], BF16) as IDBF,
        nc.sbuf_tensor("ST", [128, 10 * NT], F32) as ST,
        nc.sbuf_tensor("COLS", [128, 64], F32) as COLS,
        nc.sbuf_tensor("ONESR", [1, 128], F32) as ONESR,
        nc.sbuf_tensor("CONSTC", [128, 16], F32) as CONSTC,
        nc.sbuf_tensor("SCRA", [128, 1024], BF16) as SCRA,
        nc.Block() as block,
    ):
        def st(i):
            return ST[:, i * NT:(i + 1) * NT]
        R0, NMR0, R2, NMR2, MX0, MX2 = (st(i) for i in range(6))
        R1, NMR1 = st(6), st(7)

        CVALS = [1e-5, 1e-8, 4096.0 * 1e-8, float(np.log(127.0 * 1e5)),
                 float(0.5 * np.log(4096.0)), 1.0,
                 float(-0.5 * np.log(32.0)), float(np.log(2.0)), 1e-35,
                 float(-3.0 * np.log(2.0)), float(3.0 * np.log(2.0)),
                 float(-10.0 * np.log(2.0)), float(10.0 * np.log(2.0))]
        CIDX = {v: i for i, v in enumerate(CVALS)}

        def getc(v):
            return CONSTC[:, CIDX[v]:CIDX[v] + 1]

        x_toks = [P.dma("y", lambda e, ti=ti: e.dma_start(out=XRES[:, ts(ti, 1024)],
                    in_=x_in[ts(ti), :])) for ti in range(NT)]
        id_tok = P.dma("g", lambda e: e.dma_start(out=IDBF[:], in_=c_ident.ap()[:, :]))
        P.cop("vector", lambda e: e.memset(ONESR[:], 1.0))
        for i, v in enumerate(CVALS):
            P.cop("vector", lambda e, i=i, v=v: e.memset(CONSTC[:, i:i + 1], v))
        ones_done = P.cop("vector", lambda e: e.memset(COLS[:, 63:64], 1.0))
        P.wait("scalar", "v", ones_done)

        def ln_stats(r_col, nmr_col, dep_v):
            SUM, SUM2 = COLS[:, 0:NT], COLS[:, 16:16 + NT]
            for tok in x_toks:
                P.wait_tok("vector", tok)
                P.wait_tok("scalar", tok)
            P.wait("vector", "v", dep_v)
            if dep_v:
                P.wait("scalar", "v", dep_v)
            for ti in range(NT):
                P.cop("vector", lambda e, ti=ti: e.tensor_reduce(
                    SUM[:, ti:ti + 1], XRES[:, ts(ti, 1024)], AX.X, ALU.add))
                P.wait("scalar", "s", P.cnt["s"])
                P.cop("scalar", lambda e, ti=ti: e.activation(
                    SCRA[:], XRES[:, ts(ti, 1024)], AF.Square, bias=0.0, scale=1.0,
                    accum_out=SUM2[:, ti:ti + 1]))
            P.wait("vector", "s", P.cnt["s"])
            P.wait("vector", "v", None)
            M = COLS[:, 32:32 + NT]
            VAR = COLS[:, 0:NT]
            P.cop("vector", lambda e: e.tensor_scalar(M, SUM, 1.0 / D, None, ALU.mult))
            P.cop("vector", lambda e: e.tensor_scalar(SUM2, SUM2, 1.0 / D, None, ALU.mult))
            P.cop("vector", lambda e: e.tensor_tensor(VAR, M, M, ALU.mult))
            vd = P.cop("vector", lambda e: e.tensor_tensor(VAR, SUM2, VAR, ALU.subtract))
            P.wait("scalar", "v", vd)
            P.cop("scalar", lambda e: e.activation(SUM2, VAR, AF.Ln, bias=getc(1e-5), scale=1.0))
            rd = P.cop("scalar", lambda e: e.activation(r_col, SUM2, AF.Exp, bias=0.0, scale=-0.5))
            P.wait("vector", "s", rd)
            P.cop("vector", lambda e: e.tensor_tensor(nmr_col, M, r_col, ALU.mult))
            nm = P.cop("vector", lambda e: e.tensor_scalar(nmr_col, nmr_col, -1.0, None, ALU.mult))
            return VAR, nm

        VAR0, nm0 = ln_stats(R0, NMR0, dep_v=0)
        TMP = COLS[:, 16:16 + NT]
        P.cop("vector", lambda e: e.tensor_tensor(TMP, VAR0, R0, ALU.mult))
        t2 = P.cop("vector", lambda e: e.tensor_tensor(TMP, TMP, R0, ALU.mult))
        P.wait("scalar", "v", t2)
        P.cop("scalar", lambda e: e.activation(TMP, TMP, AF.Ln, bias=getc(1e-8), scale=1.0))
        iv = P.cop("scalar", lambda e: e.activation(TMP, TMP, AF.Exp, bias=0.0, scale=-0.5))
        P.wait("vector", "s", iv)
        P.cop("vector", lambda e: e.tensor_tensor(R2, R0, TMP, ALU.mult))
        l0_cols = P.cop("vector", lambda e: e.tensor_tensor(NMR2, NMR0, TMP, ALU.mult))

        with nc.sbuf_tensor("XNT", [128, ND * T], BF16) as XNT:
            P.wait("vector", "v", l0_cols)
            P.wait_tok("tensor", id_tok)
            l0_ffn(nc, P, XRES, w1t0, w2t0, n1w_bc, n2w_bc, actsb, R0, NMR0,
                   R2, NMR2, MX0, COLS, ONESR, IDBF, SCRA, getc, XNT)
            if stop_after == "l0ffn":
                finish(nc, P, XRES, y_out)
                return P

            P.fence()
            l0_attn(nc, P, XRES, XNT, mq, mk, mv, COLS)
        if stop_after == "l0":
            finish(nc, P, XRES, y_out)
            return P

        P.fence()
        _, nm1 = ln_stats(R1, NMR1, dep_v=P.cnt["v"])
        l1_attn(nc, P, XRES, R1, NMR1, COLS, ST, IDBF, ONESR, c_ident, c_triu,
                lnw_bc, lnb_bc, wq1, wk1, wv1, wg1, gkw1, gkw2, gkb2_bc, rmsw_bc,
                wo1, wmlp1, bmlp1, wmlp2, bmlp2_bc, SCRA, getc, x2d, xn1td)
        if stop_after == "l1attn":
            finish(nc, P, XRES, y_out)
            return P
        P.fence()
        l1_ffn(nc, P, XRES, COLS, IDBF, w1t1, b1f, w2t1, b2f_bc, getc, xn1td)

        finish(nc, P, XRES, y_out)
    return P


def finish(nc, P, XRES, y_out):
    P.wait("gpsimd", "v", P.cnt["v"])
    P.wait("gpsimd", "s", P.cnt["s"])
    P.wait("gpsimd", "t", P.cnt["t"])
    for ti in range(NT):
        P.dma("g", lambda e, ti=ti: e.dma_start(
            out=y_out[ts(ti), :], in_=XRES[:, ts(ti, 1024)]))
    P.dma_barrier("gpsimd")
    P.dma_barrier("sync")



def l0_attn(nc, P, XRES, XNT, mq, mk, mv, COLS):
    with (
        nc.sbuf_tensor("QT2", [128, 4 * T], BF16) as QT2,
        nc.sbuf_tensor("KT2", [128, 4 * T], BF16) as KT2,
        nc.sbuf_tensor("VB", [128, 16384], BF16) as VB,
        nc.sbuf_tensor("EB", [128, 2 * T], BF16) as EB,
        nc.sbuf_tensor("VSC", [128, 256], BF16) as VSC,
        nc.sbuf_tensor("MW", [128, 8192], BF16) as MW,
        nc.psum_tensor("SC", [128, 2048], F32) as SC,
        nc.psum_tensor("OPS", [128, 2048], F32) as OPS,
    ):
        for wsrc, dstT in ((mq, QT2), (mk, KT2)):
            P.wait("gpsimd", "t", P.cnt["t"])
            wl = P.dma("g", lambda e, wsrc=wsrc: e.dma_start(
                out=MW[:, 0:4096].rearrange("p (k o) -> p k o", k=ND),
                in_=wsrc.rearrange("(k p) o -> p k o", p=128)))
            P.wait_tok("tensor", wl)
            P.wait("tensor", "v", P.cnt["v"])
            for m in range(4):
                h0 = 2 * m
                r0 = 64 * (h0 // 4)
                for sub in range(8):
                    g = (m * 8 + sub) % 4
                    reg = SC[r0:r0 + 64, 512 * g:512 * g + 512]
                    for k in range(ND):
                        mm = P.inc("tensor", lambda e, m=m, sub=sub, k=k, reg=reg: e.matmul(
                            reg,
                            MW[:, k * 512 + 64 * sub: k * 512 + 64 * (sub + 1)],
                            XNT[:, k * T + 512 * m: k * T + 512 * (m + 1)],
                            start=(k == 0), stop=(k == ND - 1)), "t")
                    P.wait("vector", "t", mm)
                    blk0 = h0 % 4
                    ev = P.cop("vector", lambda e, dstT=dstT, r0=r0, blk0=blk0, sub=sub,
                               reg=reg: e.tensor_copy(
                        dstT[r0:r0 + 64, blk0 * T:(blk0 + 2) * T].rearrange(
                            "p (hh tl rest) -> p hh tl rest", hh=2, tl=2)
                            [:, :, :, sub * 128:(sub + 1) * 128],
                        reg.rearrange("p (hh tl t) -> p hh tl t", hh=2, tl=2)))
                    if g == 3:
                        P.wait("tensor", "v", ev)
        P.wait("gpsimd", "t", P.cnt["t"])
        wl = P.dma("g", lambda e: e.dma_start(
            out=MW[:].rearrange("p (k o) -> p k o", k=ND),
            in_=mv.rearrange("(k p) o -> p k o", p=128)))
        P.wait_tok("tensor", wl)
        P.wait("tensor", "v", P.cnt["v"])
        for h in range(8):
            for tlc in range(2):
                for sg in range(2):
                    g = (h * 4 + tlc * 2 + sg) % 4
                    reg = OPS[:, 512 * g:512 * g + 512]
                    for k in range(ND):
                        mm = P.inc("tensor", lambda e, h=h, tlc=tlc, sg=sg, k=k, reg=reg:
                            e.matmul(
                                reg,
                                XNT[:, k * T + 256 * h + 128 * tlc:
                                    k * T + 256 * h + 128 * (tlc + 1)],
                                MW[:, k * 1024 + 512 * sg: k * 1024 + 512 * (sg + 1)],
                                start=(k == 0), stop=(k == ND - 1)), "t")
                    P.wait("vector", "t", mm)
                    t0_ = h * 16 + 8 * sg + tlc
                    ev = P.cop("vector", lambda e, t0_=t0_, reg=reg: e.tensor_copy(
                        VB[:].rearrange("p (n t) -> p n t", t=128)[:, t0_:t0_ + 7:2, :],
                        reg.rearrange("p (s t) -> p s t", s=4)))
                    if g == 3:
                        P.wait("tensor", "v", ev)
        exp_s = {}
        outmm_t = {}
        rv_v = {}
        vs_v = {}
        evac_v = None

        def emit_out(h, jt):
            par = jt % 2
            P.wait("tensor", "v", vs_v[jt])
            if jt == 0 and evac_v is not None:
                P.wait("tensor", "v", evac_v)
            for tau in range(2):
                for s_ in range(8):
                    outmm_t[par] = P.inc("tensor", lambda e, tau=tau, s_=s_, jt=jt,
                                         par=par: e.matmul(
                        OPS[:, ts(8 * tau + s_)],
                        EB[:, 2048 * par + 1024 * tau + 128 * s_:
                            2048 * par + 1024 * tau + 128 * (s_ + 1)],
                        VSC[:, 128 * par:128 * (par + 1)],
                        start=(jt == 0 and (8 * tau + s_) % 4 == 0),
                        stop=(jt == NT - 1 and (8 * tau + s_) % 4 == 3)), "t")

        for h in range(8):
            r0 = 64 * (h // 4)
            blk = h % 4
            for jt in range(NT):
                sub, tlc = jt // 2, jt % 2
                par = jt % 2
                jb = blk * T + 1024 * tlc + 128 * sub
                for half in range(2):
                    if half in exp_s:
                        P.wait("tensor", "s", exp_s[half])
                    for nn in range(2):
                        scd = P.inc("tensor", lambda e, r0=r0, blk=blk, jb=jb,
                                    half=half, nn=nn: e.matmul(
                            SC[:, 1024 * half + 512 * nn:1024 * half + 512 * (nn + 1)],
                            KT2[r0:r0 + 64, jb:jb + 128],
                            QT2[r0:r0 + 64, blk * T + 1024 * half + 512 * nn:
                                blk * T + 1024 * half + 512 * (nn + 1)],
                            start=True, stop=True), "t")
                    P.wait("scalar", "t", scd)
                    if half == 0:
                        if par in rv_v:
                            P.wait("scalar", "v", rv_v[par])
                        if par in outmm_t:
                            P.wait("scalar", "t", outmm_t[par])
                    exp_s[half] = P.inc("scalar", lambda e, par=par, half=half: e.activation(
                        EB[:, 2048 * par + 1024 * half:2048 * par + 1024 * (half + 1)],
                        SC[:, 1024 * half:1024 * (half + 1)],
                        AF.Exp, bias=0.0, scale=0.125,
                        accum_out=COLS[:, 44 + 2 * par + half:45 + 2 * par + half]), "s")
                P.wait("vector", "s", exp_s[1])
                P.cop("vector", lambda e, par=par: e.tensor_tensor(
                    COLS[:, 51 + par:52 + par], COLS[:, 44 + 2 * par:45 + 2 * par],
                    COLS[:, 45 + 2 * par:46 + 2 * par], ALU.add))
                rv_v[par] = P.cop("vector", lambda e, par=par: e.reciprocal(
                    COLS[:, 51 + par:52 + par], COLS[:, 51 + par:52 + par]))
                if par in outmm_t:
                    P.wait("vector", "t", outmm_t[par])
                vs_v[jt] = P.cop("vector", lambda e, h=h, jt=jt, par=par: e.tensor_scalar(
                    VSC[:, 128 * par:128 * (par + 1)],
                    VB[:, (h * 16 + jt) * 128:(h * 16 + jt) * 128 + 128],
                    COLS[:, 51 + par:52 + par], None, ALU.mult))
                if jt >= 1:
                    emit_out(h, jt - 1)
            emit_out(h, NT - 1)
            P.wait("vector", "t", outmm_t[(NT - 1) % 2])
            evac_v = P.cop("vector", lambda e, h=h: e.tensor_tensor(
                XRES[:, 2048 * h:2048 * (h + 1)], XRES[:, 2048 * h:2048 * (h + 1)],
                OPS[:, 0:2048], ALU.add))



def l0_ffn(nc, P, XRES, w1t0, w2t0, n1w_bc, n2w_bc, actsb, R0, NMR0, R2, NMR2,
           MX0, COLS, ONESR, IDBF, SCRA, getc, XNT):
    SY, MXC = 0, 8
    CS, CLN, CRSQ, CMZ, CMXP, CF, CD2 = 16, 17, 18, 19, 20, 21, 22
    CDSC = 24
    CPM, CPR = 26, 27
    CW1A, CW1T, CW1NT = 53, 54, 55
    CW2A, CW2T, CW2NT = 56, 49, 50
    T2A, R2A, T1A, R1A = 57, 58, 59, 60
    CCW1, CCW2 = 46, 48
    CA, CB = 61, 62
    CONES = 63
    with (
        nc.sbuf_tensor("IW1", [128, ND * 4096], FP8) as IW1,
        nc.sbuf_tensor("IW2", [128, 32 * 1024], FP8) as IW2,
        nc.sbuf_tensor("N1W", [128, D], F32) as N1W,
        nc.sbuf_tensor("N2W", [128, 4096], F32) as N2W,
    ):
        n1ld = P.dma("g", lambda e: e.dma_start(out=N1W[:], in_=n1w_bc[:, :]))
        n2ld = P.dma("g", lambda e: e.dma_start(out=N2W[:], in_=n2w_bc[:, :]))
        asld = P.dma("y", lambda e: e.dma_start(out=COLS[:, CA:CB + 1], in_=actsb[:, :]))
        iw2_v = {}
        with (
            nc.sbuf_tensor("WR0", [128, 1024], F32) as WR0,
            nc.sbuf_tensor("WR1", [128, 1024], F32) as WR1,
            nc.sbuf_tensor("WR2", [128, 1024], F32) as WR2,
            nc.sbuf_tensor("WR3", [128, 1024], F32) as WR3,
            nc.sbuf_tensor("PRA", [128, 2048], BF16) as PRA,
            nc.sbuf_tensor("TILE_A", [128, 1024], BF16) as TILE_A,
            nc.psum_tensor("PT", [128, 1024], BF16) as PT,
            nc.psum_tensor("RPS", [128, 512], F32) as RPS,
        ):
            WY, WG_ = (WR0, WR1), (WR2, WR3)

            def w1piece(i):
                k, c = i // 4, i % 4
                return w1t0[ts(k), c * 1024:(c + 1) * 1024]

            def w2piece(i):
                return w2t0[ts(i), :]

            streams = (("y", WY, w1piece, R1A, T1A, 0),
                       ("g", WG_, w2piece, R2A, T2A, 1024))
            P.cop("vector", lambda e: e.memset(COLS[:, R1A:R1A + 1], 0.0))
            P.cop("vector", lambda e: e.memset(COLS[:, R2A:R2A + 1], 0.0))
            abs_s = {}
            acc_v = {}
            for i in range(32):
                for (qn, Wp, piece, runc, tmpc, po) in streams:
                    eng = "sync" if qn == "y" else "gpsimd"
                    buf = Wp[i % 2]
                    if i >= 2:
                        P.wait(eng, "s", abs_s[(qn, i - 2)])
                    tok = P.dma(qn, lambda e, buf=buf, src=piece(i): e.dma_start(
                        out=buf[:], in_=src))
                    P.wait_tok("scalar", tok)
                    if i >= 1:
                        P.wait("scalar", "v", acc_v[(qn, i - 1)])
                    abs_s[(qn, i)] = P.cop("scalar", lambda e, buf=buf, po=po, tmpc=tmpc:
                        e.activation(PRA[:, po:po + 1024], buf[:], AF.Abs, bias=0.0,
                                     scale=1.0, accum_out=COLS[:, tmpc:tmpc + 1]))
                    P.wait("vector", "s", abs_s[(qn, i)])
                    acc_v[(qn, i)] = P.cop("vector", lambda e, runc=runc, tmpc=tmpc:
                        e.tensor_tensor(
                            COLS[:, runc:runc + 1], COLS[:, runc:runc + 1],
                            COLS[:, tmpc:tmpc + 1], ALU.add))
            acc = P.cnt["v"]
            P.wait("tensor", "v", acc)
            P.inc("tensor", lambda e: e.matmul(
                RPS[0:1, 0:1], COLS[:, R1A:R1A + 1], COLS[:, CONES:CONES + 1],
                start=True, stop=True), "t")
            mmc = P.inc("tensor", lambda e: e.matmul(
                RPS[0:1, 1:2], COLS[:, R2A:R2A + 1], COLS[:, CONES:CONES + 1],
                start=True, stop=True), "t")
            P.wait("vector", "t", mmc)
            P.cop("vector", lambda e: e.tensor_copy(COLS[0:1, T1A:T1A + 1], RPS[0:1, 0:1]))
            ev = P.cop("vector", lambda e: e.tensor_copy(COLS[0:1, T2A:T2A + 1], RPS[0:1, 1:2]))
            P.wait("tensor", "v", ev)
            P.inc("tensor", lambda e: e.matmul(
                RPS[:, 0:1], ONESR[:], COLS[0:1, T1A:T1A + 1], start=True, stop=True), "t")
            bc = P.inc("tensor", lambda e: e.matmul(
                RPS[:, 1:2], ONESR[:], COLS[0:1, T2A:T2A + 1], start=True, stop=True), "t")
            P.wait("vector", "t", bc)
            numel = float(D * 4096)
            P.cop("vector", lambda e: e.tensor_scalar(
                COLS[:, CW1A:CW1A + 1], RPS[:, 0:1], 1.0 / numel, 1e-5, ALU.mult, ALU.max))
            P.cop("vector", lambda e: e.tensor_scalar(
                COLS[:, CW2A:CW2A + 1], RPS[:, 1:2], 1.0 / numel, 1e-5, ALU.mult, ALU.max))
            for (ca, ct, cnt_) in ((CW1A, CW1T, CW1NT), (CW2A, CW2T, CW2NT)):
                P.cop("vector", lambda e, ca=ca, ct=ct: e.tensor_scalar(
                    COLS[:, ct:ct + 1], COLS[:, ca:ca + 1], 0.5, None, ALU.mult))
                P.cop("vector", lambda e, ct=ct, cnt_=cnt_: e.tensor_scalar(
                    COLS[:, cnt_:cnt_ + 1], COLS[:, ct:ct + 1], -1.0, None, ALU.mult))
            P.cop("vector", lambda e: e.tensor_scalar(
                COLS[:, CCW1:CCW1 + 1], COLS[:, CW1A:CW1A + 1], -1.0 / 127.0, None, ALU.mult))
            thd = P.cop("vector", lambda e: e.tensor_scalar(
                COLS[:, CCW2:CCW2 + 1], COLS[:, CW2A:CW2A + 1], -1.0 / 127.0, None, ALU.mult))
            P.wait("vector", "v", thd)
            stt_v = {}
            for i in range(32):
                for (qn, Wp, piece, runc, tmpc, po) in streams:
                    eng = "sync" if qn == "y" else "gpsimd"
                    tcol = CW1T if qn == "y" else CW2T
                    ncol = CW1NT if qn == "y" else CW2NT
                    buf = Wp[i % 2]
                    if i >= 2:
                        P.wait(eng, "v", stt_v[(qn, i - 2)])
                    else:
                        P.wait(eng, "s", abs_s[(qn, 30 + i)])
                    tok = P.dma(qn, lambda e, buf=buf, src=piece(i): e.dma_start(
                        out=buf[:], in_=src))
                    P.wait_tok("vector", tok)
                    if qn == "y":
                        k, c = i // 4, i % 4
                        dst = IW1[:, k * 4096 + c * 1024: k * 4096 + (c + 1) * 1024]
                    else:
                        dst = IW2[:, i * 1024:(i + 1) * 1024]
                    P.cop("vector", lambda e, buf=buf, po=po, tcol=tcol: e.tensor_scalar(
                        PRA[:, po:po + 1024], buf[:], COLS[:, tcol:tcol + 1], None, ALU.is_ge))
                    stt_v[(qn, i)] = P.cop("vector", lambda e, buf=buf, po=po, dst=dst, ncol=ncol:
                        e.scalar_tensor_tensor(
                            dst, buf[:], COLS[:, ncol:ncol + 1], PRA[:, po:po + 1024],
                            ALU.is_le, ALU.subtract))
                    if qn == "g":
                        iw2_v[i] = stt_v[(qn, i)]
            iw1_v = stt_v[("y", 31)]
            def pt(k):
                return PT[:, ts(k)]
            for ti in range(NT):
                xn = P.cop("vector", lambda e, ti=ti: e.tensor_scalar(
                    TILE_A[:], XRES[:, ts(ti, 1024)], R0[:, ti:ti + 1],
                    NMR0[:, ti:ti + 1], ALU.mult, ALU.add))
                P.wait("tensor", "v", xn)
                for k in range(ND):
                    trd = P.inc("tensor", lambda e, k=k: e.transpose(
                        pt(k), TILE_A[:, ts(k)], IDBF[:]), "t")
                P.wait("vector", "t", trd)
                for k in range(ND):
                    P.cop("vector", lambda e, ti=ti, k=k: e.tensor_copy(
                        XNT[:, k * T + 128 * ti: k * T + 128 * (ti + 1)], pt(k)))
                P.wait("tensor", "v", P.cnt["v"])
        with (
            nc.sbuf_tensor("YB", [128, 4096], F32) as YB,
            nc.sbuf_tensor("IXT2", [128, 2048], BF16) as IXT2,
            nc.sbuf_tensor("TILE_I", [128, 512], I32) as TILE_I,
            nc.sbuf_tensor("IHT", [128, 512], BF16) as IHT,
            nc.sbuf_tensor("QCB", [128, 512], BF16) as QCB,
            nc.psum_tensor("HPS", [128, 1024], F32) as HPS,
            nc.psum_tensor("TPS", [128, 1024], BF16) as TPS,
            nc.psum_tensor("OPS2", [128, 2048], F32) as OPS2,
        ):
            trc = [0]
            tps_cons = {}

            def tslot():
                n = trc[0]
                trc[0] += 1
                return n

            def pslot(n):
                return TPS[:, ts(n % 8)]

            cast_v = {}
            s1_s = {}
            s3_s = {}
            prep_pb_v = {}
            prep_t = {}
            prepvb_v = {}
            mm1_last = {}
            mm2_t = {}
            tr_t = {}
            add_v = {}
            dsc_v = {}

            def emit_prep(tj):
                PB = YB[:, 0:1024]
                P.wait_tok("vector", n1ld)
                P.cop("vector", lambda e, tj=tj: e.tensor_scalar(
                    PB, XRES[:, ts(tj, 1024)], R2[:, tj:tj + 1],
                    NMR2[:, tj:tj + 1], ALU.mult, ALU.add))
                P.cop("vector", lambda e: e.tensor_tensor(PB, PB, N1W[:], ALU.mult))
                P.cop("vector", lambda e: e.tensor_reduce(
                    COLS[:, CPM:CPM + 1], PB, AX.X, ALU.max, apply_absolute_value=True))
                P.cop("vector", lambda e: e.tensor_scalar(
                    COLS[:, CPM:CPM + 1], COLS[:, CPM:CPM + 1], 1e-5, None, ALU.max))
                P.cop("vector", lambda e: e.reciprocal(COLS[:, CPR:CPR + 1], COLS[:, CPM:CPM + 1]))
                P.cop("vector", lambda e: e.tensor_scalar(
                    PB, PB, COLS[:, CPR:CPR + 1], 127.0, ALU.mult, ALU.mult))
                if tj >= 1:
                    P.wait("vector", "s", s3_s[(tj - 1, 3)])
                for q in range(2):
                    P.cop("vector", lambda e, q=q: e.tensor_copy(
                        TILE_I[:], PB[:, 512 * q:512 * (q + 1)]))
                    cp = P.cop("vector", lambda e, q=q: e.tensor_copy(
                        SCRA[:, 512 * q:512 * (q + 1)], TILE_I[:]))
                prep_pb_v[tj] = cp
                P.wait("tensor", "v", cp)
                if tj >= 2:
                    P.wait("tensor", "t", mm1_last[tj - 2])
                for k in range(ND):
                    n = tslot()
                    if n >= 8:
                        P.wait("tensor", "v", tps_cons[n - 8])
                    trd = P.inc("tensor", lambda e, n=n, k=k: e.transpose(
                        pslot(n), SCRA[:, ts(k)], IDBF[:]), "t")
                prep_t[tj] = trd
                base = trc[0] - 8
                P.wait("vector", "t", trd)
                for g2 in range(2):
                    cc = P.cop("vector", lambda e, tj=tj, g2=g2, base=base: e.tensor_copy(
                        IXT2[:, (tj % 2) * 1024 + 512 * g2:(tj % 2) * 1024 + 512 * (g2 + 1)],
                        TPS[:, ((base + 4 * g2) % 8) * 128:((base + 4 * g2) % 8) * 128 + 512]))
                    for n in range(base + 4 * g2, base + 4 * g2 + 4):
                        tps_cons[n] = cc
                prepvb_v[tj] = cc
                if tj >= 2:
                    P.wait("vector", "s", s1_s[(tj - 2, 3)])
                dsc_v[tj] = P.cop("vector", lambda e, tj=tj: e.tensor_tensor(
                    COLS[:, CDSC + tj % 2:CDSC + tj % 2 + 1], COLS[:, CCW1:CCW1 + 1],
                    COLS[:, CPM:CPM + 1], ALU.mult))

            emit_prep(0)
            for ti in range(NT):
                for hp in range(4):
                    for hc in (2 * hp, 2 * hp + 1):
                        P.wait("tensor", "v", prepvb_v[ti])
                        if ti == 0 and hc == 0:
                            P.wait("tensor", "v", iw1_v)
                        if hp >= 1:
                            P.wait("tensor", "s", s1_s[(ti, hp - 1)])
                        elif ti > 0:
                            P.wait("tensor", "s", s1_s[(ti - 1, 3)])
                        for k in range(ND):
                            mm = P.inc("tensor", lambda e, ti=ti, hc=hc, k=k: e.matmul(
                                HPS[:, ts(hc % 2, 512)],
                                IXT2[:, (ti % 2) * 1024 + 128 * k:(ti % 2) * 1024 + 128 * (k + 1)],
                                IW1[:, k * 4096 + 512 * hc:k * 4096 + 512 * (hc + 1)],
                                start=(k == 0), stop=(k == ND - 1)), "t")
                    mm1_last[ti] = mm
                    YBc = YB[:, 1024 * hp:1024 * (hp + 1)]
                    P.wait("scalar", "t", mm)
                    P.wait("scalar", "v", dsc_v[ti])
                    if ti > 0:
                        P.wait("scalar", "v", cast_v[(ti - 1, 2 * hp + 1)])
                    if hp == 0:
                        P.wait("scalar", "v", prep_pb_v[ti])
                        P.wait("scalar", "t", prep_t[ti])
                    s1_s[(ti, hp)] = P.cop("scalar", lambda e, ti=ti, YBc=YBc:
                        e.activation(YBc, HPS[:, 0:1024], AF.Relu, bias=0.0,
                                     scale=COLS[:, CDSC + ti % 2:CDSC + ti % 2 + 1]))
                    sq2 = P.cop("scalar", lambda e, YBc=YBc: e.activation(
                        YBc, YBc, AF.Square, bias=0.0, scale=1.0))
                    P.wait("vector", "s", sq2)
                    P.wait_tok("vector", asld)
                    v1 = P.cop("vector", lambda e, YBc=YBc: e.tensor_scalar(
                        YBc, YBc, COLS[:, CA:CA + 1], COLS[:, CB:CB + 1], ALU.mult, ALU.add))
                    P.wait("scalar", "v", v1)
                    s3_s[(ti, hp)] = P.cop("scalar", lambda e, hp=hp, YBc=YBc: e.activation(
                        SCRA[:], YBc, AF.Square, bias=0.0, scale=1.0,
                        accum_out=COLS[:, SY + hp:SY + hp + 1]))
                    P.wait("vector", "s", s3_s[(ti, hp)])
                    P.wait_tok("vector", n2ld)
                    P.cop("vector", lambda e, hp=hp, YBc=YBc: e.tensor_tensor(
                        YBc, YBc, N2W[:, 1024 * hp:1024 * (hp + 1)], ALU.mult))
                    P.cop("vector", lambda e, hp=hp, YBc=YBc: e.tensor_reduce(
                        COLS[:, MXC + hp:MXC + hp + 1], YBc, AX.X, ALU.max,
                        apply_absolute_value=True))
                sv = P.cop("vector", lambda e: e.tensor_reduce(
                    COLS[:, CS:CS + 1], COLS[:, SY:SY + 4], AX.X, ALU.add))
                P.cop("vector", lambda e: e.tensor_reduce(
                    COLS[:, CMZ:CMZ + 1], COLS[:, MXC:MXC + 4], AX.X, ALU.max))
                P.wait("scalar", "v", sv)
                P.cop("scalar", lambda e: e.activation(
                    COLS[:, CLN:CLN + 1], COLS[:, CS:CS + 1], AF.Ln,
                    bias=getc(4096.0 * 1e-8), scale=1.0))
                rq = P.cop("scalar", lambda e: e.activation(
                    COLS[:, CRSQ:CRSQ + 1], COLS[:, CLN:CLN + 1], AF.Exp,
                    bias=getc(float(0.5 * np.log(4096.0))), scale=-0.5))
                P.wait("vector", "s", rq)
                P.cop("vector", lambda e: e.tensor_tensor(
                    COLS[:, CMXP:CMXP + 1], COLS[:, CRSQ:CRSQ + 1],
                    COLS[:, CMZ:CMZ + 1], ALU.mult))
                P.cop("vector", lambda e: e.tensor_scalar(
                    COLS[:, CMXP:CMXP + 1], COLS[:, CMXP:CMXP + 1], 1e-5, None, ALU.max))
                P.cop("vector", lambda e: e.reciprocal(COLS[:, CF:CF + 1], COLS[:, CMXP:CMXP + 1]))
                P.cop("vector", lambda e: e.tensor_scalar(
                    COLS[:, CF:CF + 1], COLS[:, CF:CF + 1],
                    COLS[:, CRSQ:CRSQ + 1], 127.0, ALU.mult, ALU.mult))
                P.cop("vector", lambda e: e.tensor_tensor(
                    COLS[:, CD2:CD2 + 1], COLS[:, CCW2:CCW2 + 1],
                    COLS[:, CMXP:CMXP + 1], ALU.mult))
                for c8 in range(8):
                    YBc = YB[:, ts(c8, 512)]
                    P.cop("vector", lambda e, YBc=YBc: e.tensor_scalar(
                        YBc, YBc, COLS[:, CF:CF + 1], None, ALU.mult))
                    if c8 >= 1:
                        P.wait("vector", "t", tr_t[(ti, c8 - 1)])
                    P.cop("vector", lambda e, YBc=YBc: e.tensor_copy(TILE_I[:], YBc))
                    ct = P.cop("vector", lambda e: e.tensor_copy(QCB[:], TILE_I[:]))
                    cast_v[(ti, c8)] = ct
                    P.wait("tensor", "v", ct)
                    base = trc[0]
                    for k4 in range(4):
                        n = tslot()
                        if n >= 8:
                            P.wait("tensor", "v", tps_cons[n - 8])
                        trd = P.inc("tensor", lambda e, n=n, k4=k4: e.transpose(
                            pslot(n), QCB[:, ts(k4)], IDBF[:]), "t")
                    tr_t[(ti, c8)] = trd
                    P.wait("vector", "t", trd)
                    if c8 >= 1:
                        P.wait("vector", "t", mm2_t[(ti, c8 - 1)])
                    cc = P.cop("vector", lambda e, base=base: e.tensor_copy(
                        IHT[:], TPS[:, (base % 8) * 128:(base % 8) * 128 + 512]))
                    for n in range(base, base + 4):
                        tps_cons[n] = cc
                    P.wait("tensor", "v", cc)
                    if ti == 0:
                        P.wait("tensor", "v", iw2_v[4 * c8 + 3])
                    if ti >= 2 and c8 == 0:
                        P.wait("tensor", "v", add_v[ti - 2])
                    for k4 in range(4):
                        k = 4 * c8 + k4
                        for nn2 in range(2):
                            mm2 = P.inc("tensor", lambda e, ti=ti, k=k, k4=k4, nn2=nn2: e.matmul(
                                OPS2[:, ts((ti % 2) * 2 + nn2, 512)],
                                IHT[:, ts(k4)],
                                IW2[:, k * 1024 + 512 * nn2:k * 1024 + 512 * (nn2 + 1)],
                                start=(k == 0), stop=(k == 31)), "t")
                    mm2_t[(ti, c8)] = mm2
                if ti + 1 < NT:
                    emit_prep(ti + 1)
                P.wait("vector", "t", mm2_t[(ti, 7)])
                add_v[ti] = P.cop("vector", lambda e, ti=ti: e.scalar_tensor_tensor(
                    XRES[:, ts(ti, 1024)],
                    OPS2[:, (ti % 2) * 1024:(ti % 2) * 1024 + 1024],
                    COLS[:, CD2:CD2 + 1], XRES[:, ts(ti, 1024)], ALU.mult, ALU.add))



def l1_attn(nc, P, XRES, R1, NMR1, COLS, ST, IDBF, ONESR, c_ident, c_triu,
            lnw_bc, lnb_bc, wq1, wk1, wv1, wg1, gkw1, gkw2, gkb2_bc, rmsw_bc,
            wo1, wmlp1, bmlp1, wmlp2, bmlp2_bc, SCRA, getc, x2d, xn1td):
    HEADS = 16
    with (
        nc.sbuf_tensor("HT", [128, ND * T], BF16) as HT,
        nc.sbuf_tensor("TRIUF", [128, 128], F32) as TRIUF,
        nc.sbuf_tensor("MSK4", [128, 512], BF16) as MSK4,
        nc.sbuf_tensor("LNW", [128, D], BF16) as LNW,
        nc.sbuf_tensor("LNB", [128, D], BF16) as LNB,
        nc.sbuf_tensor("LNWI", [128, D], BF16) as LNWI,
        nc.sbuf_tensor("GK2S", [16, 512], BF16) as GK2S,
        nc.sbuf_tensor("GK1S", [128, ND * 16], BF16) as GK1S,
    ):
        trl = P.dma("y", lambda e: e.dma_start(out=TRIUF[:], in_=c_triu.ap()[:, :]))
        for q4 in range(4):
            mkl = P.dma("g", lambda e, q4=q4: e.dma_start(
                out=MSK4[:, 128 * q4:128 * (q4 + 1)], in_=c_triu.ap()[:, :]))
        lwl = P.dma("g", lambda e: e.dma_start(out=LNW[:], in_=lnw_bc[:, :]))
        lbl = P.dma("g", lambda e: e.dma_start(out=LNB[:], in_=lnb_bc[:, :]))
        g2l = P.dma("g", lambda e: e.dma_start(out=GK2S[:], in_=gkw2[:, :]))
        g1l = P.dma("g", lambda e: e.dma_start(
            out=GK1S[:].rearrange("p (k o) -> p k o", k=ND),
            in_=gkw1.rearrange("(k p) o -> p k o", p=128)))
        for tok in (lwl, lbl, mkl, g2l):
            P.wait_tok("vector", tok)
        P.wait("vector", "v", None)

        def _recip_lnw(e, nc=nc):
            with nc.allow_low_precision("bf16 reciprocal of ln gamma (xn1 recovery)"):
                return e.reciprocal(LNWI[:], LNW[:])
        lwi = P.cop("vector", _recip_lnw)

        es = contextlib.ExitStack()
        WQ = es.enter_context(nc.sbuf_tensor("WQ", [128, ND * 512], BF16))
        WK = es.enter_context(nc.sbuf_tensor("WK", [128, ND * 512], BF16))
        WV = es.enter_context(nc.sbuf_tensor("WV", [128, ND * 1024], BF16))
        WG = es.enter_context(nc.sbuf_tensor("WG", [128, ND * 1024], BF16))
        WO = es.enter_context(nc.sbuf_tensor("WO", [128, ND * 1024], BF16))
        GKB = es.enter_context(nc.sbuf_tensor("GKB", [128, 512], BF16))
        RMSW = es.enter_context(nc.sbuf_tensor("RMSW", [128, D], BF16))
        QC = es.enter_context(nc.sbuf_tensor("QC", [128, 1024], BF16))
        KC = es.enter_context(nc.sbuf_tensor("KC", [128, 1024], BF16))
        VC = es.enter_context(nc.sbuf_tensor("VC", [128, 2048], BF16))
        GC = es.enter_context(nc.sbuf_tensor("GC", [128, 2048], BF16))
        SPF = es.enter_context(nc.sbuf_tensor("SPF", [128, 512], F32))
        ASB = es.enter_context(nc.sbuf_tensor("ASB1", [128, 512], F32))
        E1 = es.enter_context(nc.sbuf_tensor("E1", [128, 512], BF16))
        E2 = es.enter_context(nc.sbuf_tensor("E2", [128, 512], BF16))
        ALC = es.enter_context(nc.sbuf_tensor("ALC", [128, 4], F32))
        EALC = es.enter_context(nc.sbuf_tensor("EALC", [128, 8], F32))
        QPT = es.enter_context(nc.sbuf_tensor("QPT", [128, 1024], BF16))
        KPT = es.enter_context(nc.sbuf_tensor("KPT", [128, 1024], BF16))
        Z1C = es.enter_context(nc.sbuf_tensor("Z1C", [128, 16], BF16))
        Z1TC = es.enter_context(nc.sbuf_tensor("Z1TC", [16, 128], BF16))
        PB2 = es.enter_context(nc.sbuf_tensor("PB2", [128, 512], BF16))
        OG = es.enter_context(nc.sbuf_tensor("OG", [128, 1024], BF16))
        OGT = es.enter_context(nc.sbuf_tensor("OGT", [128, 1024], BF16))
        SSB = es.enter_context(nc.sbuf_tensor("SSB", [128, 256], F32))
        SBB = es.enter_context(nc.sbuf_tensor("SBB", [128, 256], BF16))
        OCOL = es.enter_context(nc.sbuf_tensor("OCOL", [128, 16], F32))
        INVC = es.enter_context(nc.sbuf_tensor("INVC", [128, 16], F32))
        SQB = es.enter_context(nc.sbuf_tensor("SQB", [128, 1024], BF16))
        PRJ = es.enter_context(nc.psum_tensor("PRJ", [128, 1024], F32))
        TPB = es.enter_context(nc.psum_tensor("TPB", [128, 1024], BF16))
        PPS = es.enter_context(nc.psum_tensor("PPS", [128, 512], F32))
        OP2 = es.enter_context(nc.psum_tensor("OP2", [128, 1024], F32))
        XP = es.enter_context(nc.psum_tensor("XP", [128, 1024], F32))
        for wsrc, wdst in ((wq1, WQ), (wk1, WK), (wv1, WV), (wg1, WG), (wo1, WO)):
            P.dma("g", lambda e, wsrc=wsrc, wdst=wdst: e.dma_start(
                out=wdst[:].rearrange("p (k o) -> p k o", k=ND),
                in_=wsrc.rearrange("(k p) o -> p k o", p=128)))
        P.dma("g", lambda e: e.dma_start(out=GKB[:], in_=gkb2_bc[:, :]))
        P.dma("g", lambda e: e.dma_start(out=RMSW[:], in_=rmsw_bc[:, :]))

        btr = [0]
        bcons = {}

        def bslot():
            n = btr[0]
            btr[0] += 1
            return n

        tb_t = {}
        tc_tok = {}
        for ti in range(NT):
            par = ti % 2
            TAb = OG if par == 0 else OGT
            TBb = QC if par == 0 else KC
            TCb = VC[:, 1024 * par:1024 * (par + 1)]
            if par in tb_t:
                P.wait("vector", "t", tb_t[par])
            P.cop("vector", lambda e, ti=ti, TAb=TAb: e.tensor_scalar(
                TAb[:], XRES[:, ts(ti, 1024)], R1[:, ti:ti + 1],
                NMR1[:, ti:ti + 1], ALU.mult, ALU.add))
            P.cop("vector", lambda e, TAb=TAb, TBb=TBb: e.tensor_tensor(
                TBb[:], TAb[:], LNW[:], ALU.mult))
            hb = P.cop("vector", lambda e, TBb=TBb: e.tensor_tensor(
                TBb[:], TBb[:], LNB[:], ALU.add))
            P.wait("tensor", "v", hb)
            if btr[0] > 0:
                P.wait("tensor", "v", bcons[btr[0] - 1])
            for k in range(ND):
                n = bslot()
                trd = P.inc("tensor", lambda e, n=n, k=k, TAb=TAb: e.transpose(
                    TPB[:, ts(n % 8)], TAb[:, ts(k)], IDBF[:]), "t")
            base = btr[0] - 8
            P.wait("vector", "t", trd)
            if par in tc_tok:
                P.wait_tok("vector", tc_tok[par])
            for g2 in range(2):
                cc = P.cop("vector", lambda e, TCb=TCb, g2=g2, base=base: e.tensor_copy(
                    TCb[:, 512 * g2:512 * (g2 + 1)], TPSLICE(TPB, base + 4 * g2)))
                for n in range(base + 4 * g2, base + 4 * g2 + 4):
                    bcons[n] = cc
            P.wait("sync", "v", cc)
            tc_tok[par] = P.dma("y", lambda e, ti=ti, TCb=TCb: e.dma_start(
                out=xn1td.rearrange("p (k t) -> p k t", k=ND)[:, :, 128 * ti:128 * (ti + 1)],
                in_=TCb.rearrange("p (k t) -> p k t", k=ND)))
            if btr[0] > 0:
                P.wait("tensor", "v", bcons[btr[0] - 1])
            for k in range(ND):
                n = bslot()
                trd = P.inc("tensor", lambda e, n=n, k=k, TBb=TBb: e.transpose(
                    TPB[:, ts(n % 8)], TBb[:, ts(k)], IDBF[:]), "t")
            tb_t[par] = trd
            base = btr[0] - 8
            P.wait("vector", "t", trd)
            for k in range(ND):
                cc = P.cop("vector", lambda e, ti=ti, k=k, base=base: e.tensor_copy(
                    HT[:, k * T + 128 * ti:k * T + 128 * (ti + 1)],
                    TPB[:, ((base + k) % 8) * 128:((base + k) % 8) * 128 + 128]))
                bcons[base + k] = cc

        qk_v = {}
        ealc_s = {}
        vc_v = {}
        gc_v = {}
        sbb_v = {}
        st_t = {}
        og_v = {}
        sq_s = {}
        oc_v = {}
        x2add_v = {}
        x2_tok = {}
        prj_free_v = [0]

        def emit_prep(c):
            par = c % 2
            qo, vo = 512 * par, 1024 * par

            def hsl(k, c=c):
                return HT[:, k * T + 128 * c:k * T + 128 * (c + 1)]
            P.wait("tensor", "v", prj_free_v[0])
            if c >= 2:
                P.wait("tensor", "t", st_t[c - 2])
            for p0, wbuf in ((0, WQ), (512, WK)):
                for k in range(ND):
                    mm = P.inc("tensor", lambda e, wbuf=wbuf, p0=p0, k=k, hk=hsl(k): e.matmul(
                        PRJ[:, p0:p0 + 512], hk,
                        wbuf[:, 512 * k:512 * (k + 1)],
                        start=(k == 0), stop=(k == ND - 1)), "t")
            P.wait("vector", "t", mm)
            P.cop("vector", lambda e, qo=qo: e.tensor_copy(QC[:, qo:qo + 512], PRJ[:, 0:512]))
            ev = P.cop("vector", lambda e, qo=qo: e.tensor_copy(KC[:, qo:qo + 512], PRJ[:, 512:1024]))
            P.wait("tensor", "v", ev)
            for k in range(ND):
                for nn in range(2):
                    mm = P.inc("tensor", lambda e, k=k, nn=nn, hk=hsl(k): e.matmul(
                        PRJ[:, ts(nn, 512)], hk,
                        WV[:, 1024 * k + 512 * nn:1024 * k + 512 * (nn + 1)],
                        start=(k == 0), stop=(k == ND - 1)), "t")
            P.wait("vector", "t", mm)
            if c < 2 and par in tc_tok:
                P.wait_tok("vector", tc_tok[par])
            ev = P.cop("vector", lambda e, vo=vo: e.tensor_copy(VC[:, vo:vo + 1024], PRJ[:, 0:1024]))
            vc_v[c] = ev
            P.wait("tensor", "v", ev)
            for k in range(ND):
                for nn in range(2):
                    mm = P.inc("tensor", lambda e, k=k, nn=nn, hk=hsl(k): e.matmul(
                        PRJ[:, ts(nn, 512)], hk,
                        WG[:, 1024 * k + 512 * nn:1024 * k + 512 * (nn + 1)],
                        start=(k == 0), stop=(k == ND - 1)), "t")
            P.wait("scalar", "t", mm)
            if c >= 2:
                P.wait("scalar", "v", oc_v[c - 2])
            P.wait("scalar", "s", P.cnt["s"])
            P.inc("scalar", lambda e: e.activation(
                SQB[:], PRJ[:, 0:1024], AF.Exp, bias=0.0, scale=-1.0), "s")
            P.wait("scalar", "s", P.cnt["s"])
            P.inc("scalar", lambda e: e.activation(
                SQB[:], SQB[:], AF.Ln, bias=getc(1.0), scale=1.0), "s")
            P.wait("scalar", "s", P.cnt["s"])
            sgd = P.inc("scalar", lambda e: e.activation(
                SQB[:], SQB[:], AF.Exp, bias=0.0, scale=-1.0), "s")
            P.wait("vector", "s", sgd)
            P.cop("vector", lambda e, vo=vo: e.tensor_tensor(
                GC[:, vo:vo + 1024], PRJ[:, 0:1024], SQB[:], ALU.mult))
            ev = P.cop("vector", lambda e, vo=vo: e.tensor_tensor(
                GC[:, vo:vo + 1024], GC[:, vo:vo + 1024], RMSW[:], ALU.mult))
            gc_v[c] = ev
            P.wait("tensor", "v", ev)
            for k in range(ND):
                mm = P.inc("tensor", lambda e, k=k, hk=hsl(k): e.matmul(
                    PRJ[:, 0:16], hk, GK1S[:, 16 * k:16 * (k + 1)],
                    start=(k == 0), stop=(k == ND - 1)), "t")
            P.wait("vector", "t", mm)
            zv = P.cop("vector", lambda e: e.tensor_copy(Z1C[:], PRJ[:, 0:16]))
            P.wait("tensor", "v", zv)
            zbase = btr[0]
            if btr[0] > 0:
                P.wait("tensor", "v", bcons[btr[0] - 1])
            for _ in range(4):
                n = bslot()
            trd = P.inc("tensor", lambda e, zbase=zbase: e.transpose(
                TPB[0:16, (zbase % 8) * 128:(zbase % 8) * 128 + 128], Z1C[:], IDBF[:]), "t")
            P.wait("vector", "t", trd)
            ztc = P.cop("vector", lambda e, zbase=zbase: e.tensor_copy(
                Z1TC[:], TPB[0:16, (zbase % 8) * 128:(zbase % 8) * 128 + 128]))
            for n in range(zbase, zbase + 4):
                bcons[n] = ztc
            P.wait("tensor", "v", ztc)
            mm = P.inc("tensor", lambda e: e.matmul(
                PRJ[:, 0:512], Z1TC[:], GK2S[:], start=True, stop=True), "t")
            P.wait("vector", "t", mm)
            zb = P.cop("vector", lambda e: e.tensor_tensor(
                SPF[:], PRJ[:, 0:512], GKB[:], ALU.add))
            P.wait("scalar", "v", zb)
            P.wait("scalar", "s", P.cnt["s"])
            P.inc("scalar", lambda e: e.activation(SPF[:], SPF[:], AF.Exp, bias=0.0, scale=-1.0), "s")
            P.wait("scalar", "s", P.cnt["s"])
            spd = P.inc("scalar", lambda e: e.activation(
                SPF[:], SPF[:], AF.Ln, bias=getc(1.0), scale=1.0), "s")
            P.wait("tensor", "s", spd)
            mm = P.inc("tensor", lambda e: e.matmul(
                PRJ[:, 0:512], TRIUF[:], SPF[:], start=True, stop=True), "t")
            P.wait("vector", "t", mm)
            asd = P.cop("vector", lambda e: e.tensor_copy(ASB[:], PRJ[:, 0:512]))
            P.wait("scalar", "v", asd)
            P.wait("scalar", "s", P.cnt["s"])
            P.inc("scalar", lambda e: e.activation(
                E1[:], ASB[:], AF.Exp, bias=getc(float(-0.5 * np.log(32.0))),
                scale=-1.0 / 16.0), "s")
            P.wait("scalar", "s", P.cnt["s"])
            e2d = P.inc("scalar", lambda e: e.activation(
                E2[:], ASB[:], AF.Exp, bias=0.0, scale=1.0 / 16.0), "s")
            P.wait("tensor", "v", asd)
            for b in range(4):
                alm = P.inc("tensor", lambda e, b=b: e.matmul(
                    PRJ[:, 512 + b:513 + b], SPF[:, ts(b)], COLS[:, 63:64],
                    start=True, stop=True), "t")
            P.wait("vector", "t", alm)
            P.cop("vector", lambda e: e.tensor_copy(ALC[:, 0:4], PRJ[:, 512:516]))
            alc = P.cnt["v"]
            prj_free_v[0] = alc
            P.wait("scalar", "v", alc)
            if c >= 2:
                P.wait("scalar", "v", sbb_v[(c - 2, 3)])
            P.wait("scalar", "s", P.cnt["s"])
            ealc_s[c] = P.inc("scalar", lambda e, par=par: e.activation(
                EALC[:, 4 * par:4 * (par + 1)], ALC[:], AF.Exp, bias=0.0,
                scale=-1.0 / 16.0), "s")
            P.wait("vector", "s", e2d)
            P.cop("vector", lambda e, qo=qo: e.tensor_tensor(
                QC[:, qo:qo + 512], QC[:, qo:qo + 512], E1[:], ALU.mult))
            qpd = P.cop("vector", lambda e, qo=qo: e.tensor_tensor(
                KC[:, qo:qo + 512], KC[:, qo:qo + 512], E2[:], ALU.mult))
            P.wait("tensor", "v", qpd)
            for src_, dst in ((QC, QPT), (KC, KPT)):
                if btr[0] > 0:
                    P.wait("tensor", "v", bcons[btr[0] - 1])
                base = btr[0]
                for b in range(4):
                    n = bslot()
                    trd = P.inc("tensor", lambda e, n=n, b=b, src_=src_, qo=qo: e.transpose(
                        TPB[:, ts(n % 8)], src_[:, qo + 128 * b:qo + 128 * (b + 1)],
                        IDBF[:]), "t")
                P.wait("vector", "t", trd)
                cc = P.cop("vector", lambda e, dst=dst, qo=qo, base=base: e.tensor_copy(
                    dst[:, qo:qo + 512], TPSLICE(TPB, base)))
                for n in range(base, base + 4):
                    bcons[n] = cc
            qk_v[c] = cc

        def emit_state(c):
            par = c % 2
            qo, vo = 512 * par, 1024 * par
            P.wait("tensor", "v", qk_v[c])
            P.wait("tensor", "v", vc_v[c])
            if c >= 1:
                P.wait("tensor", "s", sq_s[c - 1])
                P.wait("tensor", "v", og_v[c - 1])
                P.wait("tensor", "v", x2add_v[c - 1])
            def emit_pt(g, m):
                r0 = 32 * m
                cb = 128 * g
                return P.inc("tensor", lambda e, r0=r0, cb=cb, qo=qo, m=m: e.matmul(
                    PPS[:, 128 * m:128 * (m + 1)],
                    KPT[r0:r0 + 32, qo + cb:qo + cb + 128],
                    QPT[r0:r0 + 32, qo + cb:qo + cb + 128],
                    start=True, stop=True, tile_position=(r0, 0)), "t")

            for m in range(4):
                if m > 0:
                    P.wait("tensor", "t", P.cnt["t"])
                pt_mm = emit_pt(0, m)
            for g in range(4):
                P.wait("vector", "t", pt_mm)
                if c == 0 and g == 0:
                    P.wait("vector", "v", sbb_init)
                msk = P.cop("vector", lambda e: e.tensor_tensor(
                    PB2[:], PPS[:, 0:512], MSK4[:], ALU.mult))
                P.wait("tensor", "v", msk)
                if c >= 1:
                    P.wait("tensor", "v", sbb_v[(c - 1, g)])
                for m in range(4):
                    h = 4 * g + m
                    r0 = 32 * m
                    P.inc("tensor", lambda e, h=h, m=m, vo=vo: e.matmul(
                        OP2[:, 64 * h:64 * h + 64], PB2[:, 128 * m:128 * (m + 1)],
                        VC[:, vo + 64 * h:vo + 64 * h + 64],
                        start=(h % 8 == 0), stop=False), "t")
                    P.inc("tensor", lambda e, h=h, r0=r0, g=g, qo=qo: e.matmul(
                        OP2[:, 64 * h:64 * h + 64],
                        QPT[r0:r0 + 32, qo + 128 * g:qo + 128 * (g + 1)],
                        SBB[r0:r0 + 32, 64 * g:64 * g + 64],
                        start=False, stop=(h % 8 == 7), tile_position=(r0, 0)), "t")
                    if g + 1 < 4:
                        pt_mm = emit_pt(g + 1, m)
                for m in range(4):
                    h = 4 * g + m
                    r0 = 32 * m
                    ds = P.inc("tensor", lambda e, h=h, r0=r0, g=g, qo=qo, vo=vo: e.matmul(
                        XP[r0:r0 + 32, 64 * g:64 * g + 64],
                        KC[:, qo + 32 * h:qo + 32 * h + 32],
                        VC[:, vo + 64 * h:vo + 64 * h + 64],
                        start=True, stop=True, tile_position=(0, r0)), "t")
                st_t[c] = ds
            P.wait("vector", "t", st_t[c])
            P.wait("vector", "s", ealc_s[c])
            P.cop("vector", lambda e: e.tensor_tensor(
                SSB[:], SSB[:], XP[:, 0:256], ALU.add))
            for g in range(4):
                P.cop("vector", lambda e, g=g, par=par: e.tensor_scalar(
                    SSB[:, 64 * g:64 * g + 64], SSB[:, 64 * g:64 * g + 64],
                    EALC[:, 4 * par + g:4 * par + g + 1], None, ALU.mult))
            cpall = P.cop("vector", lambda e: e.tensor_copy(SBB[:], SSB[:]))
            for g in range(4):
                sbb_v[(c, g)] = cpall

        def emit_epi(c):
            par = c % 2
            vo = 1024 * par
            P.wait("scalar", "t", st_t[c])
            P.wait("scalar", "v", gc_v[min(c + 1, NT - 1)])
            P.wait("scalar", "s", P.cnt["s"])
            sq_s[c] = P.inc("scalar", lambda e: e.activation(
                SQB[:], OP2[:, 0:1024], AF.Square, bias=0.0, scale=1.0), "s")
            P.wait("vector", "s", sq_s[c])
            P.cop("vector", lambda e: e.tensor_reduce(
                OCOL[:].rearrange("p (h one) -> p h one", one=1),
                SQB[:].rearrange("p (h v) -> p h v", h=HEADS), AX.X, ALU.add))
            oc = P.cnt["v"]
            oc_v[c] = oc
            P.wait("scalar", "v", oc)
            P.wait("scalar", "s", P.cnt["s"])
            P.inc("scalar", lambda e: e.activation(
                INVC[:], OCOL[:], AF.Ln, bias=getc(1e-5), scale=1.0 / 64.0), "s")
            P.wait("scalar", "s", P.cnt["s"])
            ivd = P.inc("scalar", lambda e: e.activation(
                INVC[:], INVC[:], AF.Exp, bias=0.0, scale=-0.5), "s")
            P.wait("vector", "s", ivd)
            for pp in (0, 1):
                if pp in x2_tok:
                    P.wait_tok("vector", x2_tok[pp])
            for h in range(HEADS):
                P.cop("vector", lambda e, h=h, vo=vo: e.scalar_tensor_tensor(
                    OG[:, 64 * h:64 * (h + 1)], OP2[:, 64 * h:64 * (h + 1)],
                    INVC[:, h:h + 1], GC[:, vo + 64 * h:vo + 64 * (h + 1)],
                    ALU.mult, ALU.mult))
            og_v[c] = P.cnt["v"]
            P.wait("tensor", "v", og_v[c])
            if btr[0] > 0:
                P.wait("tensor", "v", bcons[btr[0] - 1])
            base0 = btr[0]
            for b in range(ND):
                n = bslot()
                trd = P.inc("tensor", lambda e, n=n, b=b: e.transpose(
                    TPB[:, ts(n % 8)], OG[:, ts(b)], IDBF[:]), "t")
            P.wait("vector", "t", trd)
            for g2 in range(2):
                cc = P.cop("vector", lambda e, g2=g2, base0=base0: e.tensor_copy(
                    OGT[:, 512 * g2:512 * (g2 + 1)], TPSLICE(TPB, base0 + 4 * g2)))
                for n in range(base0 + 4 * g2, base0 + 4 * g2 + 4):
                    bcons[n] = cc
            P.wait("tensor", "v", cc)
            if c >= 1:
                P.wait("tensor", "v", x2add_v[c - 1])
            for k in range(ND):
                for nn in range(2):
                    wo_mm = P.inc("tensor", lambda e, k=k, nn=nn: e.matmul(
                        XP[:, ts(nn, 512)], OGT[:, ts(k)],
                        WO[:, 1024 * k + 512 * nn:1024 * k + 512 * (nn + 1)],
                        start=(k == 0), stop=(k == ND - 1)), "t")
            if btr[0] > 0:
                P.wait("tensor", "v", bcons[btr[0] - 1])
            base1 = btr[0]
            for k in range(ND):
                n = bslot()
                trd = P.inc("tensor", lambda e, n=n, k=k, c=c: e.transpose(
                    TPB[:, ts(n % 8)], HT[:, k * T + 128 * c:k * T + 128 * (c + 1)],
                    IDBF[:]), "t")
            P.wait("vector", "t", trd)
            P.wait("vector", "t", wo_mm)
            P.wait("vector", "s", ealc_s[min(c + 1, NT - 1)])
            for g2 in range(2):
                cc = P.cop("vector", lambda e, g2=g2, base1=base1: e.tensor_copy(
                    OGT[:, 512 * g2:512 * (g2 + 1)], TPSLICE(TPB, base1 + 4 * g2)))
                for n in range(base1 + 4 * g2, base1 + 4 * g2 + 4):
                    bcons[n] = cc
            for BUFH, ho in ((SPF, 0), (ASB, 512)):
                P.cop("vector", lambda e, BUFH=BUFH, ho=ho: e.tensor_tensor(
                    BUFH[:], OGT[:, ho:ho + 512], LNB[:, ho:ho + 512], ALU.subtract))
                P.cop("vector", lambda e, BUFH=BUFH, ho=ho: e.tensor_tensor(
                    BUFH[:], BUFH[:], LNWI[:, ho:ho + 512], ALU.mult))
                P.cop("vector", lambda e, BUFH=BUFH, ho=ho: e.tensor_tensor(
                    BUFH[:], BUFH[:], XP[:, ho:ho + 512], ALU.add))
                x2add_v[c] = P.cop("vector", lambda e, BUFH=BUFH, ho=ho, c=c: e.tensor_tensor(
                    XRES[:, 1024 * c + ho:1024 * c + ho + 512],
                    XRES[:, 1024 * c + ho:1024 * c + ho + 512], BUFH[:], ALU.add))
                x2b = P.cop("vector", lambda e, BUFH=BUFH, ho=ho: e.tensor_copy(
                    OG[:, ho:ho + 512], BUFH[:]))
            P.wait("sync", "v", x2b)
            x2_tok[par] = P.dma("y", lambda e, c=c: e.dma_start(
                out=x2d[ts(c), :], in_=OG[:]))

        P.wait_tok("tensor", g1l)
        P.wait_tok("tensor", trl)
        P.dma_barrier("tensor")
        P.dma_barrier("vector")
        P.wait("vector", "v", None)
        P.cop("vector", lambda e: e.memset(SSB[:], 0.0))
        sbb_init = P.cop("vector", lambda e: e.memset(SBB[:], 0.0))

        emit_prep(0)
        for c in range(NT):
            emit_state(c)
            if c + 1 < NT:
                emit_prep(c + 1)
            emit_epi(c)
        es.close()
    P.fence()
    l1_mlp(nc, P, XRES, COLS, ST, IDBF, SCRA, wmlp1, bmlp1, wmlp2, bmlp2_bc,
           getc, x2d, lnw_bc, lnb_bc)


def l1_mlp(nc, P, XRES, COLS, ST, IDBF, SCRA, wmlp1, bmlp1, wmlp2, bmlp2_bc,
           getc, x2d, lnw_bc, lnb_bc):
    R2C, NMR2C = ST[:, 8 * NT:9 * NT], ST[:, 9 * NT:10 * NT]
    with (
        nc.sbuf_tensor("X2G", [128, 4 * 1024], BF16) as X2G,
        nc.sbuf_tensor("X2T", [128, ND * 512], BF16) as X2T,
        nc.sbuf_tensor("HMT", [128, 16 * 512], BF16) as HMT,
        nc.sbuf_tensor("MW1", [128, ND * 2048], BF16) as MW1,
        nc.sbuf_tensor("MW2", [128, 16 * 1024], BF16) as MW2,
        nc.sbuf_tensor("BM1", [128, 16], F32) as BM1,
        nc.sbuf_tensor("BM2", [128, D], F32) as BM2,
        nc.sbuf_tensor("TD", [128, 2048], BF16) as TD,
        nc.sbuf_tensor("LNW2", [128, D], BF16) as LNW2,
        nc.sbuf_tensor("LNB2", [128, D], BF16) as LNB2,
        nc.psum_tensor("MPS", [128, 1024], F32) as MPS,
        nc.psum_tensor("TPM", [128, 1024], BF16) as TPM,
        nc.psum_tensor("OPM", [128, 2048], F32) as OPM,
    ):
        w1tok = P.dma("g", lambda e: e.dma_start(
            out=MW1[:].rearrange("p (k o) -> p k o", k=ND),
            in_=wmlp1.rearrange("(k p) o -> p k o", p=128)))
        w2tok = P.dma("g", lambda e: e.dma_start(
            out=MW2[:].rearrange("p (k o) -> p k o", k=16),
            in_=wmlp2.rearrange("(k p) o -> p k o", p=128)))
        b1tok = P.dma("y", lambda e: e.dma_start(out=BM1[:], in_=bmlp1[:, :]))
        lw2 = P.dma("g", lambda e: e.dma_start(out=LNW2[:], in_=lnw_bc[:, :]))
        lb2 = P.dma("g", lambda e: e.dma_start(out=LNB2[:], in_=lnb_bc[:, :]))
        P.wait_tok("vector", lw2)
        P.wait_tok("vector", lb2)
        b2tok = P.dma("y", lambda e: e.dma_start(out=BM2[:], in_=bmlp2_bc[:, :]))
        P.wait_tok("tensor", w1tok)
        P.wait_tok("tensor", w2tok)
        P.wait_tok("scalar", b1tok)
        P.wait_tok("vector", b2tok)
        x2toks = []
        for gb in range(4):
            x2toks.append(P.dma("y", lambda e, gb=gb: e.dma_start(
                out=X2G[:, ts(gb, 1024)], in_=x2d[ts(gb), :])))
        madd_v = {}
        SUM, SUM2 = COLS[:, 0:NT], COLS[:, 16:16 + NT]
        for ti in range(NT):
            gslot = ti % 4
            P.wait_tok("vector", x2toks[ti])
            P.wait_tok("scalar", x2toks[ti])
            P.cop("vector", lambda e, gslot=gslot, ti=ti: e.tensor_reduce(
                SUM[:, ti:ti + 1], X2G[:, ts(gslot, 1024)], AX.X, ALU.add))
            P.wait("scalar", "s", P.cnt["s"])
            sq = P.inc("scalar", lambda e, gslot=gslot, ti=ti: e.activation(
                SCRA[:], X2G[:, ts(gslot, 1024)], AF.Square, bias=0.0, scale=1.0,
                accum_out=SUM2[:, ti:ti + 1]), "s")
            if ti + 4 < NT:
                P.wait("sync", "v", P.cnt["v"])
                P.wait("sync", "s", P.cnt["s"])
                x2toks.append(P.dma("y", lambda e, ti=ti: e.dma_start(
                    out=X2G[:, ts((ti + 4) % 4, 1024)], in_=x2d[ts(ti + 4), :])))
        P.wait("vector", "s", P.cnt["s"])
        P.wait("vector", "v", None)
        M = COLS[:, 32:32 + NT]
        VAR = COLS[:, 0:NT]
        P.cop("vector", lambda e: e.tensor_scalar(M, SUM, 1.0 / D, None, ALU.mult))
        P.cop("vector", lambda e: e.tensor_scalar(SUM2, SUM2, 1.0 / D, None, ALU.mult))
        P.cop("vector", lambda e: e.tensor_tensor(VAR, M, M, ALU.mult))
        vd = P.cop("vector", lambda e: e.tensor_tensor(VAR, SUM2, VAR, ALU.subtract))
        P.wait("scalar", "v", vd)
        P.wait("scalar", "s", P.cnt["s"])
        P.inc("scalar", lambda e: e.activation(SUM2, VAR, AF.Ln, bias=getc(1e-5), scale=1.0), "s")
        P.wait("scalar", "s", P.cnt["s"])
        rd = P.inc("scalar", lambda e: e.activation(R2C, SUM2, AF.Exp, bias=0.0, scale=-0.5), "s")
        P.wait("vector", "s", rd)
        P.cop("vector", lambda e: e.tensor_tensor(NMR2C, M, R2C, ALU.mult))
        nm = P.cop("vector", lambda e: e.tensor_scalar(NMR2C, NMR2C, -1.0, None, ALU.mult))
        P.wait("sync", "v", P.cnt["v"])
        P.wait("sync", "s", P.cnt["s"])
        trm_t = {}
        cp_v = {}
        for g in range(4):
            for gb in range(4):
                ti = 4 * g + gb
                par = gb % 2
                P.wait("sync", "v", P.cnt["v"])
                tok = P.dma("y", lambda e, gb=gb, ti=ti: e.dma_start(
                    out=X2G[:, ts(gb, 1024)], in_=x2d[ts(ti), :]))
                P.wait_tok("vector", tok)
                P.wait("vector", "v", nm)
                if par in trm_t:
                    P.wait("vector", "t", trm_t[par])
                TDp = TD[:, 1024 * par:1024 * (par + 1)]
                P.cop("vector", lambda e, gb=gb, ti=ti, TDp=TDp: e.tensor_scalar(
                    TDp, X2G[:, ts(gb, 1024)], R2C[:, ti:ti + 1],
                    NMR2C[:, ti:ti + 1], ALU.mult, ALU.add))
                P.cop("vector", lambda e, TDp=TDp: e.tensor_tensor(TDp, TDp, LNW2[:], ALU.mult))
                xn = P.cop("vector", lambda e, TDp=TDp: e.tensor_tensor(TDp, TDp, LNB2[:], ALU.add))
                P.wait("tensor", "v", xn)
                if par in cp_v:
                    P.wait("tensor", "v", cp_v[par])
                for k in range(ND):
                    trd = P.inc("tensor", lambda e, k=k, par=par, TDp=TDp: e.transpose(
                        TPM[:, 512 * par + 128 * (k % 4):512 * par + 128 * (k % 4 + 1)],
                        TDp[:, ts(k)], IDBF[:]), "t")
                    if k == 3:
                        P.wait("vector", "t", trd)
                        for kk in range(4):
                            cp_v[par] = P.cop("vector", lambda e, kk=kk, gb=gb, par=par:
                                e.tensor_copy(
                                    X2T[:, kk * 512 + 128 * gb: kk * 512 + 128 * (gb + 1)],
                                    TPM[:, 512 * par + 128 * kk:512 * par + 128 * (kk + 1)]))
                        P.wait("tensor", "v", cp_v[par])
                trm_t[par] = trd
                P.wait("vector", "t", trd)
                for kk in range(4):
                    cp_v[par] = P.cop("vector", lambda e, kk=kk, gb=gb, par=par: e.tensor_copy(
                        X2T[:, (kk + 4) * 512 + 128 * gb: (kk + 4) * 512 + 128 * (gb + 1)],
                        TPM[:, 512 * par + 128 * kk:512 * par + 128 * (kk + 1)]))
            evs = 0
            for ht in range(16):
                if ht >= 2:
                    P.wait("tensor", "s", evs - 1)
                for k in range(ND):
                    mm = P.inc("tensor", lambda e, ht=ht, k=k: e.matmul(
                        MPS[:, ts(ht % 2, 512)],
                        MW1[:, 2048 * k + 128 * ht: 2048 * k + 128 * (ht + 1)],
                        X2T[:, ts(k, 512)],
                        start=(k == 0), stop=(k == ND - 1)), "t")
                P.wait("scalar", "t", mm)
                P.wait("scalar", "s", P.cnt["s"])
                evs = P.inc("scalar", lambda e, ht=ht: e.activation(
                    HMT[:, ts(ht, 512)], MPS[:, ts(ht % 2, 512)], AF.Gelu,
                    bias=BM1[:, ht:ht + 1], scale=1.0), "s")
            P.wait("tensor", "s", evs)
            for tc in range(4):
                ti = 4 * g + tc
                if (tc % 2) in madd_v:
                    P.wait("tensor", "v", madd_v[tc % 2])
                for k in range(16):
                    for nn in range(2):
                        mm = P.inc("tensor", lambda e, tc=tc, k=k, nn=nn: e.matmul(
                            OPM[:, ts((tc % 2) * 2 + nn, 512)],
                            HMT[:, 512 * k + 128 * tc: 512 * k + 128 * (tc + 1)],
                            MW2[:, 1024 * k + 512 * nn: 1024 * k + 512 * (nn + 1)],
                            start=(k == 0), stop=(k == 15)), "t")
                P.wait("vector", "t", mm)
                P.cop("vector", lambda e, ti=ti, tc=tc: e.tensor_tensor(
                    XRES[:, ts(ti, 1024)], XRES[:, ts(ti, 1024)],
                    OPM[:, (tc % 2) * 1024: (tc % 2) * 1024 + 1024], ALU.add))
                madd_v[tc % 2] = P.cop("vector", lambda e, ti=ti: e.tensor_tensor(
                    XRES[:, ts(ti, 1024)], XRES[:, ts(ti, 1024)], BM2[:], ALU.add))



def l1_ffn(nc, P, XRES, COLS, IDBF, w1t1, b1f, w2t1, b2f_bc, getc, xn1td):
    with (
        nc.sbuf_tensor("XN1F", [128, ND * T], BF16) as XN1F,
        nc.sbuf_tensor("HQT", [128, 32 * T], FP8) as HQT,
        nc.sbuf_tensor("QW2", [128, 32 * 1024], FP8) as QW2,
        nc.sbuf_tensor("MWS", [128, 2 * ND * 128], BF16) as MWS,
        nc.sbuf_tensor("B1F", [128, 32], F32) as B1F,
        nc.sbuf_tensor("BQ16", [128, D], mybir.dt.float16) as BQ16,
        nc.sbuf_tensor("HF", [128, 1024], F32) as HF,
        nc.psum_tensor("HP1", [128, 2048], F32) as HP1,
        nc.psum_tensor("OPF", [128, 2048], F32) as OPF,
    ):
        OT16 = HF[:, 0:512].bitcast(mybir.dt.float16)
        xl = P.dma("y", lambda e: e.dma_start(out=XN1F[:], in_=xn1td[:, :]))
        ql = P.dma("g", lambda e: e.dma_start(
            out=QW2[:].rearrange("p (k o) -> p k o", k=32),
            in_=w2t1.rearrange("(k p) o -> p k o", p=128)))
        bl = P.dma("y", lambda e: e.dma_start(out=B1F[:], in_=b1f[:, :]))
        b2l = P.dma("g", lambda e: e.dma_start(out=BQ16[:], in_=b2f_bc[:, :]))
        P.wait_tok("tensor", xl)
        P.wait_tok("scalar", bl)
        mm_t = {}
        hr_s = {}
        hq_v = {}
        for ht in range(32):
            PS = HP1 if ht % 2 == 0 else OPF
            if ht >= 2:
                P.wait("gpsimd", "t", mm_t[ht - 2])
            wtok = P.dma("g", lambda e, ht=ht: e.dma_start(
                out=MWS[:, 1024 * (ht % 2):1024 * (ht % 2 + 1)]
                    .rearrange("p (k o) -> p k o", k=ND),
                in_=w1t1.rearrange("(k p) o -> p k o", p=128)[:, :, 128 * ht:128 * (ht + 1)]))
            P.wait_tok("tensor", wtok)
            if ht >= 2:
                P.wait("tensor", "s", hr_s[ht - 2])
            for k in range(ND):
                for nn in range(4):
                    mm = P.inc("tensor", lambda e, ht=ht, k=k, nn=nn, PS=PS: e.matmul(
                        PS[:, ts(nn, 512)],
                        MWS[:, 1024 * (ht % 2) + 128 * k:1024 * (ht % 2) + 128 * (k + 1)],
                        XN1F[:, k * T + 512 * nn: k * T + 512 * (nn + 1)],
                        start=(k == 0), stop=(k == ND - 1)), "t")
            mm_t[ht] = mm
            for hh in range(2):
                P.wait("scalar", "t", mm)
                if ht >= 1 or hh == 1:
                    P.wait("scalar", "v", hq_v[(ht, hh - 1) if hh else (ht - 1, 1)])
                P.wait("scalar", "s", P.cnt["s"])
                hr_s[ht] = P.inc("scalar", lambda e, ht=ht, hh=hh, PS=PS: e.activation(
                    HF[:], PS[:, 1024 * hh:1024 * (hh + 1)], AF.Relu,
                    bias=B1F[:, ht:ht + 1], scale=1.0), "s")
                P.wait("vector", "s", hr_s[ht])
                hq_v[(ht, hh)] = P.cop("vector", lambda e, ht=ht, hh=hh: e.tensor_copy(
                    HQT[:, 2048 * ht + 1024 * hh:2048 * ht + 1024 * (hh + 1)], HF[:]))
        P.wait_tok("tensor", ql)
        add_v = {}
        for ti in range(NT):
            if ti == 0:
                P.wait("tensor", "s", hr_s[31])
                P.wait("tensor", "v", hq_v[(31, 1)])
            if ti >= 2:
                P.wait("tensor", "v", add_v[ti - 2])
            for k in range(32):
                for nn in range(2):
                    mm = P.inc("tensor", lambda e, ti=ti, k=k, nn=nn: e.matmul(
                        OPF[:, ts((ti % 2) * 2 + nn, 512)],
                        HQT[:, 2048 * k + 128 * ti: 2048 * k + 128 * (ti + 1)],
                        QW2[:, 1024 * k + 512 * nn: 1024 * k + 512 * (nn + 1)],
                        start=(k == 0), stop=(k == 31)), "t")
            P.wait("vector", "t", mm)
            P.wait_tok("vector", b2l)
            P.cop("vector", lambda e, ti=ti: e.tensor_tensor(
                OT16[:], OPF[:, (ti % 2) * 1024: (ti % 2) * 1024 + 1024], BQ16[:], ALU.add))
            add_v[ti] = P.cop("vector", lambda e, ti=ti: e.tensor_tensor(
                XRES[:, ts(ti, 1024)], XRES[:, ts(ti, 1024)], OT16[:], ALU.add))



_CACHE = {}


def _build_program(stop_after="full"):
    if stop_after not in _CACHE:
        nc = bass.Bass("TRN2", target_bir_lowering=False, debug=False)
        P = build(nc, stop_after)
        with nc.Block() as blk:
            P.run(blk)
        _CACHE[stop_after] = nc
    return _CACHE[stop_after]


def _in_maps(inputs):
    d = inputs
    f32 = np.float32

    def asc(a):
        return np.ascontiguousarray(np.asarray(a, dtype=f32))

    shared = {
        "mq": asc(d["L0_attn_Mq"]),
        "mk": asc(d["L0_attn_Mk"]),
        "mv": asc(d["L0_attn_Mv"]),
        "w1t0": asc(np.asarray(d["L0_ffn_w1"], dtype=f32).T),
        "w2t0": asc(np.asarray(d["L0_ffn_w2"], dtype=f32).T),
        "n1w_bc": asc(np.broadcast_to(np.asarray(d["L0_ffn_norm1_w"], f32), (128, D))),
        "n2w_bc": asc(np.broadcast_to(np.asarray(d["L0_ffn_norm2_w"], f32), (128, 4096))),
        "actsb": asc(np.broadcast_to(np.concatenate([
            np.asarray(d["L0_ffn_act_scale"], f32).reshape(1),
            np.asarray(d["L0_ffn_act_bias"], f32).reshape(1)]), (128, 2))),
        "lnw_bc": asc(np.broadcast_to(np.asarray(d["L1_attn_ln_w"], f32), (128, D))),
        "lnb_bc": asc(np.broadcast_to(np.asarray(d["L1_attn_ln_b"], f32), (128, D))),
        "wq1": asc(d["L1_attn_Wq"]),
        "wk1": asc(d["L1_attn_Wk"]),
        "wv1": asc(d["L1_attn_Wv"]),
        "wg1": asc(d["L1_attn_Wg"]),
        "gkw1": asc(d["L1_attn_gk_w1"]),
        "gkw2": asc(d["L1_attn_gk_w2"]),
        "gkb2_bc": asc(np.broadcast_to(np.asarray(d["L1_attn_gk_b2"], f32), (128, 512))),
        "rmsw_bc": asc(np.broadcast_to(np.tile(np.asarray(d["L1_attn_rms_w"], f32), 16), (128, D))),
        "wo1": asc(d["L1_attn_Wo"]),
        "wmlp1": asc(d["L1_attn_W1"]),
        "bmlp1": asc(np.asarray(d["L1_attn_b1"], f32).reshape(16, 128).T),
        "wmlp2": asc(d["L1_attn_W2"]),
        "bmlp2_bc": asc(np.broadcast_to(np.asarray(d["L1_attn_b2"], f32), (128, D))),
        "w1t1": asc(np.asarray(d["L1_ffn_w1"], dtype=f32).T),
        "b1f": asc(np.asarray(d["L1_ffn_b1"], f32).reshape(32, 128).T),
        "w2t1": asc(np.asarray(d["L1_ffn_w2"], dtype=f32).T),
        "b2f_bc": asc(np.broadcast_to(np.asarray(d["L1_ffn_b2"], f32), (128, D))),
    }
    x = np.asarray(d["L0_attn_x"], dtype=f32)
    return [{**shared, "x": np.ascontiguousarray(x[b])} for b in range(x.shape[0])]


def kernel(**inputs):
    nc = _build_program("full")
    maps = _in_maps(inputs)
    res = run_bass_kernel_spmd(nc, maps, core_ids=list(range(8)))
    return np.stack([r["y"] for r in res.results], axis=0)


def kernel_dev(stop_after, inputs, cores=1):
    nc = _build_program(stop_after)
    maps = _in_maps(inputs)[:cores]
    res = run_bass_kernel_spmd(nc, maps, core_ids=list(range(cores)))
    return np.stack([r["y"] for r in res.results], axis=0)
